# version 01
# baseline (speedup 1.0000x reference)
import numpy as np
import ml_dtypes
from contextlib import ExitStack

import concourse.bass as bass
import concourse.mybir as mybir
from concourse.bass import IndirectOffsetOnAxis
from concourse.bass_utils import run_bass_kernel_spmd

F32 = mybir.dt.float32
BF16 = mybir.dt.bfloat16
U32 = mybir.dt.uint32
I16 = mybir.dt.int16
U8 = mybir.dt.uint8
AF = mybir.ActivationFunctionType
ALU = mybir.AluOpType
AX = mybir.AxisListType

D_MODEL = 1024
NCH = D_MODEL // 128
T_CORE = 2048
N_CORES = 8
LN_EPS = 1e-5

BF = ml_dtypes.bfloat16

ENGINES = ("sync", "scalar", "vector", "tensor", "gpsimd")



def fmT(a):
    return np.ascontiguousarray(np.asarray(a, dtype=np.float32).T)


def pack_vec(v):
    return np.ascontiguousarray(np.asarray(v, dtype=np.float32).reshape(-1, 128).T)


def core_slice_tokens(x, core):
    b, h = core // 2, core % 2
    return np.asarray(x[b, h * 2048:(h + 1) * 2048])



class Seq:

    def __init__(self, nc):
        self.nc = nc
        self.steps = {e: [] for e in ENGINES}
        self.total = 0

    def step(self, engine, fn, inc=1, wait=None):
        tgt = self.total if wait is None else wait
        self.steps[engine].append((tgt, fn, inc))
        self.total += inc
        return self.total

    def dma(self, out, in_, engine="sync", wait=None, **kw):
        return self.step(engine,
                         lambda e: e.dma_start(out=out, in_=in_, **kw),
                         inc=16, wait=wait)

    def build_into(self, block, g):
        for ename in ENGINES:
            steps = self.steps[ename]
            if not steps:
                continue

            def make(steps):
                def prog(eng):
                    for tgt, fn, inc in steps:
                        if tgt > 0:
                            eng.wait_ge(g, tgt)
                        ins = fn(eng)
                        if isinstance(ins, list):
                            ins = ins[-1]
                        ins.then_inc(g, inc)
                return prog
            getattr(block, ename)(make(steps))

    def finish(self, block, g):
        tot = self.total

        def prog(eng):
            eng.wait_ge(g, tot)
        block.sync(prog)


def seq_program(build_fn):
    nc = bass.Bass("TRN2", target_bir_lowering=False, debug=False)
    sq = Seq(nc)
    with ExitStack() as es:
        g = es.enter_context(nc.semaphore("g_order"))
        build_fn(nc, sq, es)
        block = es.enter_context(nc.Block())
        sq.build_into(block, g)
        sq.finish(block, g)
    return nc



def emit_ln_fm(nc, sq, X_view, TT, xn, xn_off, scr, rows, psA, psB, ones_col,
               ones_row, load_chunk):
    raise NotImplementedError



def build_l1_ffn():
    TE = T_CORE + 1
    NT, T = 4, 512

    def prog(nc, sq, es):
        x_ext = nc.dram_tensor("x_ext", [D_MODEL, TE], F32, kind="ExternalInput").ap()
        wrT = nc.dram_tensor("wrT", [D_MODEL, D_MODEL], F32, kind="ExternalInput").ap()
        wkT = nc.dram_tensor("wkT", [D_MODEL, D_MODEL], F32, kind="ExternalInput").ap()
        wvT = nc.dram_tensor("wvT", [D_MODEL, D_MODEL], F32, kind="ExternalInput").ap()
        mu_r = nc.dram_tensor("mu_r", [128, NCH], F32, kind="ExternalInput").ap()
        mu_k = nc.dram_tensor("mu_k", [128, NCH], F32, kind="ExternalInput").ap()
        x_out = nc.dram_tensor("x_out", [D_MODEL, T_CORE], F32,
                               kind="ExternalOutput").ap()

        xe = x_ext.rearrange("(c p) t -> p c t", p=128)
        xo = x_out.rearrange("(c p) t -> p c t", p=128)

        sb = lambda n, s, d=F32: es.enter_context(nc.sbuf_tensor(n, s, d))
        pt = lambda n, s, d=F32: es.enter_context(nc.psum_tensor(n, s, d))

        Xt = sb("Xt", [128, NCH * 512])
        xn = sb("xn", [128, NCH * TE], BF16)
        sq32 = sb("sq32", [128, 512])
        a_row = sb("a_row", [1, TE])
        b_row = sb("b_row", [1, TE])
        t_row = sb("t_row", [1, 512])
        wk_s = sb("wk_s", [128, NCH * D_MODEL], BF16)
        wr_s = sb("wr_s", [128, NCH * D_MODEL], BF16)
        wv_s = sb("wv_s", [128, NCH * D_MODEL], BF16)
        mur_s = sb("mur_s", [128, NCH])
        muk_s = sb("muk_s", [128, NCH])
        dscr = sb("dscr", [128, 512])
        oscr = sb("oscr", [128, 512])
        xk_s = sb("xk_s", [128, NCH * T], BF16)
        xr_s = sb("xr_s", [128, NCH * T], BF16)
        kact = sb("kact", [128, NCH * T], BF16)
        r_s = sb("r_s", [128, NCH * T])
        ones_col = sb("ones_col", [128, 1])
        ones_row = sb("ones_row", [1, 128])
        eps_row = sb("eps_row", [1, 1])

        ps_a = pt("ps_a", [1, 512])
        ps_b = pt("ps_b", [1, 512])
        ps_bc1 = pt("ps_bc1", [128, 512])
        ps_bc2 = pt("ps_bc2", [128, 512])
        ps_mm = pt("ps_mm", [128, 512])

        sq.step("gpsimd", lambda e: e.memset(ones_col[:], 1.0))
        sq.step("gpsimd", lambda e: e.memset(ones_row[:], 1.0))
        sq.step("gpsimd", lambda e: e.memset(eps_row[:], LN_EPS))
        for nm, dst in (("wkT", wk_s), ("wrT", wr_s), ("wvT", wv_s)):
            src = {"wkT": wkT, "wrT": wrT, "wvT": wvT}[nm]
            sq.dma(dst[:].rearrange("p (c d) -> p c d", c=NCH),
                   src.rearrange("(c p) d -> p c d", p=128), engine="gpsimd")
        sq.dma(mur_s[:], mu_r)
        sq.dma(muk_s[:], mu_k)

        c0 = 0
        while c0 < TE:
            w = min(512, TE - c0)
            sq.dma(Xt[:].rearrange("p (c t) -> p c t", c=NCH)[:, :, 0:w],
                   xe[:, :, c0:c0 + w],
                   **({"allow_slow_non_contiguous": True} if w < 8 else {}))
            sq.step("tensor", lambda e, w=w: [
                e.matmul(ps_a[:, 0:w], ones_col[:],
                         Xt[:, c * 512:c * 512 + w], start=(c == 0), stop=(c == NCH - 1))
                for c in range(NCH)])
            for c in range(NCH):
                sq.step("scalar", lambda e, c=c, w=w: e.activation(
                    sq32[:, 0:w], Xt[:, c * 512:c * 512 + w], AF.Square))
                sq.step("tensor", lambda e, c=c, w=w: e.matmul(
                    ps_b[:, 0:w], ones_col[:], sq32[:, 0:w],
                    start=(c == 0), stop=(c == NCH - 1)))
            sq.step("vector", lambda e, c0=c0, w=w: e.tensor_single_scalar(
                b_row[:, c0:c0 + w], ps_a[:, 0:w], 1.0 / D_MODEL, ALU.mult))
            sq.step("vector", lambda e, c0=c0, w=w: e.tensor_single_scalar(
                a_row[:, c0:c0 + w], ps_b[:, 0:w], 1.0 / D_MODEL, ALU.mult))
            sq.step("scalar", lambda e, c0=c0, w=w: e.activation(
                t_row[:, 0:w], b_row[:, c0:c0 + w], AF.Square))
            sq.step("vector", lambda e, c0=c0, w=w: e.tensor_sub(
                a_row[:, c0:c0 + w], a_row[:, c0:c0 + w], t_row[:, 0:w]))
            sq.step("scalar", lambda e, c0=c0, w=w: e.activation(
                a_row[:, c0:c0 + w], a_row[:, c0:c0 + w], AF.Ln, bias=eps_row[:]))
            sq.step("scalar", lambda e, c0=c0, w=w: e.activation(
                a_row[:, c0:c0 + w], a_row[:, c0:c0 + w], AF.Exp, scale=-0.5))
            sq.step("tensor", lambda e, c0=c0, w=w: e.matmul(
                ps_bc1[:, 0:w], ones_row[:], b_row[:, c0:c0 + w]))
            sq.step("tensor", lambda e, c0=c0, w=w: e.matmul(
                ps_bc2[:, 0:w], ones_row[:], a_row[:, c0:c0 + w]))
            for c in range(NCH):
                sq.step("vector", lambda e, c=c, w=w: e.tensor_sub(
                    dscr[:, 0:w], Xt[:, c * 512:c * 512 + w], ps_bc1[:, 0:w]))
                sq.step("vector", lambda e, c=c, c0=c0, w=w: e.tensor_mul(
                    xn[:, c * TE + c0:c * TE + c0 + w], dscr[:, 0:w], ps_bc2[:, 0:w]))
            c0 += w

        for i in range(NT):
            t0 = 1 + i * T
            for c in range(NCH):
                sq.step("vector", lambda e, c=c, t0=t0: e.tensor_sub(
                    dscr[:, 0:T], xn[:, c * TE + t0:c * TE + t0 + T],
                    xn[:, c * TE + t0 - 1:c * TE + t0 - 1 + T]))
                sq.step("vector", lambda e, c=c, t0=t0: e.scalar_tensor_tensor(
                    xk_s[:, c * T:(c + 1) * T], dscr[:, 0:T], muk_s[:, c:c + 1],
                    xn[:, c * TE + t0 - 1:c * TE + t0 - 1 + T], ALU.mult, ALU.add))
                sq.step("vector", lambda e, c=c, t0=t0: e.scalar_tensor_tensor(
                    xr_s[:, c * T:(c + 1) * T], dscr[:, 0:T], mur_s[:, c:c + 1],
                    xn[:, c * TE + t0 - 1:c * TE + t0 - 1 + T], ALU.mult, ALU.add))
            for d in range(NCH):
                sq.step("tensor", lambda e, d=d: [
                    e.matmul(ps_mm[:], wk_s[:, c * D_MODEL + d * 128:
                                            c * D_MODEL + (d + 1) * 128],
                             xk_s[:, c * T:(c + 1) * T],
                             start=(c == 0), stop=(c == NCH - 1))
                    for c in range(NCH)])
                sq.step("vector", lambda e, d=d: e.tensor_single_scalar(
                    dscr[:, 0:T], ps_mm[:], 0.0, ALU.max))
                sq.step("scalar", lambda e, d=d: e.activation(
                    kact[:, d * T:(d + 1) * T], dscr[:, 0:T], AF.Square))
            for d in range(NCH):
                sq.step("tensor", lambda e, d=d: [
                    e.matmul(ps_mm[:], wr_s[:, c * D_MODEL + d * 128:
                                            c * D_MODEL + (d + 1) * 128],
                             xr_s[:, c * T:(c + 1) * T],
                             start=(c == 0), stop=(c == NCH - 1))
                    for c in range(NCH)])
                sq.step("scalar", lambda e, d=d: e.activation(
                    r_s[:, d * T:(d + 1) * T], ps_mm[:], AF.Sigmoid))
            for d in range(NCH):
                sq.step("tensor", lambda e, d=d: [
                    e.matmul(ps_mm[:], wv_s[:, c * D_MODEL + d * 128:
                                            c * D_MODEL + (d + 1) * 128],
                             kact[:, c * T:(c + 1) * T],
                             start=(c == 0), stop=(c == NCH - 1))
                    for c in range(NCH)])
                sq.step("vector", lambda e, d=d: e.tensor_mul(
                    oscr[:, 0:T], r_s[:, d * T:(d + 1) * T], ps_mm[:]))
                sq.dma(Xt[:, 0:T], xe[:, d, t0:t0 + T])
                sq.step("vector", lambda e, d=d: e.tensor_add(
                    oscr[:, 0:T], oscr[:, 0:T], Xt[:, 0:T]))
                sq.dma(xo[:, d, i * T:(i + 1) * T], oscr[:, 0:T])

    return seq_program(prog)


def host_l1_ffn_inputs(x, weights, core):
    b, h = core // 2, core % 2
    sl = core_slice_tokens(x, core)
    if h == 0:
        halo = np.zeros((1, D_MODEL), np.float32)
    else:
        halo = np.asarray(x[b, h * 2048 - 1:h * 2048])
    x_ext = fmT(np.concatenate([halo, sl], axis=0))
    return {
        "x_ext": x_ext,
        "wrT": fmT(weights["W_r"]),
        "wkT": fmT(weights["W_k"]),
        "wvT": fmT(weights["W_v"]),
        "mu_r": pack_vec(weights["mu_r"]),
        "mu_k": pack_vec(weights["mu_k"]),
    }


def unpack_fm_out(res_list, key="x_out"):
    out = np.empty((4, 4096, 1024), np.float32)
    for core in range(N_CORES):
        b, h = core // 2, core % 2
        out[b, h * 2048:(h + 1) * 2048] = res_list[core][key].T
    return out



def build_l0_ffn():
    NEXP = 8
    HID = 4096
    NHT = HID // 128
    NHG = 4
    NT, T = 4, 512

    def prog(nc, sq, es):
        x_fm = nc.dram_tensor("x_fm", [D_MODEL, T_CORE], F32, kind="ExternalInput").ap()
        router_wT = nc.dram_tensor("router_wT", [D_MODEL, 8], F32, kind="ExternalInput").ap()
        w_gate = nc.dram_tensor("w_gate", [NEXP, D_MODEL, HID], F32, kind="ExternalInput").ap()
        w_up = nc.dram_tensor("w_up", [NEXP, D_MODEL, HID], F32, kind="ExternalInput").ap()
        w_down = nc.dram_tensor("w_down", [NEXP, HID, D_MODEL], F32, kind="ExternalInput").ap()
        x_out = nc.dram_tensor("x_out", [D_MODEL, T_CORE], F32, kind="ExternalOutput").ap()
        cmb_dram = nc.dram_tensor("cmb_dram", [8, T_CORE], F32, kind="Internal").ap()

        xv = x_fm.rearrange("(c p) t -> p c t", p=128)
        xo = x_out.rearrange("(c p) t -> p c t", p=128)
        wg_v = w_gate.rearrange("e (c p) h -> e p c h", p=128)
        wu_v = w_up.rearrange("e (c p) h -> e p c h", p=128)
        wd_v = w_down.rearrange("e (ht p) d -> e p ht d", p=128)
        rw_v = router_wT.rearrange("(c p) e -> p c e", p=128)

        sb = lambda n, s, d=F32: es.enter_context(nc.sbuf_tensor(n, s, d))
        pt = lambda n, s, d=F32: es.enter_context(nc.psum_tensor(n, s, d))

        Xt = sb("Xt", [128, NCH * 512])
        xn = sb("xn", [128, NCH * T_CORE], BF16)
        sq32 = sb("sq32", [128, 512])
        a_row = sb("a_row", [1, 512])
        b_row = sb("b_row", [1, 512])
        t_row = sb("t_row", [1, 512])
        ones_col = sb("ones_col", [128, 1])
        ones_row = sb("ones_row", [1, 128])
        eps_row = sb("eps_row", [1, 1])
        rw_s = sb("rw_s", [128, NCH * 8], BF16)
        logit = sb("logit", [128, 8])
        m1 = sb("m1", [128, 1])
        m2 = sb("m2", [128, 1])
        nm1 = sb("nm1", [128, 1])
        e2 = sb("e2", [128, 1])
        rden = sb("rden", [128, 1])
        gemsk = sb("gemsk", [128, 8])
        expv = sb("expv", [128, 8])
        cmb = sb("cmb", [128, 8])
        neg_inf8 = sb("neg_inf8", [128, 8])
        msk8 = sb("msk8", [128, 8], U8)
        cmb_row1 = sb("cmb_row1", [1, 128])
        cmb_row = sb("cmb_row", [1, T_CORE])
        cmb_bc = sb("cmb_bc", [128, T_CORE])
        identw = sb("identw", [128, 128], F32)
        wgs = sb("wgs", [128, NCH * 1024], BF16)
        wus = sb("wus", [128, NCH * 1024], BF16)
        wds = sb("wds", [128, 8 * 1024], BF16)
        hact = sb("hact", [128, 8 * 512], BF16)
        silu_scr = sb("silu_scr", [128, 512])
        h_scr = sb("h_scr", [128, 512])
        mixed = sb("mixed", [128, NCH * T_CORE])
        oscr = sb("oscr", [128, 512])

        ps_a = pt("ps_a", [1, 512])
        ps_b = pt("ps_b", [1, 512])
        ps_bc1 = pt("ps_bc1", [128, 512])
        ps_bc2 = pt("ps_bc2", [128, 512])
        ps_g = pt("ps_g", [128, 512])
        ps_u = pt("ps_u", [128, 512])
        ps_dn = pt("ps_dn", [128, 512])
        ps_r = pt("ps_r", [128, 128])

        sq.step("gpsimd", lambda e: e.memset(ones_col[:], 1.0))
        sq.step("gpsimd", lambda e: e.memset(ones_row[:], 1.0))
        sq.step("gpsimd", lambda e: e.memset(eps_row[:], LN_EPS))
        sq.step("gpsimd", lambda e: e.memset(neg_inf8[:], -1e30))
        sq.step("gpsimd", lambda e: e.memset(identw[:], 1.0))
        sq.step("gpsimd", lambda e: e.affine_select(
            identw[:], identw[:], pattern=[[1, 128]], compare_op=ALU.is_equal,
            fill=0.0, base=0, channel_multiplier=-1))
        identb = sb("identb", [128, 128], BF16)
        sq.step("vector", lambda e: e.tensor_copy(identb[:], identw[:]))
        sq.dma(rw_s[:].rearrange("p (c e) -> p c e", c=NCH), rw_v, engine="gpsimd")

        for it in range(4):
            c0, w = it * 512, 512
            sq.dma(Xt[:].rearrange("p (c t) -> p c t", c=NCH), xv[:, :, c0:c0 + w])
            sq.step("tensor", lambda e, w=w: [
                e.matmul(ps_a[:, 0:w], ones_col[:], Xt[:, c * 512:c * 512 + w],
                         start=(c == 0), stop=(c == NCH - 1)) for c in range(NCH)])
            for c in range(NCH):
                sq.step("scalar", lambda e, c=c, w=w: e.activation(
                    sq32[:, 0:w], Xt[:, c * 512:c * 512 + w], AF.Square))
                sq.step("tensor", lambda e, c=c, w=w: e.matmul(
                    ps_b[:, 0:w], ones_col[:], sq32[:, 0:w],
                    start=(c == 0), stop=(c == NCH - 1)))
            sq.step("vector", lambda e, c0=c0, w=w: e.tensor_single_scalar(
                b_row[:, 0:w], ps_a[:, 0:w], 1.0 / D_MODEL, ALU.mult))
            sq.step("vector", lambda e, c0=c0, w=w: e.tensor_single_scalar(
                a_row[:, 0:w], ps_b[:, 0:w], 1.0 / D_MODEL, ALU.mult))
            sq.step("scalar", lambda e, c0=c0, w=w: e.activation(
                t_row[:, 0:w], b_row[:, 0:w], AF.Square))
            sq.step("vector", lambda e, c0=c0, w=w: e.tensor_sub(
                a_row[:, 0:w], a_row[:, 0:w], t_row[:, 0:w]))
            sq.step("scalar", lambda e, c0=c0, w=w: e.activation(
                a_row[:, 0:w], a_row[:, 0:w], AF.Ln, bias=eps_row[:]))
            sq.step("scalar", lambda e, c0=c0, w=w: e.activation(
                a_row[:, 0:w], a_row[:, 0:w], AF.Exp, scale=-0.5))
            sq.step("tensor", lambda e, c0=c0, w=w: e.matmul(
                ps_bc1[:, 0:w], ones_row[:], b_row[:, 0:w]))
            sq.step("tensor", lambda e, c0=c0, w=w: e.matmul(
                ps_bc2[:, 0:w], ones_row[:], a_row[:, 0:w]))
            for c in range(NCH):
                sq.step("vector", lambda e, c=c, w=w: e.tensor_sub(
                    sq32[:, 0:w], Xt[:, c * 512:c * 512 + w], ps_bc1[:, 0:w]))
                sq.step("vector", lambda e, c=c, c0=c0, w=w: e.tensor_mul(
                    xn[:, c * T_CORE + c0:c * T_CORE + c0 + w], sq32[:, 0:w],
                    ps_bc2[:, 0:w]))

        for q in range(T_CORE // 128):
            sq.step("tensor", lambda e, q=q: [
                e.matmul(ps_r[:, 0:8],
                         xn[:, c * T_CORE + q * 128:c * T_CORE + (q + 1) * 128],
                         rw_s[:, c * 8:(c + 1) * 8],
                         start=(c == 0), stop=(c == NCH - 1)) for c in range(NCH)])
            sq.step("vector", lambda e: e.tensor_copy(logit[:], ps_r[:, 0:8]))
            sq.step("vector", lambda e: e.reduce_max(m1[:], logit[:, 0:7], axis=AX.X))
            sq.step("vector", lambda e: e.tensor_scalar(
                msk8[:, 0:7], logit[:, 0:7], m1[:], None, ALU.is_ge))
            sq.step("vector", lambda e: e.select(
                sq32[:, 0:7], msk8[:, 0:7], neg_inf8[:, 0:7], logit[:, 0:7]))
            sq.step("vector", lambda e: e.reduce_max(m2[:], sq32[:, 0:7], axis=AX.X))
            sq.step("vector", lambda e: e.tensor_single_scalar(
                nm1[:], m1[:], -1.0, ALU.mult))
            sq.step("scalar", lambda e: e.activation(e2[:], m2[:], AF.Exp, bias=nm1[:]))
            sq.step("vector", lambda e: e.tensor_single_scalar(
                e2[:], e2[:], 1.0, ALU.add))
            sq.step("vector", lambda e: e.reciprocal(rden[:], e2[:]))
            sq.step("vector", lambda e: e.tensor_scalar(
                gemsk[:, 0:7], logit[:, 0:7], m2[:], None, ALU.is_ge))
            sq.step("scalar", lambda e: e.activation(
                expv[:, 0:7], logit[:, 0:7], AF.Exp, bias=nm1[:]))
            sq.step("vector", lambda e: e.tensor_mul(
                cmb[:, 0:7], expv[:, 0:7], gemsk[:, 0:7]))
            sq.step("vector", lambda e: e.tensor_scalar(
                cmb[:, 0:7], cmb[:, 0:7], rden[:], None, ALU.mult))
            sq.step("gpsimd", lambda e: e.memset(cmb[:, 7:8], 1.0))
            for ee in range(NEXP):
                sq.step("tensor", lambda e, ee=ee: e.transpose(
                    ps_r[0:1, 0:128], cmb[:, ee:ee + 1], identw[:]))
                sq.step("vector", lambda e: e.tensor_copy(
                    cmb_row1[:], ps_r[0:1, 0:128]))
                sq.dma(cmb_dram[ee:ee + 1, q * 128:(q + 1) * 128], cmb_row1[:])

        sq.step("gpsimd", lambda e: e.memset(mixed[:], 0.0))
        for ee in range(NEXP):
            sq.dma(cmb_row[:], cmb_dram[ee:ee + 1, :])
            for tt in range(NT):
                sq.step("tensor", lambda e, ee=ee, tt=tt: e.matmul(
                    ps_bc1[:], ones_row[:],
                    cmb_row[:, tt * T:(tt + 1) * T]))
                sq.step("vector", lambda e, tt=tt: e.tensor_copy(
                    cmb_bc[:, tt * T:(tt + 1) * T], ps_bc1[:]))
            for hg in range(NHG):
                sq.dma(wgs[:].rearrange("p (c h) -> p c h", c=NCH),
                       wg_v[ee, :, :, hg * 1024:(hg + 1) * 1024], engine="gpsimd")
                sq.dma(wus[:].rearrange("p (c h) -> p c h", c=NCH),
                       wu_v[ee, :, :, hg * 1024:(hg + 1) * 1024], engine="gpsimd")
                sq.dma(wds[:].rearrange("p (ht d) -> p ht d", ht=8),
                       wd_v[ee, :, hg * 8:(hg + 1) * 8, :], engine="gpsimd")
                for tt in range(NT):
                    for ht in range(8):
                        sq.step("tensor", lambda e, ht=ht, tt=tt: [
                            e.matmul(ps_g[:],
                                     wgs[:, c * 1024 + ht * 128:c * 1024 + (ht + 1) * 128],
                                     xn[:, c * T_CORE + tt * T:c * T_CORE + tt * T + T],
                                     start=(c == 0), stop=(c == NCH - 1))
                            for c in range(NCH)])
                        sq.step("tensor", lambda e, ht=ht, tt=tt: [
                            e.matmul(ps_u[:],
                                     wus[:, c * 1024 + ht * 128:c * 1024 + (ht + 1) * 128],
                                     xn[:, c * T_CORE + tt * T:c * T_CORE + tt * T + T],
                                     start=(c == 0), stop=(c == NCH - 1))
                            for c in range(NCH)])
                        sq.step("scalar", lambda e: e.activation(
                            silu_scr[:], ps_g[:], AF.Silu))
                        sq.step("vector", lambda e: e.tensor_mul(
                            h_scr[:], silu_scr[:], ps_u[:]))
                        sq.step("vector", lambda e, ht=ht, tt=tt: e.tensor_mul(
                            hact[:, ht * 512:(ht + 1) * 512], h_scr[:],
                            cmb_bc[:, tt * T:(tt + 1) * T]))
                    for dm in range(NCH):
                        sq.step("tensor", lambda e, dm=dm: [
                            e.matmul(ps_dn[:],
                                     wds[:, ht * 1024 + dm * 128:ht * 1024 + (dm + 1) * 128],
                                     hact[:, ht * 512:(ht + 1) * 512],
                                     start=(ht == 0), stop=(ht == 7))
                            for ht in range(8)])
                        sq.step("vector", lambda e, dm=dm, tt=tt: e.tensor_add(
                            mixed[:, dm * T_CORE + tt * T:dm * T_CORE + tt * T + T],
                            mixed[:, dm * T_CORE + tt * T:dm * T_CORE + tt * T + T],
                            ps_dn[:]))

        for tt in range(NT):
            sq.dma(Xt[:].rearrange("p (c t) -> p c t", c=NCH), xv[:, :, tt * T:(tt + 1) * T])
            for c in range(NCH):
                sq.step("vector", lambda e, c=c, tt=tt: e.tensor_add(
                    oscr[:], mixed[:, c * T_CORE + tt * T:c * T_CORE + tt * T + T],
                    Xt[:, c * 512:(c + 1) * 512]))
                sq.dma(xo[:, c, tt * T:(tt + 1) * T], oscr[:])

    return seq_program(prog)


def host_l0_ffn_inputs(x, weights, core):
    sl = core_slice_tokens(x, core)
    rw = np.full((8, D_MODEL), -1e30, np.float32)
    rw[0:7] = np.asarray(weights["router_w"], np.float32)
    wg = np.concatenate([np.asarray(weights["w_gate"], np.float32),
                         np.asarray(weights["sw_gate"], np.float32)], axis=0)
    wu = np.concatenate([np.asarray(weights["w_up"], np.float32),
                         np.asarray(weights["sw_up"], np.float32)], axis=0)
    wd = np.concatenate([np.asarray(weights["w_down"], np.float32),
                         np.asarray(weights["sw_down"], np.float32)], axis=0)
    return {
        "x_fm": fmT(sl),
        "router_wT": fmT(rw),
        "w_gate": np.ascontiguousarray(wg),
        "w_up": np.ascontiguousarray(wu),
        "w_down": np.ascontiguousarray(wd),
    }



D_INNER = 2048
NEIN = D_INNER // 128
D_STATE = 16


def build_mamba_pre():
    TE = T_CORE + 3
    COLT = [(0, 512), (512, 512), (1024, 512), (1536, 512), (2048, 3)]
    TT4 = [(0, 512), (512, 512), (1024, 512), (1536, 512)]

    def prog(nc, sq, es):
        x_ext = nc.dram_tensor("x_ext", [D_MODEL, TE], F32, kind="ExternalInput").ap()
        in_projT = nc.dram_tensor("in_projT", [D_MODEL, 2 * D_INNER], F32, kind="ExternalInput").ap()
        x_projT = nc.dram_tensor("x_projT", [D_INNER, 2080], F32, kind="ExternalInput").ap()
        dt_projT = nc.dram_tensor("dt_projT", [D_INNER, D_INNER], F32, kind="ExternalInput").ap()
        conv_w = nc.dram_tensor("conv_w", [128, 64], F32, kind="ExternalInput").ap()
        conv_b = nc.dram_tensor("conv_b", [128, NEIN], F32, kind="ExternalInput").ap()
        dt_proj_b = nc.dram_tensor("dt_proj_b", [128, NEIN], F32, kind="ExternalInput").ap()
        dt_o = nc.dram_tensor("dt_o", [D_INNER, T_CORE], F32, kind="ExternalOutput").ap()
        xin_o = nc.dram_tensor("xin_o", [D_INNER, T_CORE], F32, kind="ExternalOutput").ap()
        zs_o = nc.dram_tensor("zs_o", [D_INNER, T_CORE], F32, kind="ExternalOutput").ap()
        B_o = nc.dram_tensor("B_o", [D_STATE, T_CORE], F32, kind="ExternalOutput").ap()
        C_o = nc.dram_tensor("C_o", [D_STATE, T_CORE], F32, kind="ExternalOutput").ap()

        xe = x_ext.rearrange("(c p) t -> p c t", p=128)
        ipv = in_projT.rearrange("(c p) e -> p c e", p=128)
        xpv = x_projT.rearrange("(c p) e -> p c e", p=128)
        dpv = dt_projT.rearrange("(c p) e -> p c e", p=128)
        dt_v = dt_o.rearrange("(c p) t -> p c t", p=128)
        xin_v = xin_o.rearrange("(c p) t -> p c t", p=128)
        zs_v = zs_o.rearrange("(c p) t -> p c t", p=128)

        sb = lambda n, s, d=F32: es.enter_context(nc.sbuf_tensor(n, s, d))
        pt = lambda n, s, d=F32: es.enter_context(nc.psum_tensor(n, s, d))

        ones_col = sb("ones_col", [128, 1])
        ones_row = sb("ones_row", [1, 128])
        eps_row = sb("eps_row", [1, 1])
        Xt = sb("Xt", [128, NCH * 512])
        sq32 = sb("sq32", [128, 512])
        scr2 = sb("scr2", [128, 512])
        a_row = sb("a_row", [1, 512])
        b_row = sb("b_row", [1, 512])
        t_row = sb("t_row", [1, 512])
        xn = sb("xn", [128, NCH * TE], BF16)
        wsl = sb("wsl", [128, NCH * 128], BF16)
        wsl2 = sb("wsl2", [128, NEIN * 128], BF16)
        xraw = sb("xraw", [128, TE], BF16)
        xconv = sb("xconv", [128, NEIN * T_CORE], BF16)
        din = sb("din", [128, NEIN * T_CORE], BF16)
        convw_s = sb("convw_s", [128, 64])
        convb_s = sb("convb_s", [128, NEIN])
        dtb_s = sb("dtb_s", [128, NEIN])
        bc_scr = sb("bc_scr", [16, 512])

        ps_a = pt("ps_a", [1, 512])
        ps_b = pt("ps_b", [1, 512])
        ps_bc1 = pt("ps_bc1", [128, 512])
        ps_bc2 = pt("ps_bc2", [128, 512])
        ps_mm = pt("ps_mm", [128, 512])
        ps_sm = pt("ps_sm", [16, 512])

        sq.step("gpsimd", lambda e: e.memset(ones_col[:], 1.0))
        sq.step("gpsimd", lambda e: e.memset(ones_row[:], 1.0))
        sq.step("gpsimd", lambda e: e.memset(eps_row[:], LN_EPS))
        sq.dma(convw_s[:], conv_w)
        sq.dma(convb_s[:], conv_b)
        sq.dma(dtb_s[:], dt_proj_b)

        for (c0, w) in COLT:
            slow = {"allow_slow_non_contiguous": True} if w < 8 else {}
            sq.dma(Xt[:].rearrange("p (c t) -> p c t", c=NCH)[:, :, 0:w],
                   xe[:, :, c0:c0 + w], **slow)
            sq.step("tensor", lambda e, w=w: [
                e.matmul(ps_a[:, 0:w], ones_col[:], Xt[:, c * 512:c * 512 + w],
                         start=(c == 0), stop=(c == NCH - 1)) for c in range(NCH)])
            for c in range(NCH):
                sq.step("scalar", lambda e, c=c, w=w: e.activation(
                    sq32[:, 0:w], Xt[:, c * 512:c * 512 + w], AF.Square))
                sq.step("tensor", lambda e, c=c, w=w: e.matmul(
                    ps_b[:, 0:w], ones_col[:], sq32[:, 0:w],
                    start=(c == 0), stop=(c == NCH - 1)))
            sq.step("vector", lambda e, w=w: e.tensor_single_scalar(
                b_row[:, 0:w], ps_a[:, 0:w], 1.0 / D_MODEL, ALU.mult))
            sq.step("vector", lambda e, w=w: e.tensor_single_scalar(
                a_row[:, 0:w], ps_b[:, 0:w], 1.0 / D_MODEL, ALU.mult))
            sq.step("scalar", lambda e, w=w: e.activation(
                t_row[:, 0:w], b_row[:, 0:w], AF.Square))
            sq.step("vector", lambda e, w=w: e.tensor_sub(
                a_row[:, 0:w], a_row[:, 0:w], t_row[:, 0:w]))
            sq.step("scalar", lambda e, w=w: e.activation(
                a_row[:, 0:w], a_row[:, 0:w], AF.Ln, bias=eps_row[:]))
            sq.step("scalar", lambda e, w=w: e.activation(
                a_row[:, 0:w], a_row[:, 0:w], AF.Exp, scale=-0.5))
            sq.step("tensor", lambda e, w=w: e.matmul(
                ps_bc1[:, 0:w], ones_row[:], b_row[:, 0:w]))
            sq.step("tensor", lambda e, w=w: e.matmul(
                ps_bc2[:, 0:w], ones_row[:], a_row[:, 0:w]))
            for c in range(NCH):
                sq.step("vector", lambda e, c=c, w=w: e.tensor_sub(
                    sq32[:, 0:w], Xt[:, c * 512:c * 512 + w], ps_bc1[:, 0:w]))
                sq.step("vector", lambda e, c=c, c0=c0, w=w: e.tensor_mul(
                    xn[:, c * TE + c0:c * TE + c0 + w], sq32[:, 0:w], ps_bc2[:, 0:w]))

        for et in range(NEIN):
            sq.dma(wsl[:].rearrange("p (c d) -> p c d", c=NCH),
                   ipv[:, :, et * 128:(et + 1) * 128], engine="gpsimd")
            for (c0, w) in COLT:
                sq.step("tensor", lambda e, et=et, c0=c0, w=w: [
                    e.matmul(ps_mm[:, 0:w], wsl[:, c * 128:(c + 1) * 128],
                             xn[:, c * TE + c0:c * TE + c0 + w],
                             start=(c == 0), stop=(c == NCH - 1)) for c in range(NCH)])
                sq.step("vector", lambda e, c0=c0, w=w: e.tensor_copy(
                    xraw[:, c0:c0 + w], ps_mm[:, 0:w]))
            for (t0, w) in TT4:
                sq.step("vector", lambda e, et=et, t0=t0, w=w: e.tensor_scalar(
                    sq32[:, 0:w], xraw[:, t0:t0 + w], convw_s[:, et * 4:et * 4 + 1],
                    None, ALU.mult))
                for j in range(1, 4):
                    sq.step("vector", lambda e, et=et, t0=t0, w=w, j=j:
                        e.scalar_tensor_tensor(
                            sq32[:, 0:w], xraw[:, t0 + j:t0 + j + w],
                            convw_s[:, et * 4 + j:et * 4 + j + 1],
                            sq32[:, 0:w], ALU.mult, ALU.add))
                sq.step("scalar", lambda e, et=et, t0=t0, w=w: e.activation(
                    xconv[:, et * T_CORE + t0:et * T_CORE + t0 + w], sq32[:, 0:w],
                    AF.Silu, bias=convb_s[:, et:et + 1]))
                sq.step("vector", lambda e, et=et, t0=t0, w=w: e.tensor_copy(
                    scr2[:, 0:w], xconv[:, et * T_CORE + t0:et * T_CORE + t0 + w]))
                sq.dma(xin_v[:, et, t0:t0 + w], scr2[:, 0:w])

        for et in range(NEIN):
            sq.dma(wsl[:].rearrange("p (c d) -> p c d", c=NCH),
                   ipv[:, :, D_INNER + et * 128:D_INNER + (et + 1) * 128],
                   engine="gpsimd")
            for (t0, w) in TT4:
                sq.step("tensor", lambda e, t0=t0, w=w: [
                    e.matmul(ps_mm[:, 0:w], wsl[:, c * 128:(c + 1) * 128],
                             xn[:, c * TE + 3 + t0:c * TE + 3 + t0 + w],
                             start=(c == 0), stop=(c == NCH - 1)) for c in range(NCH)])
                sq.step("scalar", lambda e, w=w: e.activation(
                    scr2[:, 0:w], ps_mm[:, 0:w], AF.Silu))
                sq.dma(zs_v[:, et, t0:t0 + w], scr2[:, 0:w])

        for et in range(NEIN):
            sq.dma(wsl2[:].rearrange("p (c d) -> p c d", c=NEIN),
                   xpv[:, :, et * 128:(et + 1) * 128], engine="gpsimd")
            for (t0, w) in TT4:
                sq.step("tensor", lambda e, t0=t0, w=w: [
                    e.matmul(ps_mm[:, 0:w], wsl2[:, c * 128:(c + 1) * 128],
                             xconv[:, c * T_CORE + t0:c * T_CORE + t0 + w],
                             start=(c == 0), stop=(c == NEIN - 1)) for c in range(NEIN)])
                sq.step("vector", lambda e, et=et, t0=t0, w=w: e.tensor_copy(
                    din[:, et * T_CORE + t0:et * T_CORE + t0 + w], ps_mm[:, 0:w]))
        for bi, out_t in ((0, B_o), (1, C_o)):
            sq.dma(wsl2[:, 0:NEIN * 16].rearrange("p (c d) -> p c d", c=NEIN),
                   xpv[:, :, D_INNER + bi * 16:D_INNER + (bi + 1) * 16],
                   engine="gpsimd")
            for (t0, w) in TT4:
                sq.step("tensor", lambda e, t0=t0, w=w: [
                    e.matmul(ps_sm[:, 0:w], wsl2[:, c * 16:(c + 1) * 16],
                             xconv[:, c * T_CORE + t0:c * T_CORE + t0 + w],
                             start=(c == 0), stop=(c == NEIN - 1)) for c in range(NEIN)])
                sq.step("vector", lambda e, w=w: e.tensor_copy(
                    bc_scr[:, 0:w], ps_sm[:, 0:w]))
                sq.dma(out_t[:, t0:t0 + w], bc_scr[:, 0:w])

        for et in range(NEIN):
            sq.dma(wsl2[:].rearrange("p (c d) -> p c d", c=NEIN),
                   dpv[:, :, et * 128:(et + 1) * 128], engine="gpsimd")
            for (t0, w) in TT4:
                sq.step("tensor", lambda e, t0=t0, w=w: [
                    e.matmul(ps_mm[:, 0:w], wsl2[:, c * 128:(c + 1) * 128],
                             din[:, c * T_CORE + t0:c * T_CORE + t0 + w],
                             start=(c == 0), stop=(c == NEIN - 1)) for c in range(NEIN)])
                sq.step("vector", lambda e, et=et, w=w: e.tensor_scalar(
                    sq32[:, 0:w], ps_mm[:, 0:w], dtb_s[:, et:et + 1], None, ALU.add))
                sq.step("scalar", lambda e, w=w: e.activation(
                    scr2[:, 0:w], sq32[:, 0:w], AF.Abs))
                sq.step("scalar", lambda e, w=w: e.activation(
                    scr2[:, 0:w], scr2[:, 0:w], AF.Exp, scale=-1.0))
                sq.step("scalar", lambda e, w=w: e.activation(
                    scr2[:, 0:w], scr2[:, 0:w], AF.Ln, bias=1.0))
                sq.step("vector", lambda e, w=w: e.tensor_single_scalar(
                    sq32[:, 0:w], sq32[:, 0:w], 0.0, ALU.max))
                sq.step("vector", lambda e, w=w: e.tensor_add(
                    sq32[:, 0:w], sq32[:, 0:w], scr2[:, 0:w]))
                sq.dma(dt_v[:, et, t0:t0 + w], sq32[:, 0:w])

    return seq_program(prog)


def build_mamba_scan():
    L = 4096
    NDT = 8
    NTC = 8

    def prog(nc, sq, es):
        dt_i = nc.dram_tensor("dt_i", [1024, L], F32, kind="ExternalInput").ap()
        xin_i = nc.dram_tensor("xin_i", [1024, L], F32, kind="ExternalInput").ap()
        B_i = nc.dram_tensor("B_i", [D_STATE, L], F32, kind="ExternalInput").ap()
        C_i = nc.dram_tensor("C_i", [D_STATE, L], F32, kind="ExternalInput").ap()
        Alog_i = nc.dram_tensor("Alog_i", [128, NDT * D_STATE], F32, kind="ExternalInput").ap()
        D_i = nc.dram_tensor("D_i", [128, NDT], F32, kind="ExternalInput").ap()
        y_o = nc.dram_tensor("y_o", [1024, L], F32, kind="ExternalOutput").ap()

        dt_vv = dt_i.rearrange("(c p) t -> p c t", p=128)
        xin_vv = xin_i.rearrange("(c p) t -> p c t", p=128)
        y_vv = y_o.rearrange("(c p) t -> p c t", p=128)

        sb = lambda n, s, d=F32: es.enter_context(nc.sbuf_tensor(n, s, d))
        pt = lambda n, s, d=F32: es.enter_context(nc.psum_tensor(n, s, d))

        ones_row = sb("ones_row", [1, 128])
        dt_s = sb("dt_s", [128, NDT * 512])
        xin_s = sb("xin_s", [128, NDT * 512])
        dtx = sb("dtx", [128, NDT * 512])
        y_s = sb("y_s", [128, NDT * 512])
        A_s = sb("A_s", [128, NDT * D_STATE])
        D_s = sb("D_s", [128, NDT])
        h_st = sb("h_st", [128, NDT * D_STATE])
        B_row = sb("B_row", [1, 512])
        C_row = sb("C_row", [1, 512])
        Brep = sb("Brep", [128, 512])
        Crep = sb("Crep", [128, 512])
        dA = sb("dA", [128, 512])
        u_s = sb("u_s", [128, 512])
        hs = sb("hs", [128, 512])
        t1 = sb("t1", [128, 512])

        ps_b1 = pt("ps_b1", [128, 512])
        ps_b2 = pt("ps_b2", [128, 512])

        sq.step("gpsimd", lambda e: e.memset(ones_row[:], 1.0))
        sq.step("gpsimd", lambda e: e.memset(h_st[:], 0.0))
        sq.dma(A_s[:], Alog_i)
        sq.dma(D_s[:], D_i)
        sq.step("scalar", lambda e: e.activation(A_s[:], A_s[:], AF.Exp))
        sq.step("vector", lambda e: e.tensor_single_scalar(A_s[:], A_s[:], -1.0, ALU.mult))

        for tc in range(NTC):
            t0 = tc * 512
            sq.dma(dt_s[:].rearrange("p (c t) -> p c t", c=NDT), dt_vv[:, :, t0:t0 + 512])
            sq.dma(xin_s[:].rearrange("p (c t) -> p c t", c=NDT), xin_vv[:, :, t0:t0 + 512])
            for d in range(NDT):
                sq.step("vector", lambda e, d=d: e.tensor_mul(
                    dtx[:, d * 512:(d + 1) * 512], dt_s[:, d * 512:(d + 1) * 512],
                    xin_s[:, d * 512:(d + 1) * 512]))
            sq.step("gpsimd", lambda e: e.memset(y_s[:], 0.0))
            for s in range(D_STATE):
                sq.dma(B_row[:], B_i[s:s + 1, t0:t0 + 512])
                sq.dma(C_row[:], C_i[s:s + 1, t0:t0 + 512])
                sq.step("tensor", lambda e: e.matmul(ps_b1[:], ones_row[:], B_row[:]))
                sq.step("tensor", lambda e: e.matmul(ps_b2[:], ones_row[:], C_row[:]))
                sq.step("vector", lambda e: e.tensor_copy(Brep[:], ps_b1[:]))
                sq.step("vector", lambda e: e.tensor_copy(Crep[:], ps_b2[:]))
                for d in range(NDT):
                    col = d * D_STATE + s
                    sq.step("scalar", lambda e, d=d, col=col: e.activation(
                        dA[:], dt_s[:, d * 512:(d + 1) * 512], AF.Exp,
                        scale=A_s[:, col:col + 1]))
                    sq.step("vector", lambda e, d=d: e.tensor_mul(
                        u_s[:], dtx[:, d * 512:(d + 1) * 512], Brep[:]))
                    sq.step("vector", lambda e, col=col: e.tensor_tensor_scan(
                        hs[:], dA[:], u_s[:], h_st[:, col:col + 1],
                        ALU.mult, ALU.add))
                    sq.step("vector", lambda e, col=col: e.tensor_copy(
                        h_st[:, col:col + 1], hs[:, 511:512]))
                    sq.step("vector", lambda e: e.tensor_mul(t1[:], hs[:], Crep[:]))
                    sq.step("vector", lambda e, d=d: e.tensor_add(
                        y_s[:, d * 512:(d + 1) * 512],
                        y_s[:, d * 512:(d + 1) * 512], t1[:]))
            for d in range(NDT):
                sq.step("vector", lambda e, d=d: e.scalar_tensor_tensor(
                    y_s[:, d * 512:(d + 1) * 512], xin_s[:, d * 512:(d + 1) * 512],
                    D_s[:, d:d + 1], y_s[:, d * 512:(d + 1) * 512],
                    ALU.mult, ALU.add))
                sq.dma(y_vv[:, d, t0:t0 + 512], y_s[:, d * 512:(d + 1) * 512])

    return seq_program(prog)


def build_mamba_post():
    NT, T = 4, 512

    def prog(nc, sq, es):
        x_fm = nc.dram_tensor("x_fm", [D_MODEL, T_CORE], F32, kind="ExternalInput").ap()
        y_fm = nc.dram_tensor("y_fm", [D_INNER, T_CORE], F32, kind="ExternalInput").ap()
        zs_fm = nc.dram_tensor("zs_fm", [D_INNER, T_CORE], F32, kind="ExternalInput").ap()
        opT = nc.dram_tensor("opT", [D_INNER, D_MODEL], F32, kind="ExternalInput").ap()
        x_out = nc.dram_tensor("x_out", [D_MODEL, T_CORE], F32, kind="ExternalOutput").ap()

        xv = x_fm.rearrange("(c p) t -> p c t", p=128)
        yv = y_fm.rearrange("(c p) t -> p c t", p=128)
        zv = zs_fm.rearrange("(c p) t -> p c t", p=128)
        ov = opT.rearrange("(c p) d -> p c d", p=128)
        xo = x_out.rearrange("(c p) t -> p c t", p=128)

        sb = lambda n, s, d=F32: es.enter_context(nc.sbuf_tensor(n, s, d))
        pt = lambda n, s, d=F32: es.enter_context(nc.psum_tensor(n, s, d))

        w_s = sb("w_s", [128, NEIN * D_MODEL], BF16)
        y_t = sb("y_t", [128, NEIN * T])
        z_t = sb("z_t", [128, NEIN * T])
        g_t = sb("g_t", [128, NEIN * T], BF16)
        Xt = sb("Xt", [128, NCH * T])
        oscr = sb("oscr", [128, T])
        ps_mm = pt("ps_mm", [128, 512])

        sq.dma(w_s[:].rearrange("p (c d) -> p c d", c=NEIN), ov, engine="gpsimd")
        for tt in range(NT):
            t0 = tt * T
            sq.dma(y_t[:].rearrange("p (c t) -> p c t", c=NEIN), yv[:, :, t0:t0 + T])
            sq.dma(z_t[:].rearrange("p (c t) -> p c t", c=NEIN), zv[:, :, t0:t0 + T])
            sq.dma(Xt[:].rearrange("p (c t) -> p c t", c=NCH), xv[:, :, t0:t0 + T])
            for c in range(NEIN):
                sq.step("vector", lambda e, c=c: e.tensor_mul(
                    g_t[:, c * T:(c + 1) * T], y_t[:, c * T:(c + 1) * T],
                    z_t[:, c * T:(c + 1) * T]))
            for d in range(NCH):
                sq.step("tensor", lambda e, d=d: [
                    e.matmul(ps_mm[:], w_s[:, c * D_MODEL + d * 128:
                                           c * D_MODEL + (d + 1) * 128],
                             g_t[:, c * T:(c + 1) * T],
                             start=(c == 0), stop=(c == NEIN - 1))
                    for c in range(NEIN)])
                sq.step("vector", lambda e, d=d: e.tensor_add(
                    oscr[:], ps_mm[:], Xt[:, d * T:(d + 1) * T]))
                sq.dma(xo[:, d, t0:t0 + T], oscr[:])

    return seq_program(prog)


def host_mamba_pre_inputs(x, W, core):
    b, h = core // 2, core % 2
    sl = core_slice_tokens(x, core)
    if h == 0:
        halo = np.zeros((3, D_MODEL), np.float32)
    else:
        halo = np.asarray(x[b, h * 2048 - 3:h * 2048])
    x_ext = fmT(np.concatenate([halo, sl], axis=0))
    cw = np.asarray(W["conv_w"], np.float32).reshape(D_INNER, 4)
    cw_p = np.ascontiguousarray(cw.reshape(NEIN, 128, 4).transpose(1, 0, 2).reshape(128, 64))
    return {
        "x_ext": x_ext,
        "in_projT": fmT(W["in_proj_w"]),
        "x_projT": fmT(W["x_proj_w"]),
        "dt_projT": fmT(W["dt_proj_w"]),
        "conv_w": cw_p,
        "conv_b": pack_vec(W["conv_b"]),
        "dt_proj_b": pack_vec(W["dt_proj_b"]),
    }


def host_mamba_scan_inputs(pre_outs, W, core):
    b, j = core // 2, core % 2
    def full(key, rows):
        return np.concatenate([pre_outs[2 * b][key], pre_outs[2 * b + 1][key]], axis=1)
    dt_f = full("dt_o", D_INNER)[j * 1024:(j + 1) * 1024]
    xin_f = full("xin_o", D_INNER)[j * 1024:(j + 1) * 1024]
    B_f = full("B_o", D_STATE)
    C_f = full("C_o", D_STATE)
    Alog = np.asarray(W["A_log"], np.float32)[j * 1024:(j + 1) * 1024]
    Alog_p = np.ascontiguousarray(
        Alog.reshape(8, 128, D_STATE).transpose(1, 0, 2).reshape(128, 8 * D_STATE))
    Dv = np.asarray(W["D"], np.float32)[j * 1024:(j + 1) * 1024]
    return {
        "dt_i": np.ascontiguousarray(dt_f), "xin_i": np.ascontiguousarray(xin_f),
        "B_i": np.ascontiguousarray(B_f), "C_i": np.ascontiguousarray(C_f),
        "Alog_i": Alog_p, "D_i": pack_vec(Dv),
    }


def host_mamba_post_inputs(x, scan_outs, pre_outs, W, core):
    b, h = core // 2, core % 2
    y_full = np.concatenate([scan_outs[2 * b]["y_o"], scan_outs[2 * b + 1]["y_o"]],
                            axis=0)
    y_slice = y_full[:, h * 2048:(h + 1) * 2048]
    zs = pre_outs[core]["zs_o"]
    return {
        "x_fm": fmT(core_slice_tokens(x, core)),
        "y_fm": np.ascontiguousarray(y_slice),
        "zs_fm": np.ascontiguousarray(zs),
        "opT": fmT(W["out_proj_w"]),
    }



def build_wkv_kv():
    NT, T = 4, 512

    def prog(nc, sq, es):
        x_fm = nc.dram_tensor("x_fm", [D_MODEL, T_CORE], F32, kind="ExternalInput").ap()
        xp_fm = nc.dram_tensor("xp_fm", [D_MODEL, T_CORE], F32, kind="ExternalInput").ap()
        wkT = nc.dram_tensor("wkT", [D_MODEL, D_MODEL], F32, kind="ExternalInput").ap()
        wvT = nc.dram_tensor("wvT", [D_MODEL, D_MODEL], F32, kind="ExternalInput").ap()
        wrT = nc.dram_tensor("wrT", [D_MODEL, D_MODEL], F32, kind="ExternalInput").ap()
        mu_k = nc.dram_tensor("mu_k", [128, NCH], F32, kind="ExternalInput").ap()
        mu_v = nc.dram_tensor("mu_v", [128, NCH], F32, kind="ExternalInput").ap()
        mu_r = nc.dram_tensor("mu_r", [128, NCH], F32, kind="ExternalInput").ap()
        ek_o = nc.dram_tensor("ek_o", [D_MODEL, T_CORE], F32, kind="ExternalOutput").ap()
        ekv_o = nc.dram_tensor("ekv_o", [D_MODEL, T_CORE], F32, kind="ExternalOutput").ap()
        r_o = nc.dram_tensor("r_o", [D_MODEL, T_CORE], F32, kind="ExternalOutput").ap()

        xv_ = x_fm.rearrange("(c p) t -> p c t", p=128)
        xpv = xp_fm.rearrange("(c p) t -> p c t", p=128)
        eko = ek_o.rearrange("(c p) t -> p c t", p=128)
        ekvo = ekv_o.rearrange("(c p) t -> p c t", p=128)
        ro = r_o.rearrange("(c p) t -> p c t", p=128)

        sb = lambda n, s, d=F32: es.enter_context(nc.sbuf_tensor(n, s, d))
        pt = lambda n, s, d=F32: es.enter_context(nc.psum_tensor(n, s, d))

        ones_col = sb("ones_col", [128, 1])
        ones_row = sb("ones_row", [1, 128])
        eps_row = sb("eps_row", [1, 1])
        Xt = sb("Xt", [128, NCH * 512])
        sq32 = sb("sq32", [128, 512])
        a_row = sb("a_row", [1, 512])
        b_row = sb("b_row", [1, 512])
        t_row = sb("t_row", [1, 512])
        xn = sb("xn", [128, NCH * T_CORE], BF16)
        xnp = sb("xnp", [128, NCH * T_CORE], BF16)
        w_s = sb("w_s", [128, NCH * D_MODEL], BF16)
        mus = {nm: sb(f"mus_{nm}", [128, NCH]) for nm in ("k", "v", "r")}
        xm = sb("xm", [128, NCH * T], BF16)
        ek_t = sb("ek_t", [128, 512])
        v_t = sb("v_t", [128, 512])
        oscr = sb("oscr", [128, 512])
        ps_a = pt("ps_a", [1, 512])
        ps_b = pt("ps_b", [1, 512])
        ps_bc1 = pt("ps_bc1", [128, 512])
        ps_bc2 = pt("ps_bc2", [128, 512])
        ps_mm = pt("ps_mm", [128, 512])
        ps_mm2 = pt("ps_mm2", [128, 512])

        sq.step("gpsimd", lambda e: e.memset(ones_col[:], 1.0))
        sq.step("gpsimd", lambda e: e.memset(ones_row[:], 1.0))
        sq.step("gpsimd", lambda e: e.memset(eps_row[:], LN_EPS))
        for nm, t in mus.items():
            sq.dma(t[:], {"k": mu_k, "v": mu_v, "r": mu_r}[nm])

        def ln_to(xsrc, dst):
            for it in range(4):
                c0, w = it * 512, 512
                sq.dma(Xt[:].rearrange("p (c t) -> p c t", c=NCH), xsrc[:, :, c0:c0 + w])
                sq.step("tensor", lambda e, w=w: [
                    e.matmul(ps_a[:, 0:w], ones_col[:], Xt[:, c * 512:c * 512 + w],
                             start=(c == 0), stop=(c == NCH - 1)) for c in range(NCH)])
                for c in range(NCH):
                    sq.step("scalar", lambda e, c=c, w=w: e.activation(
                        sq32[:, 0:w], Xt[:, c * 512:c * 512 + w], AF.Square))
                    sq.step("tensor", lambda e, c=c, w=w: e.matmul(
                        ps_b[:, 0:w], ones_col[:], sq32[:, 0:w],
                        start=(c == 0), stop=(c == NCH - 1)))
                sq.step("vector", lambda e, w=w: e.tensor_single_scalar(
                    b_row[:, 0:w], ps_a[:, 0:w], 1.0 / D_MODEL, ALU.mult))
                sq.step("vector", lambda e, w=w: e.tensor_single_scalar(
                    a_row[:, 0:w], ps_b[:, 0:w], 1.0 / D_MODEL, ALU.mult))
                sq.step("scalar", lambda e, w=w: e.activation(
                    t_row[:, 0:w], b_row[:, 0:w], AF.Square))
                sq.step("vector", lambda e, w=w: e.tensor_sub(
                    a_row[:, 0:w], a_row[:, 0:w], t_row[:, 0:w]))
                sq.step("scalar", lambda e, w=w: e.activation(
                    a_row[:, 0:w], a_row[:, 0:w], AF.Ln, bias=eps_row[:]))
                sq.step("scalar", lambda e, w=w: e.activation(
                    a_row[:, 0:w], a_row[:, 0:w], AF.Exp, scale=-0.5))
                sq.step("tensor", lambda e, w=w: e.matmul(
                    ps_bc1[:, 0:w], ones_row[:], b_row[:, 0:w]))
                sq.step("tensor", lambda e, w=w: e.matmul(
                    ps_bc2[:, 0:w], ones_row[:], a_row[:, 0:w]))
                for c in range(NCH):
                    sq.step("vector", lambda e, c=c, w=w: e.tensor_sub(
                        sq32[:, 0:w], Xt[:, c * 512:c * 512 + w], ps_bc1[:, 0:w]))
                    sq.step("vector", lambda e, c=c, c0=c0, w=w: e.tensor_mul(
                        dst[:, c * T_CORE + c0:c * T_CORE + c0 + w], sq32[:, 0:w],
                        ps_bc2[:, 0:w]))

        ln_to(xv_, xn)
        ln_to(xpv, xnp)

        for kind in ("k", "v", "r"):
            wsrc = {"k": wkT, "v": wvT, "r": wrT}[kind]
            sq.dma(w_s[:].rearrange("p (c d) -> p c d", c=NCH),
                   wsrc.rearrange("(c p) d -> p c d", p=128), engine="gpsimd")
            mu = mus[kind]
            for tt in range(NT):
                t0 = tt * T
                for c in range(NCH):
                    sq.step("vector", lambda e, c=c, t0=t0: e.tensor_sub(
                        sq32[:], xn[:, c * T_CORE + t0:c * T_CORE + t0 + T],
                        xnp[:, c * T_CORE + t0:c * T_CORE + t0 + T]))
                    sq.step("vector", lambda e, c=c, t0=t0, mu=mu: e.scalar_tensor_tensor(
                        xm[:, c * T:(c + 1) * T], sq32[:], mu[:, c:c + 1],
                        xnp[:, c * T_CORE + t0:c * T_CORE + t0 + T],
                        ALU.mult, ALU.add))
                for d in range(NCH):
                    sq.step("tensor", lambda e, d=d: [
                        e.matmul(ps_mm[:], w_s[:, c * D_MODEL + d * 128:
                                               c * D_MODEL + (d + 1) * 128],
                                 xm[:, c * T:(c + 1) * T],
                                 start=(c == 0), stop=(c == NCH - 1))
                        for c in range(NCH)])
                    if kind == "k":
                        sq.step("scalar", lambda e: e.activation(
                            ek_t[:], ps_mm[:], AF.Exp))
                        sq.dma(eko[:, d, t0:t0 + T], ek_t[:])
                    elif kind == "v":
                        sq.dma(ek_t[:], eko[:, d, t0:t0 + T])
                        sq.step("vector", lambda e: e.tensor_copy(v_t[:], ps_mm[:]))
                        sq.step("vector", lambda e: e.tensor_mul(
                            oscr[:], ek_t[:], v_t[:]))
                        sq.dma(ekvo[:, d, t0:t0 + T], oscr[:])
                    else:
                        sq.step("vector", lambda e: e.tensor_copy(oscr[:], ps_mm[:]))
                        sq.dma(ro[:, d, t0:t0 + T], oscr[:])

    return seq_program(prog)


def build_wkv_out():
    NT, T = 4, 512

    def prog(nc, sq, es):
        x_fm = nc.dram_tensor("x_fm", [D_MODEL, T_CORE], F32, kind="ExternalInput").ap()
        ek_all = nc.dram_tensor("ek_all", [4, D_MODEL, T_CORE], F32, kind="ExternalInput").ap()
        ekv_all = nc.dram_tensor("ekv_all", [4, D_MODEL, T_CORE], F32, kind="ExternalInput").ap()
        r_fm = nc.dram_tensor("r_fm", [D_MODEL, T_CORE], F32, kind="ExternalInput").ap()
        woT = nc.dram_tensor("woT", [D_MODEL, D_MODEL], F32, kind="ExternalInput").ap()
        wv_ = nc.dram_tensor("wv_", [128, NCH], F32, kind="ExternalInput").ap()
        uv_ = nc.dram_tensor("uv_", [128, NCH], F32, kind="ExternalInput").ap()
        steps = nc.dram_tensor("steps", [128, 4], F32, kind="ExternalInput").ap()
        mask = nc.dram_tensor("mask", [128, 4], F32, kind="ExternalInput").ap()
        x_out = nc.dram_tensor("x_out", [D_MODEL, T_CORE], F32, kind="ExternalOutput").ap()

        xv_ = x_fm.rearrange("(c p) t -> p c t", p=128)
        ekv_ = ek_all.rearrange("b (c p) t -> b p c t", p=128)
        ekvv = ekv_all.rearrange("b (c p) t -> b p c t", p=128)
        rv_ = r_fm.rearrange("(c p) t -> p c t", p=128)
        wov = woT.rearrange("(c p) d -> p c d", p=128)
        xo = x_out.rearrange("(c p) t -> p c t", p=128)

        sb = lambda n, s, d=F32: es.enter_context(nc.sbuf_tensor(n, s, d))
        pt = lambda n, s, d=F32: es.enter_context(nc.psum_tensor(n, s, d))

        w_s = sb("w_s", [128, NCH * D_MODEL], BF16)
        wneg = sb("wneg", [128, NCH])
        iew = sb("iew", [128, NCH])
        eu = sb("eu", [128, NCH])
        dec = sb("dec", [128, NCH * 4])
        st_s = sb("st_s", [128, 4])
        mk_s = sb("mk_s", [128, 4])
        ek_own = sb("ek_own", [128, NCH * T])
        ekv_own = sb("ekv_own", [128, NCH * T])
        cek = sb("cek", [128, NCH * T])
        cekv = sb("cekv", [128, NCH * T])
        buf = sb("buf", [128, NCH * T])
        num = sb("num", [128, NCH * T])
        wkb = sb("wkb", [128, NCH * T], BF16)
        sq32 = sb("sq32", [128, 512])
        scr2 = sb("scr2", [128, 512])
        Xt = sb("Xt", [128, NCH * T])
        oscr = sb("oscr", [128, 512])
        ps_mm = pt("ps_mm", [128, 512])

        sq.dma(wneg[:], wv_)
        sq.dma(eu[:], uv_)
        sq.dma(st_s[:], steps)
        sq.dma(mk_s[:], mask)
        sq.dma(w_s[:].rearrange("p (c d) -> p c d", c=NCH),
               wov, engine="gpsimd")
        sq.step("scalar", lambda e: e.activation(wneg[:], wneg[:], AF.Exp))
        sq.step("vector", lambda e: e.tensor_single_scalar(
            wneg[:], wneg[:], -1.0, ALU.mult))
        sq.step("scalar", lambda e: e.activation(iew[:], wneg[:], AF.Exp, scale=-1.0))
        sq.step("scalar", lambda e: e.activation(eu[:], eu[:], AF.Exp))
        for c in range(NCH):
            for bp in range(4):
                sq.step("vector", lambda e, c=c, bp=bp: e.tensor_mul(
                    dec[:, c * 4 + bp:c * 4 + bp + 1], wneg[:, c:c + 1],
                    st_s[:, bp:bp + 1]))
        sq.step("scalar", lambda e: e.activation(dec[:], dec[:], AF.Exp))
        for c in range(NCH):
            sq.step("vector", lambda e, c=c: e.tensor_mul(
                dec[:, c * 4:(c + 1) * 4], dec[:, c * 4:(c + 1) * 4], mk_s[:]))

        for tt in range(NT):
            t0 = tt * T
            sq.step("gpsimd", lambda e: e.memset(cek[:], 0.0))
            sq.step("gpsimd", lambda e: e.memset(cekv[:], 0.0))
            for bp in range(4):
                sq.dma(buf[:].rearrange("p (c t) -> p c t", c=NCH),
                       ekv_[bp, :, :, t0:t0 + T])
                for c in range(NCH):
                    sq.step("vector", lambda e, c=c, bp=bp: e.scalar_tensor_tensor(
                        cek[:, c * T:(c + 1) * T], buf[:, c * T:(c + 1) * T],
                        dec[:, c * 4 + bp:c * 4 + bp + 1],
                        cek[:, c * T:(c + 1) * T], ALU.mult, ALU.add))
                sq.dma(buf[:].rearrange("p (c t) -> p c t", c=NCH),
                       ekvv[bp, :, :, t0:t0 + T])
                for c in range(NCH):
                    sq.step("vector", lambda e, c=c, bp=bp: e.scalar_tensor_tensor(
                        cekv[:, c * T:(c + 1) * T], buf[:, c * T:(c + 1) * T],
                        dec[:, c * 4 + bp:c * 4 + bp + 1],
                        cekv[:, c * T:(c + 1) * T], ALU.mult, ALU.add))
            sq.dma(ek_own[:].rearrange("p (c t) -> p c t", c=NCH),
                   ekv_[0, :, :, t0:t0 + T])
            sq.dma(ekv_own[:].rearrange("p (c t) -> p c t", c=NCH),
                   ekvv[0, :, :, t0:t0 + T])
            for c in range(NCH):
                sq.step("vector", lambda e, c=c: e.tensor_sub(
                    sq32[:], cek[:, c * T:(c + 1) * T], ek_own[:, c * T:(c + 1) * T]))
                sq.step("vector", lambda e, c=c: e.tensor_scalar(
                    sq32[:], sq32[:], iew[:, c:c + 1], None, ALU.mult))
                sq.step("vector", lambda e, c=c: e.scalar_tensor_tensor(
                    buf[:, c * T:(c + 1) * T], ek_own[:, c * T:(c + 1) * T],
                    eu[:, c:c + 1], sq32[:], ALU.mult, ALU.add))
                sq.step("vector", lambda e, c=c: e.tensor_sub(
                    sq32[:], cekv[:, c * T:(c + 1) * T], ekv_own[:, c * T:(c + 1) * T]))
                sq.step("vector", lambda e, c=c: e.tensor_scalar(
                    sq32[:], sq32[:], iew[:, c:c + 1], None, ALU.mult))
                sq.step("vector", lambda e, c=c: e.scalar_tensor_tensor(
                    num[:, c * T:(c + 1) * T], ekv_own[:, c * T:(c + 1) * T],
                    eu[:, c:c + 1], sq32[:], ALU.mult, ALU.add))
                sq.step("vector", lambda e, c=c: e.reciprocal(
                    scr2[:], buf[:, c * T:(c + 1) * T]))
                sq.step("vector", lambda e, c=c: e.tensor_mul(
                    sq32[:], num[:, c * T:(c + 1) * T], scr2[:]))
                sq.dma(Xt[:, 0:T], rv_[:, c, t0:t0 + T])
                sq.step("scalar", lambda e: e.activation(scr2[:], Xt[:, 0:T], AF.Sigmoid))
                sq.step("vector", lambda e, c=c: e.tensor_mul(
                    wkb[:, c * T:(c + 1) * T], sq32[:], scr2[:]))
            sq.dma(Xt[:].rearrange("p (c t) -> p c t", c=NCH), xv_[:, :, t0:t0 + T])
            for d in range(NCH):
                sq.step("tensor", lambda e, d=d: [
                    e.matmul(ps_mm[:], w_s[:, c * D_MODEL + d * 128:
                                           c * D_MODEL + (d + 1) * 128],
                             wkb[:, c * T:(c + 1) * T],
                             start=(c == 0), stop=(c == NCH - 1))
                    for c in range(NCH)])
                sq.step("vector", lambda e, d=d: e.tensor_add(
                    oscr[:], ps_mm[:], Xt[:, d * T:(d + 1) * T]))
                sq.dma(xo[:, d, t0:t0 + T], oscr[:])

    return seq_program(prog)


def host_wkv_kv_inputs(x, W, core):
    b, h = core // 2, core % 2
    if b == 0:
        xp = np.zeros((T_CORE, D_MODEL), np.float32)
    else:
        xp = np.asarray(x[b - 1, h * 2048:(h + 1) * 2048])
    return {
        "x_fm": fmT(core_slice_tokens(x, core)),
        "xp_fm": fmT(xp),
        "wkT": fmT(W["Wk"]), "wvT": fmT(W["Wv"]), "wrT": fmT(W["Wr"]),
        "mu_k": pack_vec(W["mu_k"]), "mu_v": pack_vec(W["mu_v"]),
        "mu_r": pack_vec(W["mu_r"]),
    }


def host_wkv_out_inputs(x, kv_outs, W, core):
    b, h = core // 2, core % 2
    order = [(b - j) % 4 for j in range(4)]
    ek = np.stack([kv_outs[2 * bp + h]["ek_o"] for bp in order])
    ekv = np.stack([kv_outs[2 * bp + h]["ekv_o"] for bp in order])
    steps = np.zeros((128, 4), np.float32)
    maskv = np.zeros((128, 4), np.float32)
    for j, bp in enumerate(order):
        steps[:, j] = b - bp
        maskv[:, j] = 1.0 if bp <= b else 0.0
    return {
        "x_fm": fmT(core_slice_tokens(x, core)),
        "ek_all": np.ascontiguousarray(ek),
        "ekv_all": np.ascontiguousarray(ekv),
        "r_fm": kv_outs[core]["r_o"],
        "woT": fmT(W["W_out"]),
        "wv_": pack_vec(W["w"]), "uv_": pack_vec(W["u"]),
        "steps": steps, "mask": maskv,
    }



ATT_SCALE = float(1.0 / np.sqrt(16 + 512 + 512))


def build_mla_qkv():
    DQ = 512
    NQ = 4

    def prog(nc, sq, es):
        x_fm = nc.dram_tensor("x_fm", [D_MODEL, T_CORE], F32, kind="ExternalInput").ap()
        wdq = nc.dram_tensor("wdq", [D_MODEL, DQ], F32, kind="ExternalInput").ap()
        wdkv = nc.dram_tensor("wdkv", [D_MODEL, DQ], F32, kind="ExternalInput").ap()
        wuq = nc.dram_tensor("wuq", [DQ, D_MODEL], F32, kind="ExternalInput").ap()
        wqr = nc.dram_tensor("wqr", [DQ, D_MODEL], F32, kind="ExternalInput").ap()
        wuk = nc.dram_tensor("wuk", [DQ, D_MODEL], F32, kind="ExternalInput").ap()
        wuv = nc.dram_tensor("wuv", [DQ, D_MODEL], F32, kind="ExternalInput").ap()
        wkr = nc.dram_tensor("wkr", [D_MODEL, D_MODEL], F32, kind="ExternalInput").ap()
        b_dq = nc.dram_tensor("b_dq", [128, NQ], F32, kind="ExternalInput").ap()
        b_dkv = nc.dram_tensor("b_dkv", [128, NQ], F32, kind="ExternalInput").ap()
        brows = {nm: nc.dram_tensor(f"brow_{nm}", [1, D_MODEL], F32,
                                    kind="ExternalInput").ap()
                 for nm in ("uq", "qr", "kr", "uk", "uv")}
        cos_t = nc.dram_tensor("cos_t", [T_CORE, DQ], F32, kind="ExternalInput").ap()
        sin_t = nc.dram_tensor("sin_t", [T_CORE, DQ], F32, kind="ExternalInput").ap()
        q_o = nc.dram_tensor("q_o", [D_MODEL, T_CORE], BF16, kind="ExternalOutput").ap()
        k_o = nc.dram_tensor("k_o", [D_MODEL, T_CORE], BF16, kind="ExternalOutput").ap()
        v_o = nc.dram_tensor("v_o", [T_CORE, D_MODEL], BF16, kind="ExternalOutput").ap()

        xv_ = x_fm.rearrange("(c p) t -> p c t", p=128)
        qo = q_o.rearrange("(c p) t -> p c t", p=128)
        ko = k_o.rearrange("(c p) t -> p c t", p=128)

        sb = lambda n, s, d=F32: es.enter_context(nc.sbuf_tensor(n, s, d))
        pt = lambda n, s, d=F32: es.enter_context(nc.psum_tensor(n, s, d))

        ones_col = sb("ones_col", [128, 1])
        ones_row = sb("ones_row", [1, 128])
        eps_row = sb("eps_row", [1, 1])
        identw = sb("identw", [128, 128])
        identb = sb("identb", [128, 128], BF16)
        Xt = sb("Xt", [128, NCH * 512])
        sq32 = sb("sq32", [128, 512])
        scr_a = sb("scr_a", [128, 512])
        scr_b = sb("scr_b", [128, 512])
        a_row = sb("a_row", [1, 512])
        b_row = sb("b_row", [1, 512])
        t_row = sb("t_row", [1, 512])
        xn = sb("xn", [128, NCH * T_CORE], BF16)
        ctq = sb("ctq", [128, NQ * T_CORE], BF16)
        ctkv = sb("ctkv", [128, NQ * T_CORE], BF16)
        w_dq = sb("w_dq", [128, NCH * DQ], BF16)
        w_dkv = sb("w_dkv", [128, NCH * DQ], BF16)
        w_uq = sb("w_uq", [128, NQ * D_MODEL], BF16)
        w_qr = sb("w_qr", [128, NQ * D_MODEL], BF16)
        w_uk = sb("w_uk", [128, NQ * D_MODEL], BF16)
        w_uv = sb("w_uv", [128, NQ * D_MODEL], BF16)
        w_kr = sb("w_kr", [128, NCH * D_MODEL], BF16)
        bdq_s = sb("bdq_s", [128, NQ])
        bdkv_s = sb("bdkv_s", [128, NQ])
        brep = {nm: sb(f"brep_{nm}", [128, D_MODEL]) for nm in
                ("uq", "qr", "kr", "uk", "uv")}
        brow_s = sb("brow_s", [1, D_MODEL])
        cs_t = sb("cs_t", [128, DQ])
        sn_t = sb("sn_t", [128, DQ])
        qtm = sb("qtm", [128, D_MODEL])
        rtm = sb("rtm", [128, D_MODEL])
        qtb = sb("qtb", [128, D_MODEL], BF16)
        tfm = sb("tfm", [128, 128], BF16)

        ps_a = pt("ps_a", [1, 512])
        ps_b = pt("ps_b", [1, 512])
        ps_bc1 = pt("ps_bc1", [128, 512])
        ps_bc2 = pt("ps_bc2", [128, 512])
        ps_m1 = pt("ps_m1", [128, 512])
        ps_m2 = pt("ps_m2", [128, 512])
        ps_tr = pt("ps_tr", [128, 128], BF16)

        sq.step("gpsimd", lambda e: e.memset(ones_col[:], 1.0))
        sq.step("gpsimd", lambda e: e.memset(ones_row[:], 1.0))
        sq.step("gpsimd", lambda e: e.memset(eps_row[:], LN_EPS))
        sq.step("gpsimd", lambda e: e.memset(identw[:], 1.0))
        sq.step("gpsimd", lambda e: e.affine_select(
            identw[:], identw[:], pattern=[[1, 128]], compare_op=ALU.is_equal,
            fill=0.0, base=0, channel_multiplier=-1))
        sq.step("vector", lambda e: e.tensor_copy(identb[:], identw[:]))
        sq.dma(bdq_s[:], b_dq)
        sq.dma(bdkv_s[:], b_dkv)
        sq.dma(w_dq[:].rearrange("p (c d) -> p c d", c=NCH),
               wdq.rearrange("(c p) d -> p c d", p=128), engine="gpsimd")
        sq.dma(w_dkv[:].rearrange("p (c d) -> p c d", c=NCH),
               wdkv.rearrange("(c p) d -> p c d", p=128), engine="gpsimd")
        sq.dma(w_uq[:].rearrange("p (c d) -> p c d", c=NQ),
               wuq.rearrange("(c p) d -> p c d", p=128), engine="gpsimd")
        sq.dma(w_qr[:].rearrange("p (c d) -> p c d", c=NQ),
               wqr.rearrange("(c p) d -> p c d", p=128), engine="gpsimd")
        sq.dma(w_uk[:].rearrange("p (c d) -> p c d", c=NQ),
               wuk.rearrange("(c p) d -> p c d", p=128), engine="gpsimd")
        sq.dma(w_uv[:].rearrange("p (c d) -> p c d", c=NQ),
               wuv.rearrange("(c p) d -> p c d", p=128), engine="gpsimd")
        sq.dma(w_kr[:].rearrange("p (c d) -> p c d", c=NCH),
               wkr.rearrange("(c p) d -> p c d", p=128), engine="gpsimd")
        for nm in brep:
            sq.dma(brow_s[:], brows[nm])
            for n in range(2):
                sq.step("tensor", lambda e, n=n, nm=nm: e.matmul(
                    ps_bc1[:], ones_row[:], brow_s[:, n * 512:(n + 1) * 512]))
                sq.step("vector", lambda e, n=n, nm=nm: e.tensor_copy(
                    brep[nm][:, n * 512:(n + 1) * 512], ps_bc1[:]))

        for it in range(4):
            c0, w = it * 512, 512
            sq.dma(Xt[:].rearrange("p (c t) -> p c t", c=NCH), xv_[:, :, c0:c0 + w])
            sq.step("tensor", lambda e, w=w: [
                e.matmul(ps_a[:, 0:w], ones_col[:], Xt[:, c * 512:c * 512 + w],
                         start=(c == 0), stop=(c == NCH - 1)) for c in range(NCH)])
            for c in range(NCH):
                sq.step("scalar", lambda e, c=c, w=w: e.activation(
                    sq32[:, 0:w], Xt[:, c * 512:c * 512 + w], AF.Square))
                sq.step("tensor", lambda e, c=c, w=w: e.matmul(
                    ps_b[:, 0:w], ones_col[:], sq32[:, 0:w],
                    start=(c == 0), stop=(c == NCH - 1)))
            sq.step("vector", lambda e, w=w: e.tensor_single_scalar(
                b_row[:, 0:w], ps_a[:, 0:w], 1.0 / D_MODEL, ALU.mult))
            sq.step("vector", lambda e, w=w: e.tensor_single_scalar(
                a_row[:, 0:w], ps_b[:, 0:w], 1.0 / D_MODEL, ALU.mult))
            sq.step("scalar", lambda e, w=w: e.activation(
                t_row[:, 0:w], b_row[:, 0:w], AF.Square))
            sq.step("vector", lambda e, w=w: e.tensor_sub(
                a_row[:, 0:w], a_row[:, 0:w], t_row[:, 0:w]))
            sq.step("scalar", lambda e, w=w: e.activation(
                a_row[:, 0:w], a_row[:, 0:w], AF.Ln, bias=eps_row[:]))
            sq.step("scalar", lambda e, w=w: e.activation(
                a_row[:, 0:w], a_row[:, 0:w], AF.Exp, scale=-0.5))
            sq.step("tensor", lambda e, w=w: e.matmul(
                ps_bc1[:, 0:w], ones_row[:], b_row[:, 0:w]))
            sq.step("tensor", lambda e, w=w: e.matmul(
                ps_bc2[:, 0:w], ones_row[:], a_row[:, 0:w]))
            for c in range(NCH):
                sq.step("vector", lambda e, c=c, w=w: e.tensor_sub(
                    sq32[:, 0:w], Xt[:, c * 512:c * 512 + w], ps_bc1[:, 0:w]))
                sq.step("vector", lambda e, c=c, c0=c0, w=w: e.tensor_mul(
                    xn[:, c * T_CORE + c0:c * T_CORE + c0 + w], sq32[:, 0:w],
                    ps_bc2[:, 0:w]))

        for nm, wsl, bias, dst in (("q", w_dq, bdq_s, ctq), ("kv", w_dkv, bdkv_s, ctkv)):
            for dqt in range(NQ):
                for tt in range(4):
                    t0 = tt * 512
                    sq.step("tensor", lambda e, wsl=wsl, dqt=dqt, t0=t0: [
                        e.matmul(ps_m1[:], wsl[:, c * DQ + dqt * 128:
                                               c * DQ + (dqt + 1) * 128],
                                 xn[:, c * T_CORE + t0:c * T_CORE + t0 + 512],
                                 start=(c == 0), stop=(c == NCH - 1))
                        for c in range(NCH)])
                    sq.step("vector", lambda e, dst=dst, bias=bias, dqt=dqt, t0=t0:
                        e.tensor_scalar(
                            dst[:, dqt * T_CORE + t0:dqt * T_CORE + t0 + 512],
                            ps_m1[:], bias[:, dqt:dqt + 1], None, ALU.add))

        for qt in range(T_CORE // 128):
            t0 = qt * 128
            sq.dma(cs_t[:], cos_t[t0:t0 + 128, :])
            sq.dma(sn_t[:], sin_t[t0:t0 + 128, :])
            for which in ("q", "k"):
                ct = ctq if which == "q" else ctkv
                wc = w_uq if which == "q" else w_uk
                bc = brep["uq"] if which == "q" else brep["uk"]
                for n in range(2):
                    sq.step("tensor", lambda e, ct=ct, wc=wc, n=n, t0=t0: [
                        e.matmul(ps_m1[:], ct[:, c * T_CORE + t0:c * T_CORE + t0 + 128],
                                 wc[:, c * D_MODEL + n * 512:c * D_MODEL + n * 512 + 512],
                                 start=(c == 0), stop=(c == NQ - 1))
                        for c in range(NQ)])
                    sq.step("vector", lambda e, bc=bc, n=n: e.tensor_add(
                        qtm[:, n * 512:(n + 1) * 512], ps_m1[:],
                        bc[:, n * 512:(n + 1) * 512]))
                if which == "q":
                    for n in range(2):
                        sq.step("tensor", lambda e, n=n, t0=t0: [
                            e.matmul(ps_m2[:],
                                     ctq[:, c * T_CORE + t0:c * T_CORE + t0 + 128],
                                     w_qr[:, c * D_MODEL + n * 512:
                                          c * D_MODEL + n * 512 + 512],
                                     start=(c == 0), stop=(c == NQ - 1))
                            for c in range(NQ)])
                        sq.step("vector", lambda e, n=n: e.tensor_add(
                            rtm[:, n * 512:(n + 1) * 512], ps_m2[:],
                            brep["qr"][:, n * 512:(n + 1) * 512]))
                else:
                    for n in range(2):
                        sq.step("tensor", lambda e, n=n, t0=t0: [
                            e.matmul(ps_m2[:],
                                     xn[:, c * T_CORE + t0:c * T_CORE + t0 + 128],
                                     w_kr[:, c * D_MODEL + n * 512:
                                          c * D_MODEL + n * 512 + 512],
                                     start=(c == 0), stop=(c == NCH - 1))
                            for c in range(NCH)])
                        sq.step("vector", lambda e, n=n: e.tensor_add(
                            rtm[:, n * 512:(n + 1) * 512], ps_m2[:],
                            brep["kr"][:, n * 512:(n + 1) * 512]))
                re = rtm[:].rearrange("p (i two) -> p two i", two=2)[:, 0, :]
                ro = rtm[:].rearrange("p (i two) -> p two i", two=2)[:, 1, :]
                qe = qtm[:].rearrange("p (i two) -> p two i", two=2)[:, 0, :]
                qo_ = qtm[:].rearrange("p (i two) -> p two i", two=2)[:, 1, :]
                sq.step("vector", lambda e, re=re: e.tensor_mul(scr_a[:], re, cs_t[:]))
                sq.step("vector", lambda e, ro=ro: e.tensor_mul(scr_b[:], ro, sn_t[:]))
                sq.step("vector", lambda e: e.tensor_sub(scr_a[:], scr_a[:], scr_b[:]))
                sq.step("vector", lambda e, qe=qe: e.tensor_add(qe, qe, scr_a[:]))
                sq.step("vector", lambda e, ro=ro: e.tensor_mul(scr_a[:], ro, cs_t[:]))
                sq.step("vector", lambda e, re=re: e.tensor_mul(scr_b[:], re, sn_t[:]))
                sq.step("vector", lambda e: e.tensor_add(scr_a[:], scr_a[:], scr_b[:]))
                sq.step("vector", lambda e, qo_=qo_: e.tensor_add(qo_, qo_, scr_a[:]))
                sq.step("vector", lambda e: e.tensor_copy(qtb[:], qtm[:]))
                dst = qo if which == "q" else ko
                for c in range(NCH):
                    sq.step("tensor", lambda e, c=c: e.transpose(
                        ps_tr[:], qtb[:, c * 128:(c + 1) * 128], identb[:]))
                    sq.step("vector", lambda e: e.tensor_copy(tfm[:], ps_tr[:]))
                    sq.dma(dst[:, c, t0:t0 + 128], tfm[:])
            for n in range(2):
                sq.step("tensor", lambda e, n=n, t0=t0: [
                    e.matmul(ps_m1[:], ctkv[:, c * T_CORE + t0:c * T_CORE + t0 + 128],
                             w_uv[:, c * D_MODEL + n * 512:c * D_MODEL + n * 512 + 512],
                             start=(c == 0), stop=(c == NQ - 1))
                    for c in range(NQ)])
                sq.step("vector", lambda e, n=n: e.tensor_add(
                    qtm[:, n * 512:(n + 1) * 512], ps_m1[:],
                    brep["uv"][:, n * 512:(n + 1) * 512]))
            sq.step("vector", lambda e: e.tensor_copy(qtb[:], qtm[:]))
            sq.dma(v_o[t0:t0 + 128, :], qtb[:])

    return seq_program(prog)


def build_mla_attn():
    S = 4096
    NKT = S // 128
    NH = 16

    def prog(nc, sq, es):
        q_i = nc.dram_tensor("q_i", [D_MODEL, T_CORE], BF16, kind="ExternalInput").ap()
        k_i = nc.dram_tensor("k_i", [D_MODEL, S], BF16, kind="ExternalInput").ap()
        v_i = nc.dram_tensor("v_i", [S, D_MODEL], BF16, kind="ExternalInput").ap()
        x_fm = nc.dram_tensor("x_fm", [D_MODEL, T_CORE], F32, kind="ExternalInput").ap()
        wo = nc.dram_tensor("wo", [D_MODEL, D_MODEL], F32, kind="ExternalInput").ap()
        b_o = nc.dram_tensor("b_o", [128, NCH], F32, kind="ExternalInput").ap()
        x_out = nc.dram_tensor("x_out", [D_MODEL, T_CORE], F32, kind="ExternalOutput").ap()

        qv = q_i.rearrange("(c p) t -> p c t", p=128)
        kv_ = k_i.rearrange("(c p) t -> p c t", p=128)
        vv = v_i.rearrange("(kt p) d -> p kt d", p=128)
        xv_ = x_fm.rearrange("(c p) t -> p c t", p=128)
        xo = x_out.rearrange("(c p) t -> p c t", p=128)

        sb = lambda n, s, d=F32: es.enter_context(nc.sbuf_tensor(n, s, d))
        pt = lambda n, s, d=F32: es.enter_context(nc.psum_tensor(n, s, d))

        ones_col = sb("ones_col", [128, 1])
        onesb_col = sb("onesb_col", [128, 1], BF16)
        ones_row = sb("ones_row", [1, 128])
        identw = sb("identw", [128, 128])
        identb = sb("identb", [128, 128], BF16)
        q_sb = sb("q_sb", [128, NCH * T_CORE], BF16)
        k_sb = sb("k_sb", [128, NCH * S], BF16)
        v_sb = sb("v_sb", [128, NKT * D_MODEL], BF16)
        wo_sb = sb("wo_sb", [128, NCH * D_MODEL], BF16)
        bo_sb = sb("bo_sb", [128, NCH])
        Ktil = sb("Ktil", [128, S], BF16)
        Qtil = sb("Qtil", [128, 128], BF16)
        mm8 = sb("mm8", [128, 8])
        m_c = sb("m_c", [128, 1])
        P_T = sb("P_T", [128, 128], BF16)
        rs_row = sb("rs_row", [1, 128])
        rc_col = sb("rc_col", [128, 1])
        out_tm = sb("out_tm", [128, NH * 64])
        otb = sb("otb", [128, NH * 64], BF16)
        ofm = sb("ofm", [128, NCH * 128], BF16)
        Xt = sb("Xt", [128, NCH * 128])
        oscr = sb("oscr", [128, 128])
        tfm = sb("tfm", [128, 128], BF16)

        ps_sc = pt("ps_sc", [128, 512])
        ps_st = pt("ps_st", [128, 128])
        ps_o = pt("ps_o", [128, 64])
        ps_rs = pt("ps_rs", [1, 128])
        ps_tr = pt("ps_tr", [128, 128], BF16)
        ps_r1 = pt("ps_r1", [1, 128])
        ps_out = pt("ps_out", [128, 128])

        sq.step("gpsimd", lambda e: e.memset(ones_col[:], 1.0))
        sq.step("gpsimd", lambda e: e.memset(onesb_col[:], 1.0))
        sq.step("gpsimd", lambda e: e.memset(ones_row[:], 1.0))
        sq.step("gpsimd", lambda e: e.memset(identw[:], 1.0))
        sq.step("gpsimd", lambda e: e.affine_select(
            identw[:], identw[:], pattern=[[1, 128]], compare_op=ALU.is_equal,
            fill=0.0, base=0, channel_multiplier=-1))
        sq.step("vector", lambda e: e.tensor_copy(identb[:], identw[:]))
        sq.dma(q_sb[:].rearrange("p (c t) -> p c t", c=NCH), qv)
        sq.dma(k_sb[:].rearrange("p (c t) -> p c t", c=NCH), kv_)
        sq.dma(v_sb[:].rearrange("p (kt d) -> p kt d", kt=NKT), vv)
        sq.dma(wo_sb[:].rearrange("p (c d) -> p c d", c=NCH),
               wo.rearrange("(c p) d -> p c d", p=128), engine="gpsimd")
        sq.dma(bo_sb[:], b_o)

        for qt in range(T_CORE // 128):
            t0 = qt * 128
            for h in range(NH):
                hb = 64 * (h % 2)
                hc = h // 2
                sq.dma(Ktil[0:64, :], k_sb[hb:hb + 64, hc * S:(hc + 1) * S],
                       engine="gpsimd")
                sq.step("gpsimd", lambda e: e.memset(Ktil[64:65, 0:S], 1.0))
                for kt8 in range(8):
                    sq.step("tensor", lambda e, h=h, hb=hb, hc=hc, t0=t0, kt8=kt8:
                        e.matmul(ps_sc[:],
                                 q_sb[hb:hb + 64, hc * T_CORE + t0:hc * T_CORE + t0 + 128],
                                 k_sb[hb:hb + 64, hc * S + kt8 * 512:hc * S + (kt8 + 1) * 512],
                                 start=True, stop=True))
                    sq.step("vector", lambda e, kt8=kt8: e.reduce_max(
                        mm8[:, kt8:kt8 + 1], ps_sc[:], axis=AX.X))
                sq.step("vector", lambda e: e.reduce_max(m_c[:], mm8[:], axis=AX.X))
                sq.step("vector", lambda e: e.tensor_single_scalar(
                    m_c[:], m_c[:], -1.0, ALU.mult))
                sq.dma(Qtil[0:64, :],
                       q_sb[hb:hb + 64, hc * T_CORE + t0:hc * T_CORE + t0 + 128],
                       engine="gpsimd")
                sq.step("tensor", lambda e: e.transpose(
                    ps_r1[:], m_c[:], identw[:]))
                sq.step("vector", lambda e: e.tensor_copy(
                    Qtil[64:65, :], ps_r1[:]))
                for kt in range(NKT):
                    sq.step("tensor", lambda e, kt=kt: e.matmul(
                        ps_st[:], Ktil[0:65, kt * 128:(kt + 1) * 128],
                        Qtil[0:65, :], start=True, stop=True))
                    sq.step("scalar", lambda e: e.activation(
                        P_T[:], ps_st[:], AF.Exp, scale=ATT_SCALE))
                    sq.step("tensor", lambda e, kt=kt: e.matmul(
                        ps_rs[:], onesb_col[:], P_T[:],
                        start=(kt == 0), stop=(kt == NKT - 1)))
                    sq.step("tensor", lambda e, h=h, kt=kt: e.matmul(
                        ps_o[:], P_T[:], v_sb[:, kt * D_MODEL + h * 64:
                                              kt * D_MODEL + (h + 1) * 64],
                        start=(kt == 0), stop=(kt == NKT - 1)))
                sq.step("vector", lambda e: e.reciprocal(rs_row[:], ps_rs[:]))
                sq.step("tensor", lambda e: e.transpose(
                    ps_st[0:128, 0:1], rs_row[:], identw[0:1, 0:1]))
                sq.step("vector", lambda e: e.tensor_copy(
                    rc_col[:], ps_st[0:128, 0:1]))
                sq.step("vector", lambda e, h=h: e.tensor_scalar(
                    out_tm[:, h * 64:(h + 1) * 64], ps_o[:], rc_col[:],
                    None, ALU.mult))
                if h == NH - 1:
                    sq.step("vector", lambda e: e.tensor_copy(otb[:], out_tm[:]))
                    for c in range(NCH):
                        sq.step("tensor", lambda e, c=c: e.transpose(
                            ps_tr[:], otb[:, c * 128:(c + 1) * 128], identb[:]))
                        sq.step("vector", lambda e, c=c: e.tensor_copy(
                            ofm[:, c * 128:(c + 1) * 128], ps_tr[:]))
                    sq.dma(Xt[:].rearrange("p (c t) -> p c t", c=NCH),
                           xv_[:, :, t0:t0 + 128])
                    for d in range(NCH):
                        sq.step("tensor", lambda e, d=d: [
                            e.matmul(ps_out[:], wo_sb[:, c * D_MODEL + d * 128:
                                                      c * D_MODEL + (d + 1) * 128],
                                     ofm[:, c * 128:(c + 1) * 128],
                                     start=(c == 0), stop=(c == NCH - 1))
                            for c in range(NCH)])
                        sq.step("vector", lambda e, d=d: e.tensor_scalar(
                            oscr[:], ps_out[:], bo_sb[:, d:d + 1], None, ALU.add))
                        sq.step("vector", lambda e, d=d: e.tensor_add(
                            oscr[:], oscr[:], Xt[:, d * 128:(d + 1) * 128]))
                        sq.dma(xo[:, d, t0:t0 + 128], oscr[:])

    return seq_program(prog)


def host_mla_qkv_inputs(x, W, core):
    ids = np.arange(512, dtype=np.float64)
    theta = (10000.0 ** (-2.0 * ids)).astype(np.float32) / 512.0
    b, h = core // 2, core % 2
    pos = np.arange(h * 2048, (h + 1) * 2048, dtype=np.float32)
    ang = pos[:, None] * theta[None, :]
    return {
        "x_fm": fmT(core_slice_tokens(x, core)),
        "wdq": np.ascontiguousarray(W["Wd_q_w"]),
        "wdkv": np.ascontiguousarray(W["Wd_kv_w"]),
        "wuq": np.ascontiguousarray(W["Wu_q_w"]),
        "wqr": np.ascontiguousarray(W["Wq_r_w"]),
        "wuk": np.ascontiguousarray(W["Wu_k_w"]),
        "wuv": np.ascontiguousarray(W["Wu_v_w"]),
        "wkr": np.ascontiguousarray(W["Wk_r_w"]),
        "b_dq": pack_vec(W["Wd_q_b"]),
        "b_dkv": pack_vec(W["Wd_kv_b"]),
        "brow_uq": np.asarray(W["Wu_q_b"], np.float32)[None, :],
        "brow_qr": np.asarray(W["Wq_r_b"], np.float32)[None, :],
        "brow_kr": np.asarray(W["Wk_r_b"], np.float32)[None, :],
        "brow_uk": np.asarray(W["Wu_k_b"], np.float32)[None, :],
        "brow_uv": np.asarray(W["Wu_v_b"], np.float32)[None, :],
        "cos_t": np.cos(ang).astype(np.float32),
        "sin_t": np.sin(ang).astype(np.float32),
    }


def host_mla_attn_inputs(x, qkv_outs, W, core):
    b, h = core // 2, core % 2
    k_full = np.concatenate([qkv_outs[2 * b]["k_o"], qkv_outs[2 * b + 1]["k_o"]],
                            axis=1)
    v_full = np.concatenate([qkv_outs[2 * b]["v_o"], qkv_outs[2 * b + 1]["v_o"]],
                            axis=0)
    return {
        "q_i": qkv_outs[core]["q_o"],
        "k_i": np.ascontiguousarray(k_full),
        "v_i": np.ascontiguousarray(v_full),
        "x_fm": fmT(core_slice_tokens(x, core)),
        "wo": np.ascontiguousarray(W["Wo_w"]),
        "b_o": pack_vec(W["Wo_b"]),
    }



D_CKV = 256
D_CQ = 512
TOPK_NH = 8
D_HEAD = 64
D_ROPE = 32
K_TS = 96
LW = 64
ATT2_SCALE = float((D_HEAD + D_ROPE) ** -0.5)
PADK = 63


def build_topk_proj():
    def prog(nc, sq, es):
        x_fm = nc.dram_tensor("x_fm", [D_MODEL, T_CORE], F32, kind="ExternalInput").ap()
        wdkv = nc.dram_tensor("wdkv", [D_MODEL, D_CKV], F32, kind="ExternalInput").ap()
        wuk = nc.dram_tensor("wuk", [D_CKV, 512], F32, kind="ExternalInput").ap()
        wuv = nc.dram_tensor("wuv", [D_CKV, 512], F32, kind="ExternalInput").ap()
        wdq = nc.dram_tensor("wdq", [D_MODEL, D_CQ], F32, kind="ExternalInput").ap()
        wuq = nc.dram_tensor("wuq", [D_CQ, 512], F32, kind="ExternalInput").ap()
        wqr = nc.dram_tensor("wqr", [D_CQ, 256], F32, kind="ExternalInput").ap()
        wkr = nc.dram_tensor("wkr", [D_MODEL, 32], F32, kind="ExternalInput").ap()
        wiq = nc.dram_tensor("wiq", [D_MODEL, 256], F32, kind="ExternalInput").ap()
        wik = nc.dram_tensor("wik", [D_MODEL, 64], F32, kind="ExternalInput").ap()
        q_c_tm = nc.dram_tensor("q_c_tm", [T_CORE, 512], BF16, kind="ExternalOutput").ap()
        qr_tm = nc.dram_tensor("qr_tm", [T_CORE, 256], F32, kind="ExternalOutput").ap()
        q_i_tm = nc.dram_tensor("q_i_tm", [T_CORE, 256], BF16, kind="ExternalOutput").ap()
        k_i_tm = nc.dram_tensor("k_i_tm", [T_CORE, 64], BF16, kind="ExternalOutput").ap()
        kv_tm = nc.dram_tensor("kv_tm", [T_CORE, 512], BF16, kind="ExternalOutput").ap()
        v_tm = nc.dram_tensor("v_tm", [T_CORE, 512], BF16, kind="ExternalOutput").ap()
        mu_tm = nc.dram_tensor("mu_tm", [T_CORE, 32], F32, kind="ExternalOutput").ap()

        xv_ = x_fm.rearrange("(c p) t -> p c t", p=128)

        sb = lambda n, s, d=F32: es.enter_context(nc.sbuf_tensor(n, s, d))
        pt = lambda n, s, d=F32: es.enter_context(nc.psum_tensor(n, s, d))

        ones_col = sb("ones_col", [128, 1])
        ones_row = sb("ones_row", [1, 128])
        eps_row = sb("eps_row", [1, 1])
        Xt = sb("Xt", [128, NCH * 512])
        sq32 = sb("sq32", [128, 512])
        a_row = sb("a_row", [1, 512])
        b_row = sb("b_row", [1, 512])
        t_row = sb("t_row", [1, 512])
        xn = sb("xn", [128, NCH * T_CORE], BF16)
        ctq = sb("ctq", [128, 4 * T_CORE], BF16)
        ctkv = sb("ctkv", [128, 2 * T_CORE], BF16)
        w_a = sb("w_a", [128, NCH * 512], BF16)
        w_b = sb("w_b", [128, NCH * 512], BF16)
        otm = sb("otm", [128, 512])
        otb = sb("otb", [128, 512], BF16)

        ps_a = pt("ps_a", [1, 512])
        ps_b = pt("ps_b", [1, 512])
        ps_bc1 = pt("ps_bc1", [128, 512])
        ps_bc2 = pt("ps_bc2", [128, 512])
        ps_m1 = pt("ps_m1", [128, 512])

        sq.step("gpsimd", lambda e: e.memset(ones_col[:], 1.0))
        sq.step("gpsimd", lambda e: e.memset(ones_row[:], 1.0))
        sq.step("gpsimd", lambda e: e.memset(eps_row[:], LN_EPS))

        for it in range(4):
            c0, w = it * 512, 512
            sq.dma(Xt[:].rearrange("p (c t) -> p c t", c=NCH), xv_[:, :, c0:c0 + w])
            sq.step("tensor", lambda e, w=w: [
                e.matmul(ps_a[:, 0:w], ones_col[:], Xt[:, c * 512:c * 512 + w],
                         start=(c == 0), stop=(c == NCH - 1)) for c in range(NCH)])
            for c in range(NCH):
                sq.step("scalar", lambda e, c=c, w=w: e.activation(
                    sq32[:, 0:w], Xt[:, c * 512:c * 512 + w], AF.Square))
                sq.step("tensor", lambda e, c=c, w=w: e.matmul(
                    ps_b[:, 0:w], ones_col[:], sq32[:, 0:w],
                    start=(c == 0), stop=(c == NCH - 1)))
            sq.step("vector", lambda e, w=w: e.tensor_single_scalar(
                b_row[:, 0:w], ps_a[:, 0:w], 1.0 / D_MODEL, ALU.mult))
            sq.step("vector", lambda e, w=w: e.tensor_single_scalar(
                a_row[:, 0:w], ps_b[:, 0:w], 1.0 / D_MODEL, ALU.mult))
            sq.step("scalar", lambda e, w=w: e.activation(
                t_row[:, 0:w], b_row[:, 0:w], AF.Square))
            sq.step("vector", lambda e, w=w: e.tensor_sub(
                a_row[:, 0:w], a_row[:, 0:w], t_row[:, 0:w]))
            sq.step("scalar", lambda e, w=w: e.activation(
                a_row[:, 0:w], a_row[:, 0:w], AF.Ln, bias=eps_row[:]))
            sq.step("scalar", lambda e, w=w: e.activation(
                a_row[:, 0:w], a_row[:, 0:w], AF.Exp, scale=-0.5))
            sq.step("tensor", lambda e, w=w: e.matmul(
                ps_bc1[:, 0:w], ones_row[:], b_row[:, 0:w]))
            sq.step("tensor", lambda e, w=w: e.matmul(
                ps_bc2[:, 0:w], ones_row[:], a_row[:, 0:w]))
            for c in range(NCH):
                sq.step("vector", lambda e, c=c, w=w: e.tensor_sub(
                    sq32[:, 0:w], Xt[:, c * 512:c * 512 + w], ps_bc1[:, 0:w]))
                sq.step("vector", lambda e, c=c, c0=c0, w=w: e.tensor_mul(
                    xn[:, c * T_CORE + c0:c * T_CORE + c0 + w], sq32[:, 0:w],
                    ps_bc2[:, 0:w]))

        for nm, wsrc, nd, dst in (("q", wdq, 4, ctq), ("kv", wdkv, 2, ctkv)):
            DOUT = nd * 128
            sq.dma(w_a[:, 0:NCH * DOUT].rearrange("p (c d) -> p c d", c=NCH),
                   wsrc.rearrange("(c p) d -> p c d", p=128), engine="gpsimd")
            for dt in range(nd):
                for tt in range(4):
                    t0 = tt * 512
                    sq.step("tensor", lambda e, dt=dt, t0=t0, DOUT=DOUT: [
                        e.matmul(ps_m1[:], w_a[:, c * DOUT + dt * 128:
                                               c * DOUT + (dt + 1) * 128],
                                 xn[:, c * T_CORE + t0:c * T_CORE + t0 + 512],
                                 start=(c == 0), stop=(c == NCH - 1))
                        for c in range(NCH)])
                    sq.step("vector", lambda e, dst=dst, dt=dt, t0=t0: e.tensor_copy(
                        dst[:, dt * T_CORE + t0:dt * T_CORE + t0 + 512], ps_m1[:]))

        def tm_proj(src, src_nc, wsrc, wid, out_t, post, wslot):
            sq.dma(wslot[:, 0:src_nc * wid].rearrange("p (c d) -> p c d", c=src_nc),
                   wsrc.rearrange("(c p) d -> p c d", p=128), engine="gpsimd")
            for qt in range(T_CORE // 128):
                t0 = qt * 128
                nn = (wid + 511) // 512
                for n in range(nn):
                    w = min(512, wid - n * 512)
                    sq.step("tensor", lambda e, src=src, wslot=wslot, src_nc=src_nc,
                            wid=wid, t0=t0, n=n, w=w: [
                        e.matmul(ps_m1[:, 0:w],
                                 src[:, c * T_CORE + t0:c * T_CORE + t0 + 128],
                                 wslot[:, c * wid + n * 512:c * wid + n * 512 + w],
                                 start=(c == 0), stop=(c == src_nc - 1))
                        for c in range(src_nc)])
                    post(t0, n, w)

        def mk_post_bf16(out_t):
            def post(t0, n, w):
                sq.step("vector", lambda e, t0=t0, n=n, w=w: e.tensor_copy(
                    otb[:, 0:w], ps_m1[:, 0:w]))
                sq.dma(out_t[t0:t0 + 128, n * 512:n * 512 + w], otb[:, 0:w])
            return post

        tm_proj(ctq, 4, wuq, 512, q_c_tm, mk_post_bf16(q_c_tm), w_b)

        def post_qr(t0, n, w):
            sq.step("vector", lambda e, w=w: e.tensor_copy(otm[:, 0:w], ps_m1[:, 0:w]))
            sq.dma(qr_tm[t0:t0 + 128, 0:w], otm[:, 0:w])
        tm_proj(ctq, 4, wqr, 256, qr_tm, post_qr, w_b)

        tm_proj(xn, NCH, wiq, 256, q_i_tm,
                lambda t0, n, w: (sq.step("vector", lambda e, w=w: e.tensor_copy(
                    otb[:, 0:w], ps_m1[:, 0:w])),
                    sq.dma(q_i_tm[t0:t0 + 128, 0:w], otb[:, 0:w])), w_b)
        tm_proj(xn, NCH, wik, 64, k_i_tm,
                lambda t0, n, w: (sq.step("vector", lambda e, w=w: e.tensor_copy(
                    otb[:, 0:w], ps_m1[:, 0:w])),
                    sq.dma(k_i_tm[t0:t0 + 128, 0:w], otb[:, 0:w])), w_b)
        tm_proj(ctkv, 2, wuk, 512, kv_tm, mk_post_bf16(kv_tm), w_b)
        tm_proj(ctkv, 2, wuv, 512, v_tm, mk_post_bf16(v_tm), w_b)

        def post_mu(t0, n, w):
            sq.step("scalar", lambda e, w=w: e.activation(
                otm[:, 0:w], ps_m1[:, 0:w], AF.Abs))
            sq.step("scalar", lambda e, w=w: e.activation(
                otm[:, 0:w], otm[:, 0:w], AF.Exp, scale=-1.0))
            sq.step("scalar", lambda e, w=w: e.activation(
                otm[:, 0:w], otm[:, 0:w], AF.Ln, bias=1.0))
            sq.step("vector", lambda e, w=w: e.tensor_single_scalar(
                sq32[:, 0:w], ps_m1[:, 0:w], 0.0, ALU.max))
            sq.step("vector", lambda e, w=w: e.tensor_add(
                otm[:, 0:w], otm[:, 0:w], sq32[:, 0:w]))
            sq.dma(mu_tm[t0:t0 + 128, 0:w], otm[:, 0:w])
        tm_proj(xn, NCH, wkr, 32, mu_tm, post_mu, w_b)

    return seq_program(prog)


def host_topk_proj_inputs(x, W, core):
    return {
        "x_fm": fmT(core_slice_tokens(x, core)),
        "wdkv": np.ascontiguousarray(W["W_dkv"]),
        "wuk": np.ascontiguousarray(W["W_uk"]),
        "wuv": np.ascontiguousarray(W["W_uv"]),
        "wdq": np.ascontiguousarray(W["W_dq"]),
        "wuq": np.ascontiguousarray(W["W_uq"]),
        "wqr": np.ascontiguousarray(W["W_qr"]),
        "wkr": np.ascontiguousarray(W["W_kr"]),
        "wiq": np.ascontiguousarray(W["idx_wq"]),
        "wik": np.ascontiguousarray(W["idx_wk"]),
    }


def build_topk_attn(debug=False):
    S = 4096
    SPB = 2112
    NQT = T_CORE // 128

    def prog(nc, sq, es):
        dt_ = nc.dram_tensor
        x_fm = dt_("x_fm", [D_MODEL, T_CORE], F32, kind="ExternalInput").ap()
        q_c_tm = dt_("q_c_tm", [T_CORE, 512], BF16, kind="ExternalInput").ap()
        q_c_fmi = dt_("q_c_fmi", [512, T_CORE], BF16, kind="ExternalInput").ap()
        qr_tmi = dt_("qr_tmi", [T_CORE, 256], F32, kind="ExternalInput").ap()
        q_i_fmi = dt_("q_i_fmi", [256, T_CORE], BF16, kind="ExternalInput").ap()
        k_i_fmi = dt_("k_i_fmi", [128, S], BF16, kind="ExternalInput").ap()
        kv_fm_pad = dt_("kv_fm_pad", [512, SPB], BF16, kind="ExternalInput").ap()
        kv_tm_full = dt_("kv_tm_full", [S, 512], BF16, kind="ExternalInput").ap()
        v_tm_full = dt_("v_tm_full", [S, 512], BF16, kind="ExternalInput").ap()
        v_pad = dt_("v_pad", [SPB, 512], BF16, kind="ExternalInput").ap()
        mu1_pad = dt_("mu1_pad", [16, SPB], BF16, kind="ExternalInput").ap()
        mu2_pad = dt_("mu2_pad", [16, SPB], BF16, kind="ExternalInput").ap()
        mu_tm_full = dt_("mu_tm_full", [S, 32], F32, kind="ExternalInput").ap()
        w_out = dt_("w_out", [512, D_MODEL], F32, kind="ExternalInput").ap()
        idxw_row = dt_("idxw_row", [1, 4], F32, kind="ExternalInput").ap()
        rdelta = dt_("rdelta", [1, 16], F32, kind="ExternalInput").ap()
        thr_i = dt_("thr_i", [128, NQT], F32, kind="ExternalInput").ap()
        ramp0 = dt_("ramp0", [128, 96], F32, kind="ExternalInput").ap()
        mask0 = dt_("mask0", [128, 512], U8, kind="ExternalInput").ap()
        cth_q = dt_("cth_q", [T_CORE, 16], F32, kind="ExternalInput").ap()
        sth_q = dt_("sth_q", [T_CORE, 16], F32, kind="ExternalInput").ap()
        cth_q2 = dt_("cth_q2", [T_CORE, 16], F32, kind="ExternalInput").ap()
        sth_q2 = dt_("sth_q2", [T_CORE, 16], F32, kind="ExternalInput").ap()
        cjb_i = dt_("cjb_i", [16, 192], F32, kind="ExternalInput").ap()
        sjb_i = dt_("sjb_i", [16, 192], F32, kind="ExternalInput").ap()
        cabs96 = dt_("cabs96", [16, 96], F32, kind="ExternalInput").ap()
        sabs96 = dt_("sabs96", [16, 96], F32, kind="ExternalInput").ap()
        crk_i = dt_("crk_i", [128, 512], F32, kind="ExternalInput").ap()
        srk_i = dt_("srk_i", [128, 512], F32, kind="ExternalInput").ap()
        x_out = dt_("x_out", [D_MODEL, T_CORE], F32, kind="ExternalOutput").ap()
        if debug:
            dbg = {nm: dt_(f"dbg_{nm}", shp, dty, kind="ExternalOutput").ap()
                   for nm, shp, dty in (
                       ("I", [128, 4096], F32), ("idx", [128, 32], U32),
                       ("sctop", [128, 32], F32), ("band", [128, 192], F32),
                       ("qh1", [128, 128], F32), ("qh2", [128, 128], F32),
                       ("mh1", [128, 512], F32), ("mh2", [128, 512], F32),
                       ("outtm", [128, 512], F32), ("rs", [128, 1], F32),
                       ("ka", [16, 192], F32), ("qpa", [16, 1024], F32),
                       ("mug", [128, 1024], F32))}

        xv_ = x_fm.rearrange("(c p) t -> p c t", p=128)
        xo = x_out.rearrange("(c p) t -> p c t", p=128)

        sb = lambda n, s, d=F32: es.enter_context(nc.sbuf_tensor(n, s, d))
        pt = lambda n, s, d=F32: es.enter_context(nc.psum_tensor(n, s, d))

        ones_row = sb("ones_row", [1, 128])
        identw = sb("identw", [128, 128])
        identb = sb("identb", [128, 128], BF16)
        kvf_sb = sb("kvf_sb", [128, 4 * SPB], BF16)
        qcf_sb = sb("qcf_sb", [128, 4 * T_CORE], BF16)
        qif_sb = sb("qif_sb", [128, 2 * T_CORE], BF16)
        kif_sb = sb("kif_sb", [128, S], BF16)
        mu1_sb = sb("mu1_sb", [16, SPB], BF16)
        mu2_sb = sb("mu2_sb", [16, SPB], BF16)
        wo_sb = sb("wo_sb", [128, 4 * D_MODEL], BF16)
        iw_bc = sb("iw_bc", [128, 4])
        thr_sb = sb("thr_sb", [128, NQT])
        ramp_sb = sb("ramp_sb", [128, 96])
        mask0_sb = sb("mask0_sb", [128, 512], U8)
        crkd = sb("crkd", [128, 512])
        srkd = sb("srkd", [128, 512])
        drow = sb("drow", [1, 16])
        cd_row = sb("cd_row", [1, 16])
        sd_row = sb("sd_row", [1, 16])
        cd_bc = sb("cd_bc", [128, 16])
        sd_bc = sb("sd_bc", [128, 16])
        cd_pp = sb("cd_pp", [16, 1])
        sd_pp = sb("sd_pp", [16, 1])
        tr16 = sb("tr16", [1, 16])
        cjb_sb = sb("cjb_sb", [16, 192])
        sjb_sb = sb("sjb_sb", [16, 192])
        Cjb = sb("Cjb", [16, 192], BF16)
        Sjb = sb("Sjb", [16, 192], BF16)
        c96 = sb("c96", [16, 96])
        s96 = sb("s96", [16, 96])
        C96 = sb("C96", [16, 96], BF16)
        S96 = sb("S96", [16, 96], BF16)
        ka_band = sb("ka_band", [16, 192], BF16)
        kb_band = sb("kb_band", [16, 192], BF16)
        ka96 = sb("ka96", [16, 96], BF16)
        kb96 = sb("kb96", [16, 96], BF16)
        I_sb = sb("I_sb", [128, S])
        scrB = sb("scrB", [128, 2048])
        scrC = sb("scrC", [128, 512])
        m8 = sb("m8", [128, 8])
        idx32 = sb("idx32", [128, 32], U32)
        mug = sb("mug", [128, 32 * 32])
        KVg = sb("KVg", [128, 32 * 512], BF16)
        Vg = sb("Vg", [128, 32 * 512], BF16)
        qr_t = sb("qr_t", [128, 256])
        qm_t = sb("qm_t", [128, 256])
        cth_t = sb("cth_t", [128, 16])
        sth_t = sb("sth_t", [128, 16])
        ct2_t = sb("ct2_t", [128, 16])
        st2_t = sb("st2_t", [128, 16])
        ctd = sb("ctd", [128, 16])
        std_ = sb("std_", [128, 16])
        ct2d = sb("ct2d", [128, 16])
        st2d = sb("st2d", [128, 16])
        qh1 = sb("qh1", [128, 128])
        qh2 = sb("qh2", [128, 128])
        qp1 = sb("qp1", [128, 128])
        qp2 = sb("qp2", [128, 128])
        qp1b = sb("qp1b", [128, 128], BF16)
        qp2b = sb("qp2b", [128, 128], BF16)
        qpa_fm = sb("qpa_fm", [16, 8 * 128], BF16)
        qpb_fm = sb("qpb_fm", [16, 8 * 128], BF16)
        qsa_fm = sb("qsa_fm", [16, 8 * 128], BF16)
        qsb_fm = sb("qsb_fm", [16, 8 * 128], BF16)
        mh1 = sb("mh1", [128, 512])
        mh2 = sb("mh2", [128, 512])
        sb_band = sb("sb_band", [128, 192])
        sc_top = sb("sc_top", [128, 32])
        mb_c = sb("mb_c", [128, 1])
        mt_c = sb("mt_c", [128, 1])
        bias_c = sb("bias_c", [128, 1])
        rs1 = sb("rs1", [128, 1])
        rs2 = sb("rs2", [128, 1])
        rr_c = sb("rr_c", [128, 1])
        P_band = sb("P_band", [128, 192], BF16)
        P_top = sb("P_top", [128, 32])
        PbT1 = sb("PbT1", [128, 128], BF16)
        PbT2 = sb("PbT2", [64, 128], BF16)
        Vb1 = sb("Vb1", [128, 512], BF16)
        Vb2 = sb("Vb2", [64, 512], BF16)
        v96_sb = sb("v96_sb", [96, 512], BF16)
        ot_top = sb("ot_top", [128, 64])
        out_tm = sb("out_tm", [128, 512])
        dense_tm = sb("dense_tm", [128, 512])
        otb = sb("otb", [128, 512], BF16)
        ofm = sb("ofm", [128, 4 * 128], BF16)
        Xt = sb("Xt", [128, NCH * 128])
        oscr = sb("oscr", [128, 128])
        qct_t = sb("qct_t", [128, 512], BF16)

        ps_i = pt("ps_i", [128, 512])
        ps_band = pt("ps_band", [128, 192])
        ps_96 = pt("ps_96", [128, 96])
        ps_av = pt("ps_av", [128, 64])
        ps_tr = pt("ps_tr", [128, 128], BF16)
        ps_trf = pt("ps_trf", [128, 128])
        ps_out = pt("ps_out", [128, 128])

        sq.step("gpsimd", lambda e: e.memset(ones_row[:], 1.0))
        sq.step("gpsimd", lambda e: e.memset(identw[:], 1.0))
        sq.step("gpsimd", lambda e: e.affine_select(
            identw[:], identw[:], pattern=[[1, 128]], compare_op=ALU.is_equal,
            fill=0.0, base=0, channel_multiplier=-1))
        sq.step("vector", lambda e: e.tensor_copy(identb[:], identw[:]))
        sq.dma(kvf_sb[:].rearrange("p (c t) -> p c t", c=4),
               kv_fm_pad.rearrange("(c p) t -> p c t", p=128))
        sq.dma(qcf_sb[:].rearrange("p (c t) -> p c t", c=4),
               q_c_fmi.rearrange("(c p) t -> p c t", p=128))
        sq.dma(qif_sb[:].rearrange("p (c t) -> p c t", c=2),
               q_i_fmi.rearrange("(c p) t -> p c t", p=128))
        sq.dma(kif_sb[:], k_i_fmi)
        sq.dma(mu1_sb[:], mu1_pad)
        sq.dma(mu2_sb[:], mu2_pad)
        sq.dma(wo_sb[:].rearrange("p (c d) -> p c d", c=4),
               w_out.rearrange("(c p) d -> p c d", p=128), engine="gpsimd")
        sq.dma(thr_sb[:], thr_i)
        sq.dma(ramp_sb[:], ramp0)
        sq.dma(mask0_sb[:], mask0)
        sq.dma(crkd[:], crk_i)
        sq.dma(srkd[:], srk_i)
        sq.dma(cjb_sb[:], cjb_i)
        sq.dma(sjb_sb[:], sjb_i)
        sq.dma(c96[:], cabs96)
        sq.dma(s96[:], sabs96)
        sq.dma(drow[:], rdelta)
        sq.dma(tr16[:, 0:4], idxw_row)
        sq.step("tensor", lambda e: e.matmul(ps_i[:, 0:4], ones_row[:], tr16[:, 0:4]))
        sq.step("vector", lambda e: e.tensor_copy(iw_bc[:], ps_i[:, 0:4]))
        sq.step("scalar", lambda e: e.activation(drow[:], drow[:], AF.Sigmoid))
        sq.step("vector", lambda e: e.tensor_single_scalar(
            drow[:], drow[:], -2.0 * np.pi, ALU.mult))
        sq.step("vector", lambda e: e.tensor_single_scalar(
            tr16[:], drow[:], -np.pi, ALU.is_lt))
        sq.step("vector", lambda e: e.scalar_tensor_tensor(
            drow[:], tr16[:], 2.0 * np.pi, drow[:], ALU.mult, ALU.add))
        sq.step("scalar", lambda e: e.activation(sd_row[:], drow[:], AF.Sin))
        sq.step("vector", lambda e: e.tensor_single_scalar(
            cd_row[:], drow[:], np.pi / 2, ALU.add))
        sq.step("vector", lambda e: e.tensor_single_scalar(
            tr16[:], cd_row[:], np.pi, ALU.is_gt))
        sq.step("vector", lambda e: e.scalar_tensor_tensor(
            cd_row[:], tr16[:], -2.0 * np.pi, cd_row[:], ALU.mult, ALU.add))
        sq.step("scalar", lambda e: e.activation(cd_row[:], cd_row[:], AF.Sin))
        for src, dstb, dstp in ((cd_row, cd_bc, cd_pp), (sd_row, sd_bc, sd_pp)):
            sq.step("tensor", lambda e, src=src: e.matmul(
                ps_i[:, 0:16], ones_row[:], src[:]))
            sq.step("vector", lambda e, dstb=dstb: e.tensor_copy(dstb[:], ps_i[:, 0:16]))
            sq.step("tensor", lambda e, src=src: e.transpose(
                ps_trf[0:16, 0:1], src[:], identw[0:1, 0:1]))
            sq.step("vector", lambda e, dstp=dstp: e.tensor_copy(
                dstp[:], ps_trf[0:16, 0:1]))
        cdrep = cd_bc[:].rearrange("p (o i) -> p o i", o=1).to_broadcast([128, 32, 16])
        sdrep = sd_bc[:].rearrange("p (o i) -> p o i", o=1).to_broadcast([128, 32, 16])
        crkv = crkd[:].rearrange("p (r i) -> p r i", i=16)
        srkv = srkd[:].rearrange("p (r i) -> p r i", i=16)
        sq.step("vector", lambda e: e.tensor_mul(
            scrB[:, 0:512].rearrange("p (r i) -> p r i", i=16), crkv, cdrep))
        sq.step("vector", lambda e: e.tensor_mul(
            scrB[:, 512:1024].rearrange("p (r i) -> p r i", i=16), srkv, sdrep))
        sq.step("vector", lambda e: e.tensor_mul(
            scrB[:, 1024:1536].rearrange("p (r i) -> p r i", i=16), srkv, cdrep))
        sq.step("vector", lambda e: e.tensor_mul(
            scrB[:, 1536:2048].rearrange("p (r i) -> p r i", i=16), crkv, sdrep))
        sq.step("vector", lambda e: e.tensor_sub(
            crkd[:], scrB[:, 0:512], scrB[:, 512:1024]))
        sq.step("vector", lambda e: e.tensor_add(
            srkd[:], scrB[:, 1024:1536], scrB[:, 1536:2048]))
        sq.step("vector", lambda e: e.tensor_scalar(
            scrB[0:16, 0:192], cjb_sb[:], cd_pp[:], None, ALU.mult))
        sq.step("vector", lambda e: e.tensor_scalar(
            scrC[0:16, 0:192], sjb_sb[:], sd_pp[:], None, ALU.mult))
        sq.step("vector", lambda e: e.tensor_sub(
            Cjb[:], scrB[0:16, 0:192], scrC[0:16, 0:192]))
        sq.step("vector", lambda e: e.tensor_scalar(
            scrB[0:16, 0:192], sjb_sb[:], cd_pp[:], None, ALU.mult))
        sq.step("vector", lambda e: e.tensor_scalar(
            scrC[0:16, 0:192], cjb_sb[:], sd_pp[:], None, ALU.mult))
        sq.step("vector", lambda e: e.tensor_add(
            Sjb[:], scrB[0:16, 0:192], scrC[0:16, 0:192]))
        sq.step("vector", lambda e: e.tensor_scalar(
            scrB[0:16, 0:96], c96[:], cd_pp[:], None, ALU.mult))
        sq.step("vector", lambda e: e.tensor_scalar(
            scrC[0:16, 0:96], s96[:], sd_pp[:], None, ALU.mult))
        sq.step("vector", lambda e: e.tensor_sub(
            C96[:], scrB[0:16, 0:96], scrC[0:16, 0:96]))
        sq.step("vector", lambda e: e.tensor_scalar(
            scrB[0:16, 0:96], s96[:], cd_pp[:], None, ALU.mult))
        sq.step("vector", lambda e: e.tensor_scalar(
            scrC[0:16, 0:96], c96[:], sd_pp[:], None, ALU.mult))
        sq.step("vector", lambda e: e.tensor_add(
            S96[:], scrB[0:16, 0:96], scrC[0:16, 0:96]))
        sq.step("vector", lambda e: e.tensor_mul(
            scrB[0:16, 0:96], mu1_sb[0:16, PADK:PADK + 96], C96[:]))
        sq.step("vector", lambda e: e.tensor_mul(
            scrC[0:16, 0:96], mu2_sb[0:16, PADK:PADK + 96], S96[:]))
        sq.step("vector", lambda e: e.tensor_sub(
            ka96[:], scrB[0:16, 0:96], scrC[0:16, 0:96]))
        sq.step("vector", lambda e: e.tensor_mul(
            scrB[0:16, 0:96], mu1_sb[0:16, PADK:PADK + 96], S96[:]))
        sq.step("vector", lambda e: e.tensor_mul(
            scrC[0:16, 0:96], mu2_sb[0:16, PADK:PADK + 96], C96[:]))
        sq.step("vector", lambda e: e.tensor_add(
            kb96[:], scrB[0:16, 0:96], scrC[0:16, 0:96]))
        sq.dma(v96_sb[:], v_pad[PADK:PADK + 96, :])
        iota_c = sb("iota_c", [128, 512])
        bandmask = sb("bandmask", [128, 192])
        thr_adj = sb("thr_adj", [128, 1])
        sq.step("gpsimd", lambda e: e.iota(
            iota_c[:], pattern=[[1, 512]], base=0, channel_multiplier=0,
            allow_small_or_imprecise_dtypes=True))
        sq.step("gpsimd", lambda e: e.memset(bandmask[:], 0.0))
        sq.step("gpsimd", lambda e: e.affine_select(
            bandmask[:], bandmask[:], pattern=[[1, 192]], compare_op=ALU.is_ge,
            fill=-3e38, base=0, channel_multiplier=-1))
        sq.step("gpsimd", lambda e: e.affine_select(
            bandmask[:], bandmask[:], pattern=[[-1, 192]], compare_op=ALU.is_ge,
            fill=-3e38, base=63, channel_multiplier=1))

        for qt in range(NQT):
            t0 = qt * 128
            for kt in range(8):
                for hh in range(4):
                    hb = 64 * (hh % 2)
                    hc = hh // 2
                    sq.step("tensor", lambda e, hb=hb, hc=hc, t0=t0, kt=kt: e.matmul(
                        ps_i[:], qif_sb[hb:hb + 64,
                                        hc * T_CORE + t0:hc * T_CORE + t0 + 128],
                        kif_sb[hb:hb + 64, kt * 512:(kt + 1) * 512],
                        start=True, stop=True))
                    if hh == 0:
                        sq.step("vector", lambda e, kt=kt, hh=hh: e.tensor_scalar(
                            I_sb[:, kt * 512:(kt + 1) * 512], ps_i[:], 0.0,
                            iw_bc[:, hh:hh + 1], ALU.max, ALU.mult))
                    else:
                        sq.step("vector", lambda e, kt=kt, hh=hh: e.tensor_scalar(
                            scrC[:], ps_i[:], 0.0, iw_bc[:, hh:hh + 1],
                            ALU.max, ALU.mult))
                        sq.step("vector", lambda e, kt=kt: e.tensor_add(
                            I_sb[:, kt * 512:(kt + 1) * 512],
                            I_sb[:, kt * 512:(kt + 1) * 512], scrC[:]))
            for kt in range(8):
                sq.step("vector", lambda e, qt=qt, kt=kt: e.tensor_single_scalar(
                    thr_adj[:], thr_sb[:, qt:qt + 1], float(-512 * kt), ALU.add))
                sq.step("vector", lambda e: e.tensor_scalar(
                    scrC[:], iota_c[:], thr_adj[:], None, ALU.is_le))
                sq.step("vector", lambda e: e.tensor_scalar(
                    scrC[:], scrC[:], 2e38, -1e38, ALU.mult, ALU.add))
                sq.step("vector", lambda e, kt=kt: e.tensor_tensor(
                    I_sb[:, kt * 512:(kt + 1) * 512],
                    I_sb[:, kt * 512:(kt + 1) * 512], scrC[:], ALU.min))
            if qt == 0:
                sq.step("vector", lambda e: e.tensor_max(
                    I_sb[:, 0:96], I_sb[:, 0:96], ramp_sb[:]))
            for rd in range(4):
                sq.step("vector", lambda e: e.max(out=m8[:], in_=I_sb[:]))
                sq.step("vector", lambda e, rd=rd: e.max_index(
                    out=idx32[:, rd * 8:(rd + 1) * 8], in_max=m8[:], in_values=I_sb[:]))
                sq.step("vector", lambda e: e.match_replace(
                    out=I_sb[:], in_to_replace=m8[:], in_values=I_sb[:],
                    imm_value=-3e38))
            for r in range(32):
                sq.step("gpsimd", lambda e, r=r: e.indirect_dma_start(
                    out=mug[:, r * 32:(r + 1) * 32], out_offset=None,
                    in_=mu_tm_full,
                    in_offset=IndirectOffsetOnAxis(ap=idx32[:, r:r + 1], axis=0),
                    oob_is_err=False), inc=16)
                sq.step("gpsimd", lambda e, r=r: e.indirect_dma_start(
                    out=KVg[:, r * 512:(r + 1) * 512], out_offset=None,
                    in_=kv_tm_full,
                    in_offset=IndirectOffsetOnAxis(ap=idx32[:, r:r + 1], axis=0),
                    oob_is_err=False), inc=16)
                sq.step("gpsimd", lambda e, r=r: e.indirect_dma_start(
                    out=Vg[:, r * 512:(r + 1) * 512], out_offset=None,
                    in_=v_tm_full,
                    in_offset=IndirectOffsetOnAxis(ap=idx32[:, r:r + 1], axis=0),
                    oob_is_err=False), inc=16)
            if debug and qt == 8:
                sq.dma(dbg["I"], I_sb[:])
                sq.dma(dbg["idx"], idx32[:])
            sq.dma(qr_t[:], qr_tmi[t0:t0 + 128, :])
            sq.dma(qct_t[:], q_c_tm[t0:t0 + 128, :])
            sq.dma(cth_t[:], cth_q[t0:t0 + 128, :])
            sq.dma(sth_t[:], sth_q[t0:t0 + 128, :])
            sq.dma(ct2_t[:], cth_q2[t0:t0 + 128, :])
            sq.dma(st2_t[:], sth_q2[t0:t0 + 128, :])
            sq.step("scalar", lambda e: e.activation(qm_t[:], qr_t[:], AF.Abs))
            sq.step("scalar", lambda e: e.activation(qm_t[:], qm_t[:], AF.Exp, scale=-1.0))
            sq.step("scalar", lambda e: e.activation(qm_t[:], qm_t[:], AF.Ln, bias=1.0))
            sq.step("vector", lambda e: e.tensor_single_scalar(
                scrB[:, 0:256], qr_t[:], 0.0, ALU.max))
            sq.step("vector", lambda e: e.tensor_add(qm_t[:], qm_t[:], scrB[:, 0:256]))
            for (csrc, ssrc, cdst, sdst) in ((cth_t, sth_t, ctd, std_),
                                             (ct2_t, st2_t, ct2d, st2d)):
                sq.step("vector", lambda e, csrc=csrc, cdst=cdst: e.tensor_mul(
                    cdst[:], csrc[:], cd_bc[:]))
                sq.step("vector", lambda e, ssrc=ssrc: e.tensor_mul(
                    scrC[:, 0:16], ssrc[:], sd_bc[:]))
                sq.step("vector", lambda e, cdst=cdst: e.tensor_sub(
                    cdst[:], cdst[:], scrC[:, 0:16]))
                sq.step("vector", lambda e, ssrc=ssrc, sdst=sdst: e.tensor_mul(
                    sdst[:], ssrc[:], cd_bc[:]))
                sq.step("vector", lambda e, csrc=csrc: e.tensor_mul(
                    scrC[:, 0:16], csrc[:], sd_bc[:]))
                sq.step("vector", lambda e, sdst=sdst: e.tensor_add(
                    sdst[:], sdst[:], scrC[:, 0:16]))
            qm1 = qm_t[:].rearrange("p (h two i) -> p h two i", two=2, i=16)[:, :, 0, :]
            qm2 = qm_t[:].rearrange("p (h two i) -> p h two i", two=2, i=16)[:, :, 1, :]
            for (cc, ss, d1, d2) in ((ctd, std_, qh1, qh2), (ct2d, st2d, qp1, qp2)):
                ccb = cc[:].rearrange("p (o i) -> p o i", o=1).to_broadcast([128, 8, 16])
                ssb = ss[:].rearrange("p (o i) -> p o i", o=1).to_broadcast([128, 8, 16])
                d1v = d1[:].rearrange("p (h i) -> p h i", i=16)
                d2v = d2[:].rearrange("p (h i) -> p h i", i=16)
                sB = scrB[:, 0:128].rearrange("p (h i) -> p h i", i=16)
                sq.step("vector", lambda e, d1v=d1v, ccb=ccb: e.tensor_mul(d1v, qm1, ccb))
                sq.step("vector", lambda e, sB=sB, ssb=ssb: e.tensor_mul(sB, qm2, ssb))
                sq.step("vector", lambda e, d1=d1: e.tensor_sub(
                    d1[:], d1[:], scrB[:, 0:128]))
                sq.step("vector", lambda e, d2v=d2v, ssb=ssb: e.tensor_mul(d2v, qm1, ssb))
                sq.step("vector", lambda e, sB=sB, ccb=ccb: e.tensor_mul(sB, qm2, ccb))
                sq.step("vector", lambda e, d2=d2: e.tensor_add(
                    d2[:], d2[:], scrB[:, 0:128]))
            sq.step("vector", lambda e: e.tensor_copy(qp1b[:], qp1[:]))
            sq.step("vector", lambda e: e.tensor_copy(qp2b[:], qp2[:]))
            for h in range(8):
                sq.step("tensor", lambda e, h=h: e.transpose(
                    ps_tr[0:16, 0:128], qp1b[:, h * 16:(h + 1) * 16], identb[:]))
                sq.step("vector", lambda e, h=h: e.tensor_copy(
                    qpa_fm[:, h * 128:(h + 1) * 128], ps_tr[0:16, 0:128]))
                sq.step("tensor", lambda e, h=h: e.transpose(
                    ps_tr[0:16, 0:128], qp2b[:, h * 16:(h + 1) * 16], identb[:]))
                sq.step("vector", lambda e, h=h: e.tensor_copy(
                    qpb_fm[:, h * 128:(h + 1) * 128], ps_tr[0:16, 0:128]))
            if qt == 0:
                sq.step("vector", lambda e: e.tensor_copy(qp1b[:], qh1[:]))
                sq.step("vector", lambda e: e.tensor_copy(qp2b[:], qh2[:]))
                for h in range(8):
                    sq.step("tensor", lambda e, h=h: e.transpose(
                        ps_tr[0:16, 0:128], qp1b[:, h * 16:(h + 1) * 16], identb[:]))
                    sq.step("vector", lambda e, h=h: e.tensor_copy(
                        qsa_fm[:, h * 128:(h + 1) * 128], ps_tr[0:16, 0:128]))
                    sq.step("tensor", lambda e, h=h: e.transpose(
                        ps_tr[0:16, 0:128], qp2b[:, h * 16:(h + 1) * 16], identb[:]))
                    sq.step("vector", lambda e, h=h: e.tensor_copy(
                        qsb_fm[:, h * 128:(h + 1) * 128], ps_tr[0:16, 0:128]))
            mg1 = mug[:].rearrange("p (r two i) -> p r two i", two=2, i=16)[:, :, 0, :]
            mg2 = mug[:].rearrange("p (r two i) -> p r two i", two=2, i=16)[:, :, 1, :]
            crv = crkd[:].rearrange("p (r i) -> p r i", i=16)
            srv = srkd[:].rearrange("p (r i) -> p r i", i=16)
            m1v = mh1[:].rearrange("p (r i) -> p r i", i=16)
            m2v = mh2[:].rearrange("p (r i) -> p r i", i=16)
            sBv = scrB[:, 0:512].rearrange("p (r i) -> p r i", i=16)
            sq.step("vector", lambda e: e.tensor_mul(m1v, mg1, crv))
            sq.step("vector", lambda e: e.tensor_mul(sBv, mg2, srv))
            sq.step("vector", lambda e: e.tensor_sub(mh1[:], mh1[:], scrB[:, 0:512]))
            sq.step("vector", lambda e: e.tensor_mul(m2v, mg1, srv))
            sq.step("vector", lambda e: e.tensor_mul(sBv, mg2, crv))
            sq.step("vector", lambda e: e.tensor_add(mh2[:], mh2[:], scrB[:, 0:512]))
            for (dst, A, Bt) in ((ka_band, Cjb, Sjb), (kb_band, Sjb, Cjb)):
                sq.step("vector", lambda e, t0=t0, A=A: e.tensor_mul(
                    scrB[0:16, 0:192], mu1_sb[0:16, t0:t0 + 192], A[:]))
                sq.step("vector", lambda e, t0=t0, Bt=Bt: e.tensor_mul(
                    scrC[0:16, 0:192], mu2_sb[0:16, t0:t0 + 192], Bt[:]))
                if dst is ka_band:
                    sq.step("vector", lambda e, dst=dst: e.tensor_sub(
                        dst[:], scrB[0:16, 0:192], scrC[0:16, 0:192]))
                else:
                    sq.step("vector", lambda e, dst=dst: e.tensor_add(
                        dst[:], scrB[0:16, 0:192], scrC[0:16, 0:192]))
            sq.dma(Vb1[:], v_pad[t0:t0 + 128, :])
            sq.dma(Vb2[:], v_pad[t0 + 128:t0 + 192, :])

            for h in range(8):
                hb = 64 * (h % 2)
                hc = h // 2
                sq.step("tensor", lambda e, hb=hb, hc=hc, t0=t0: e.matmul(
                    ps_band[:], qcf_sb[hb:hb + 64, hc * T_CORE + t0:hc * T_CORE + t0 + 128],
                    kvf_sb[hb:hb + 64, hc * SPB + t0:hc * SPB + t0 + 192],
                    start=True, stop=False))
                sq.step("tensor", lambda e, h=h: e.matmul(
                    ps_band[:], qpa_fm[:, h * 128:(h + 1) * 128], ka_band[:],
                    start=False, stop=False))
                sq.step("tensor", lambda e, h=h: e.matmul(
                    ps_band[:], qpb_fm[:, h * 128:(h + 1) * 128], kb_band[:],
                    start=False, stop=True))
                sq.step("vector", lambda e: e.tensor_add(
                    sb_band[:], ps_band[:], bandmask[:]))
                sq.step("vector", lambda e: e.reduce_max(mb_c[:], sb_band[:], axis=AX.X))
                qcb = qct_t[:].rearrange("p (h i) -> p h i", i=64)[
                    :, h:h + 1, :].to_broadcast([128, 32, 64])
                kgv = KVg[:].rearrange("p (r v) -> p r v", v=512)[:, :, h * 64:(h + 1) * 64]
                sq.step("vector", lambda e, qcb=qcb, kgv=kgv: e.tensor_mul(
                    scrB[:, 0:2048].rearrange("p (r i) -> p r i", i=64), kgv, qcb))
                sq.step("vector", lambda e: e.reduce_sum(
                    sc_top[:], scrB[:, 0:2048].rearrange("p (r i) -> p r i", i=64),
                    axis=AX.X))
                q1b = qh1[:].rearrange("p (h i) -> p h i", i=16)[
                    :, h:h + 1, :].to_broadcast([128, 32, 16])
                q2b = qh2[:].rearrange("p (h i) -> p h i", i=16)[
                    :, h:h + 1, :].to_broadcast([128, 32, 16])
                sq.step("vector", lambda e, q1b=q1b: e.tensor_mul(
                    scrB[:, 0:512].rearrange("p (r i) -> p r i", i=16), m1v, q1b))
                sq.step("vector", lambda e, q2b=q2b: e.tensor_mul(
                    scrB[:, 512:1024].rearrange("p (r i) -> p r i", i=16), m2v, q2b))
                sq.step("vector", lambda e: e.tensor_add(
                    scrB[:, 0:512], scrB[:, 0:512], scrB[:, 512:1024]))
                sq.step("vector", lambda e: e.reduce_sum(
                    scrC[:, 0:32], scrB[:, 0:512].rearrange("p (r i) -> p r i", i=16),
                    axis=AX.X))
                sq.step("vector", lambda e: e.tensor_add(
                    sc_top[:], sc_top[:], scrC[:, 0:32]))
                if debug and qt == 8 and h == 0:
                    sq.dma(dbg["sctop"], sc_top[:])
                    sq.dma(dbg["band"], sb_band[:])
                    sq.dma(dbg["qh1"], qh1[:])
                    sq.dma(dbg["qh2"], qh2[:])
                    sq.dma(dbg["mh1"], mh1[:])
                    sq.dma(dbg["mh2"], mh2[:])
                sq.step("vector", lambda e: e.reduce_max(mt_c[:], sc_top[:], axis=AX.X))
                sq.step("vector", lambda e: e.tensor_max(mb_c[:], mb_c[:], mt_c[:]))
                sq.step("vector", lambda e: e.tensor_single_scalar(
                    bias_c[:], mb_c[:], -ATT2_SCALE, ALU.mult))
                sq.step("scalar", lambda e: e.activation(
                    P_band[:], sb_band[:], AF.Exp, bias=bias_c[:],
                    scale=ATT2_SCALE, accum_out=rs1[:]))
                sq.step("scalar", lambda e: e.activation(
                    P_top[:], sc_top[:], AF.Exp, bias=bias_c[:],
                    scale=ATT2_SCALE, accum_out=rs2[:]))
                sq.step("vector", lambda e: e.tensor_add(rs1[:], rs1[:], rs2[:]))
                sq.step("vector", lambda e: e.reciprocal(rr_c[:], rs1[:]))
                sq.step("tensor", lambda e: e.transpose(
                    ps_tr[:], P_band[:, 0:128], identb[:]))
                sq.step("vector", lambda e: e.tensor_copy(PbT1[:], ps_tr[:]))
                sq.step("tensor", lambda e: e.transpose(
                    ps_tr[0:64, 0:128], P_band[:, 128:192], identb[:]))
                sq.step("vector", lambda e: e.tensor_copy(PbT2[:], ps_tr[0:64, 0:128]))
                sq.step("tensor", lambda e, h=h: e.matmul(
                    ps_av[:], PbT1[:], Vb1[:, h * 64:(h + 1) * 64],
                    start=True, stop=False))
                sq.step("tensor", lambda e, h=h: e.matmul(
                    ps_av[:], PbT2[:], Vb2[0:64, h * 64:(h + 1) * 64],
                    start=False, stop=True))
                vgv = Vg[:].rearrange("p (r v) -> p v r", v=512)[:, h * 64:(h + 1) * 64, :]
                ptb = P_top[:].rearrange("p (o r) -> p o r", o=1).to_broadcast([128, 64, 32])
                sq.step("vector", lambda e, vgv=vgv, ptb=ptb: e.tensor_mul(
                    scrB[:, 0:2048].rearrange("p (v r) -> p v r", r=32), vgv, ptb))
                sq.step("vector", lambda e: e.reduce_sum(
                    ot_top[:], scrB[:, 0:2048].rearrange("p (v r) -> p v r", r=32),
                    axis=AX.X))
                sq.step("vector", lambda e: e.tensor_add(ot_top[:], ot_top[:], ps_av[:]))
                sq.step("vector", lambda e, h=h: e.tensor_scalar(
                    out_tm[:, h * 64:(h + 1) * 64], ot_top[:], rr_c[:], None, ALU.mult))
                if qt == 0:
                    sq.step("tensor", lambda e, hb=hb, hc=hc: e.matmul(
                        ps_96[:], qcf_sb[hb:hb + 64, hc * T_CORE:hc * T_CORE + 128],
                        kvf_sb[hb:hb + 64, hc * SPB + PADK:hc * SPB + PADK + 96],
                        start=True, stop=False))
                    sq.step("tensor", lambda e, h=h: e.matmul(
                        ps_96[:], qsa_fm[:, h * 128:(h + 1) * 128], ka96[:],
                        start=False, stop=False))
                    sq.step("tensor", lambda e, h=h: e.matmul(
                        ps_96[:], qsb_fm[:, h * 128:(h + 1) * 128], kb96[:],
                        start=False, stop=True))
                    sq.step("vector", lambda e: e.tensor_copy(
                        scrB[:, 0:96], ps_96[:]))
                    sq.step("vector", lambda e: e.reduce_max(
                        mb_c[:], scrB[:, 0:96], axis=AX.X))
                    sq.step("vector", lambda e: e.tensor_single_scalar(
                        bias_c[:], mb_c[:], -ATT2_SCALE, ALU.mult))
                    sq.step("scalar", lambda e: e.activation(
                        P_band[:, 0:96], scrB[:, 0:96], AF.Exp, bias=bias_c[:],
                        scale=ATT2_SCALE, accum_out=rs1[:]))
                    sq.step("vector", lambda e: e.reciprocal(rr_c[:], rs1[:]))
                    sq.step("tensor", lambda e: e.transpose(
                        ps_tr[0:96, 0:128], P_band[:, 0:96], identb[:]))
                    sq.step("vector", lambda e: e.tensor_copy(
                        PbT1[0:96, :], ps_tr[0:96, 0:128]))
                    sq.step("tensor", lambda e, h=h: e.matmul(
                        ps_av[:], PbT1[0:96, :], v96_sb[0:96, h * 64:(h + 1) * 64],
                        start=True, stop=True))
                    sq.step("vector", lambda e, h=h: e.tensor_scalar(
                        dense_tm[:, h * 64:(h + 1) * 64], ps_av[:], rr_c[:],
                        None, ALU.mult))
            if qt == 0:
                sq.step("vector", lambda e: e.tensor_copy(scrB[:, 0:512], out_tm[:]))
                sq.step("vector", lambda e: e.select(
                    out_tm[:], mask0_sb[:], dense_tm[:], scrB[:, 0:512]))
            if debug and qt == 8:
                sq.dma(dbg["outtm"], out_tm[:])
                sq.dma(dbg["rs"], rs1[:])
                sq.step("vector", lambda e: e.tensor_copy(scrC[0:16, 0:192], ka_band[:]))
                sq.dma(dbg["ka"], scrC[0:16, 0:192])
                sq.step("vector", lambda e: e.tensor_copy(scrB[0:16, 0:1024], qpa_fm[:]))
                sq.dma(dbg["qpa"], scrB[0:16, 0:1024])
                sq.dma(dbg["mug"], mug[:])
            sq.step("vector", lambda e: e.tensor_copy(otb[:], out_tm[:]))
            for c in range(4):
                sq.step("tensor", lambda e, c=c: e.transpose(
                    ps_tr[:], otb[:, c * 128:(c + 1) * 128], identb[:]))
                sq.step("vector", lambda e, c=c: e.tensor_copy(
                    ofm[:, c * 128:(c + 1) * 128], ps_tr[:]))
            sq.dma(Xt[:].rearrange("p (c t) -> p c t", c=NCH), xv_[:, :, t0:t0 + 128])
            for d in range(NCH):
                sq.step("tensor", lambda e, d=d: [
                    e.matmul(ps_out[:], wo_sb[:, c * D_MODEL + d * 128:
                                              c * D_MODEL + (d + 1) * 128],
                             ofm[:, c * 128:(c + 1) * 128],
                             start=(c == 0), stop=(c == 3))
                    for c in range(4)])
                sq.step("vector", lambda e, d=d: e.tensor_add(
                    oscr[:], ps_out[:], Xt[:, d * 128:(d + 1) * 128]))
                sq.dma(xo[:, d, t0:t0 + 128], oscr[:])

    return seq_program(prog)


THETA16 = (10000.0 ** (-2.0 * np.arange(16) / 32.0)).astype(np.float64)


def host_topk_attn_inputs(x, proj_outs, W, core):
    b, h = core // 2, core % 2
    t0g = h * 2048
    full = lambda key: np.concatenate(
        [proj_outs[2 * b][key], proj_outs[2 * b + 1][key]], axis=0)
    kv_full = full("kv_tm")
    v_full = full("v_tm")
    mu_full = full("mu_tm")
    ki_full = full("k_i_tm")
    SPB = 2112
    zc = np.zeros((512, 1), kv_full.dtype)
    kv_fm_pad = np.concatenate(
        [np.zeros((512, PADK), kv_full.dtype), np.ascontiguousarray(kv_full.T),
         zc], axis=1)[:, t0g:t0g + SPB]
    v_pad = np.concatenate([np.zeros((PADK, 512), v_full.dtype), v_full,
                            np.zeros((1, 512), v_full.dtype)],
                           axis=0)[t0g:t0g + SPB]
    mu1_pad = np.concatenate(
        [np.zeros((16, PADK), kv_full.dtype),
         np.ascontiguousarray(mu_full[:, 0:16].T).astype(kv_full.dtype),
         zc[0:16]], axis=1)[:, t0g:t0g + SPB]
    mu2_pad = np.concatenate(
        [np.zeros((16, PADK), kv_full.dtype),
         np.ascontiguousarray(mu_full[:, 16:32].T).astype(kv_full.dtype),
         zc[0:16]], axis=1)[:, t0g:t0g + SPB]

    tg = np.arange(t0g, t0g + T_CORE, dtype=np.float64)
    t0tile = (np.arange(T_CORE) // 128 * 128) + t0g
    ang_q = tg[:, None] * THETA16[None, :]
    ang_q2 = (2 * tg - t0tile)[:, None] * THETA16[None, :]
    jj = np.arange(192, dtype=np.float64)
    ang_j = jj[None, :] * THETA16[:, None]
    ss = np.arange(96, dtype=np.float64)
    ang_s = ss[None, :] * THETA16[:, None]
    rr = np.arange(64, 96, dtype=np.float64)
    ang_r = rr[:, None] * THETA16[None, :]
    crk = np.tile(np.cos(ang_r).astype(np.float32).reshape(1, 512), (128, 1))
    srk = np.tile(np.sin(ang_r).astype(np.float32).reshape(1, 512), (128, 1))

    thr = np.zeros((128, 16), np.float32)
    for qt in range(16):
        thr[:, qt] = t0g + qt * 128 + np.arange(128) - 64.0
    ramp0 = np.full((128, 96), -3e38, np.float32)
    if h == 0:
        for p in range(64, 95):
            t = p
            for s in range(t + 1, 96):
                ramp0[p, s] = -1e30 - s * 1e26
    mask0 = np.zeros((128, 512), np.uint8)
    if h == 0:
        mask0[0:64, :] = 1

    return {
        "x_fm": fmT(core_slice_tokens(x, core)),
        "q_c_tm": proj_outs[core]["q_c_tm"],
        "q_c_fmi": np.ascontiguousarray(proj_outs[core]["q_c_tm"].T),
        "qr_tmi": proj_outs[core]["qr_tm"],
        "q_i_fmi": np.ascontiguousarray(proj_outs[core]["q_i_tm"].T),
        "k_i_fmi": np.ascontiguousarray(
            np.concatenate([ki_full.T, ki_full.T], axis=0)),
        "kv_fm_pad": np.ascontiguousarray(kv_fm_pad),
        "kv_tm_full": np.ascontiguousarray(kv_full),
        "v_tm_full": np.ascontiguousarray(v_full),
        "v_pad": np.ascontiguousarray(v_pad),
        "mu1_pad": np.ascontiguousarray(mu1_pad),
        "mu2_pad": np.ascontiguousarray(mu2_pad),
        "mu_tm_full": np.ascontiguousarray(mu_full),
        "w_out": np.ascontiguousarray(W["W_out"]),
        "idxw_row": np.asarray(W["idx_w"], np.float32)[None, :],
        "rdelta": np.asarray(W["raw_delta"], np.float32)[None, :],
        "thr_i": thr,
        "ramp0": ramp0,
        "mask0": mask0,
        "cth_q": np.cos(ang_q).astype(np.float32),
        "sth_q": np.sin(ang_q).astype(np.float32),
        "cth_q2": np.cos(ang_q2).astype(np.float32),
        "sth_q2": np.sin(ang_q2).astype(np.float32),
        "cjb_i": np.cos(ang_j).astype(np.float32),
        "sjb_i": np.sin(ang_j).astype(np.float32),
        "cabs96": np.cos(ang_s).astype(np.float32),
        "sabs96": np.sin(ang_s).astype(np.float32),
        "crk_i": crk,
        "srk_i": srk,
    }



def _run(nc, in_maps):
    res = run_bass_kernel_spmd(nc, in_maps, core_ids=list(range(N_CORES)))
    return res.results


_PROG_CACHE = {}


def _prog(name, builder):
    if name not in _PROG_CACHE:
        _PROG_CACHE[name] = builder()
    return _PROG_CACHE[name]


def kernel(**inp):
    x = np.asarray(inp["L0_attn_a_x"], np.float32)

    Wm = {k: inp[f"L0_attn_a_{k}"] for k in
          ["in_proj_w", "conv_w", "conv_b", "x_proj_w", "dt_proj_w",
           "dt_proj_b", "A_log", "D", "out_proj_w"]}
    pre = _run(_prog("mamba_pre", build_mamba_pre),
               [host_mamba_pre_inputs(x, Wm, c) for c in range(8)])
    sc = _run(_prog("mamba_scan", build_mamba_scan),
              [host_mamba_scan_inputs(pre, Wm, c) for c in range(8)])
    post = _run(_prog("mamba_post", build_mamba_post),
                [host_mamba_post_inputs(x, sc, pre, Wm, c) for c in range(8)])
    x = unpack_fm_out(post)

    Wf = {k: inp[f"L0_ffn_{k}"] for k in
          ["router_w", "w_gate", "w_up", "w_down", "sw_gate", "sw_up", "sw_down"]}
    moe = _run(_prog("l0_ffn", build_l0_ffn),
               [host_l0_ffn_inputs(x, Wf, c) for c in range(8)])
    x = unpack_fm_out(moe)

    Wb = {k: inp[f"L0_attn_b_{k}"] for k in
          ["Wd_q_w", "Wd_q_b", "Wu_q_w", "Wu_q_b", "Wq_r_w", "Wq_r_b",
           "Wk_r_w", "Wk_r_b", "Wd_kv_w", "Wd_kv_b", "Wu_k_w", "Wu_k_b",
           "Wu_v_w", "Wu_v_b", "Wo_w", "Wo_b"]}
    qkv = _run(_prog("mla_qkv", build_mla_qkv),
               [host_mla_qkv_inputs(x, Wb, c) for c in range(8)])
    att = _run(_prog("mla_attn", build_mla_attn),
               [host_mla_attn_inputs(x, qkv, Wb, c) for c in range(8)])
    x = unpack_fm_out(att)

    moe = _run(_prog("l0_ffn", build_l0_ffn),
               [host_l0_ffn_inputs(x, Wf, c) for c in range(8)])
    x = unpack_fm_out(moe)

    Wa = {k: np.asarray(inp[f"L1_attn_a_{k}"], np.float32) for k in
          ["W_dkv", "W_uk", "W_uv", "W_dq", "W_uq", "W_qr", "W_kr", "W_out",
           "idx_wq", "idx_wk", "idx_w", "raw_delta"]}
    prj = _run(_prog("topk_proj", build_topk_proj),
               [host_topk_proj_inputs(x, Wa, c) for c in range(8)])
    ta = _run(_prog("topk_attn", build_topk_attn),
              [host_topk_attn_inputs(x, prj, Wa, c) for c in range(8)])
    x = unpack_fm_out(ta)

    Wl = {k: inp[f"L1_ffn_{k}"] for k in ["W_r", "W_k", "W_v", "mu_r", "mu_k"]}
    ff = _run(_prog("l1_ffn", build_l1_ffn),
              [host_l1_ffn_inputs(x, Wl, c) for c in range(8)])
    x = unpack_fm_out(ff)

    Ww = {k: inp[f"L1_attn_b_{k}"] for k in
          ["mu_k", "mu_v", "mu_r", "Wk", "Wv", "Wr", "W_out", "w", "u"]}
    kv = _run(_prog("wkv_kv", build_wkv_kv),
              [host_wkv_kv_inputs(x, Ww, c) for c in range(8)])
    wo = _run(_prog("wkv_out", build_wkv_out),
              [host_wkv_out_inputs(x, kv, Ww, c) for c in range(8)])
    x = unpack_fm_out(wo)

    ff = _run(_prog("l1_ffn", build_l1_ffn),
              [host_l1_ffn_inputs(x, Wl, c) for c in range(8)])
    x = unpack_fm_out(ff)

    return x


# version 03
# speedup vs baseline: 1.2203x; 1.2203x over previous
import numpy as np
import ml_dtypes
from contextlib import ExitStack

import concourse.bass as bass
import concourse.mybir as mybir
from concourse.bass import IndirectOffsetOnAxis
from concourse.bass_utils import run_bass_kernel_spmd

F32 = mybir.dt.float32
BF16 = mybir.dt.bfloat16
U32 = mybir.dt.uint32
I16 = mybir.dt.int16
U8 = mybir.dt.uint8
AF = mybir.ActivationFunctionType
ALU = mybir.AluOpType
AX = mybir.AxisListType

D_MODEL = 1024
NCH = D_MODEL // 128
T_CORE = 2048
N_CORES = 8
LN_EPS = 1e-5

BF = ml_dtypes.bfloat16

ENGINES = ("sync", "scalar", "vector", "tensor", "gpsimd")



def fmT(a):
    return np.ascontiguousarray(np.asarray(a, dtype=np.float32).T)


def pack_vec(v):
    return np.ascontiguousarray(np.asarray(v, dtype=np.float32).reshape(-1, 128).T)


def core_slice_tokens(x, core):
    b, h = core // 2, core % 2
    return np.asarray(x[b, h * 2048:(h + 1) * 2048])



class Seq:

    def __init__(self, nc):
        self.nc = nc
        self.steps = {e: [] for e in ENGINES}
        self.counts = {}
        self.waited = {e: {} for e in ENGINES}

    def _tok_list(self, wait):
        if wait is None:
            return [(s, c) for s, c in self.counts.items()]
        if isinstance(wait, tuple):
            return [wait]
        out = {}
        for s, c in wait:
            if c > out.get(s, 0):
                out[s] = c
        return list(out.items())

    def step(self, engine, fn, inc=1, wait=None, dma=False):
        stream = (engine, 'q' if dma else 'c')
        toks = self._tok_list(wait)
        waits = []
        wt = self.waited[engine]
        for s, c in toks:
            assert c <= self.counts.get(s, 0), f"future wait {s}:{c}"
            if wt.get(s, 0) < c:
                waits.append((s, c))
                wt[s] = c
        self.counts[stream] = self.counts.get(stream, 0) + inc
        self.steps[engine].append((waits, fn, inc, stream))
        return (stream, self.counts[stream])

    def dma(self, out, in_, engine="sync", wait=None, **kw):
        return self.step(engine,
                         lambda e: e.dma_start(out=out, in_=in_, **kw),
                         inc=16, wait=wait, dma=True)

    def barrier(self):
        return [(s, c) for s, c in self.counts.items()]

    def build_into(self, block, sems):
        for ename in ENGINES:
            steps = self.steps[ename]
            if not steps:
                continue

            def make(steps):
                def prog(eng):
                    for waits, fn, inc, stream in steps:
                        for s, c in waits:
                            eng.wait_ge(sems[s], c)
                        ins = fn(eng)
                        if isinstance(ins, list):
                            ins = ins[-1]
                        ins.then_inc(sems[stream], inc)
                return prog
            getattr(block, ename)(make(steps))

    def finish(self, block, sems):
        toks = [(s, c) for s, c in self.counts.items()]
        wt = self.waited["sync"]

        def prog(eng):
            for s, c in toks:
                if wt.get(s, 0) < c or s == ("sync", "q"):
                    eng.wait_ge(sems[s], c)
        block.sync(prog)


def tmax(*toks):
    out = {}
    for t in toks:
        if t is None:
            continue
        items = [t] if isinstance(t, tuple) and len(t) == 2 and \
            isinstance(t[1], int) else t
        for s, c in items:
            if c > out.get(s, 0):
                out[s] = c
    return list(out.items())


def seq_program(build_fn):
    nc = bass.Bass("TRN2", target_bir_lowering=False, debug=False)
    sq = Seq(nc)
    with ExitStack() as es:
        build_fn(nc, sq, es)
        sems = {}
        for s in sq.counts:
            sems[s] = es.enter_context(nc.semaphore(f"s_{s[0]}_{s[1]}"))
        block = es.enter_context(nc.Block())
        sq.build_into(block, sems)
        sq.finish(block, sems)
    return nc



def emit_ln_fm(nc, sq, X_view, TT, xn, xn_off, scr, rows, psA, psB, ones_col,
               ones_row, load_chunk):
    raise NotImplementedError



def build_l1_ffn():
    TE = T_CORE + 1
    NT, T = 4, 512

    def prog(nc, sq, es):
        x_ext = nc.dram_tensor("x_ext", [D_MODEL, TE], F32, kind="ExternalInput").ap()
        wrT = nc.dram_tensor("wrT", [D_MODEL, D_MODEL], F32, kind="ExternalInput").ap()
        wkT = nc.dram_tensor("wkT", [D_MODEL, D_MODEL], F32, kind="ExternalInput").ap()
        wvT = nc.dram_tensor("wvT", [D_MODEL, D_MODEL], F32, kind="ExternalInput").ap()
        mu_r = nc.dram_tensor("mu_r", [128, NCH], F32, kind="ExternalInput").ap()
        mu_k = nc.dram_tensor("mu_k", [128, NCH], F32, kind="ExternalInput").ap()
        x_out = nc.dram_tensor("x_out", [D_MODEL, T_CORE], F32,
                               kind="ExternalOutput").ap()

        xe = x_ext.rearrange("(c p) t -> p c t", p=128)
        xo = x_out.rearrange("(c p) t -> p c t", p=128)

        sb = lambda n, s, d=F32: es.enter_context(nc.sbuf_tensor(n, s, d))
        pt = lambda n, s, d=F32: es.enter_context(nc.psum_tensor(n, s, d))

        Xt = sb("Xt", [128, NCH * 512])
        xn = sb("xn", [128, NCH * TE], BF16)
        sq32 = sb("sq32", [128, 512])
        a_row = sb("a_row", [1, TE])
        b_row = sb("b_row", [1, TE])
        t_row = sb("t_row", [1, 512])
        wk_s = sb("wk_s", [128, NCH * D_MODEL], BF16)
        wr_s = sb("wr_s", [128, NCH * D_MODEL], BF16)
        wv_s = sb("wv_s", [128, NCH * D_MODEL], BF16)
        mur_s = sb("mur_s", [128, NCH])
        muk_s = sb("muk_s", [128, NCH])
        dscr = sb("dscr", [128, 512])
        oscr = sb("oscr", [128, 512])
        xk_s = sb("xk_s", [128, NCH * T], BF16)
        xr_s = sb("xr_s", [128, NCH * T], BF16)
        kact = sb("kact", [128, NCH * T], BF16)
        r_s = sb("r_s", [128, NCH * T])
        ones_col = sb("ones_col", [128, 1])
        ones_row = sb("ones_row", [1, 128])
        eps_row = sb("eps_row", [1, 1])

        ps_a = pt("ps_a", [1, 512])
        ps_b = pt("ps_b", [1, 512])
        ps_bc1 = pt("ps_bc1", [128, 512])
        ps_bc2 = pt("ps_bc2", [128, 512])
        ps_mm = pt("ps_mm", [128, 512])

        sq.step("gpsimd", lambda e: e.memset(ones_col[:], 1.0))
        sq.step("gpsimd", lambda e: e.memset(ones_row[:], 1.0))
        sq.step("gpsimd", lambda e: e.memset(eps_row[:], LN_EPS))
        for nm, dst in (("wkT", wk_s), ("wrT", wr_s), ("wvT", wv_s)):
            src = {"wkT": wkT, "wrT": wrT, "wvT": wvT}[nm]
            sq.dma(dst[:].rearrange("p (c d) -> p c d", c=NCH),
                   src.rearrange("(c p) d -> p c d", p=128), engine="gpsimd")
        sq.dma(mur_s[:], mu_r)
        sq.dma(muk_s[:], mu_k)

        c0 = 0
        while c0 < TE:
            w = min(512, TE - c0)
            sq.dma(Xt[:].rearrange("p (c t) -> p c t", c=NCH)[:, :, 0:w],
                   xe[:, :, c0:c0 + w],
                   **({"allow_slow_non_contiguous": True} if w < 8 else {}))
            sq.step("tensor", lambda e, w=w: [
                e.matmul(ps_a[:, 0:w], ones_col[:],
                         Xt[:, c * 512:c * 512 + w], start=(c == 0), stop=(c == NCH - 1))
                for c in range(NCH)])
            for c in range(NCH):
                sq.step("scalar", lambda e, c=c, w=w: e.activation(
                    sq32[:, 0:w], Xt[:, c * 512:c * 512 + w], AF.Square))
                sq.step("tensor", lambda e, c=c, w=w: e.matmul(
                    ps_b[:, 0:w], ones_col[:], sq32[:, 0:w],
                    start=(c == 0), stop=(c == NCH - 1)))
            sq.step("vector", lambda e, c0=c0, w=w: e.tensor_single_scalar(
                b_row[:, c0:c0 + w], ps_a[:, 0:w], 1.0 / D_MODEL, ALU.mult))
            sq.step("vector", lambda e, c0=c0, w=w: e.tensor_single_scalar(
                a_row[:, c0:c0 + w], ps_b[:, 0:w], 1.0 / D_MODEL, ALU.mult))
            sq.step("scalar", lambda e, c0=c0, w=w: e.activation(
                t_row[:, 0:w], b_row[:, c0:c0 + w], AF.Square))
            sq.step("vector", lambda e, c0=c0, w=w: e.tensor_sub(
                a_row[:, c0:c0 + w], a_row[:, c0:c0 + w], t_row[:, 0:w]))
            sq.step("scalar", lambda e, c0=c0, w=w: e.activation(
                a_row[:, c0:c0 + w], a_row[:, c0:c0 + w], AF.Ln, bias=eps_row[:]))
            sq.step("scalar", lambda e, c0=c0, w=w: e.activation(
                a_row[:, c0:c0 + w], a_row[:, c0:c0 + w], AF.Exp, scale=-0.5))
            sq.step("tensor", lambda e, c0=c0, w=w: e.matmul(
                ps_bc1[:, 0:w], ones_row[:], b_row[:, c0:c0 + w]))
            sq.step("tensor", lambda e, c0=c0, w=w: e.matmul(
                ps_bc2[:, 0:w], ones_row[:], a_row[:, c0:c0 + w]))
            for c in range(NCH):
                sq.step("vector", lambda e, c=c, w=w: e.tensor_sub(
                    dscr[:, 0:w], Xt[:, c * 512:c * 512 + w], ps_bc1[:, 0:w]))
                sq.step("vector", lambda e, c=c, c0=c0, w=w: e.tensor_mul(
                    xn[:, c * TE + c0:c * TE + c0 + w], dscr[:, 0:w], ps_bc2[:, 0:w]))
            c0 += w

        for i in range(NT):
            t0 = 1 + i * T
            for c in range(NCH):
                sq.step("vector", lambda e, c=c, t0=t0: e.tensor_sub(
                    dscr[:, 0:T], xn[:, c * TE + t0:c * TE + t0 + T],
                    xn[:, c * TE + t0 - 1:c * TE + t0 - 1 + T]))
                sq.step("vector", lambda e, c=c, t0=t0: e.scalar_tensor_tensor(
                    xk_s[:, c * T:(c + 1) * T], dscr[:, 0:T], muk_s[:, c:c + 1],
                    xn[:, c * TE + t0 - 1:c * TE + t0 - 1 + T], ALU.mult, ALU.add))
                sq.step("vector", lambda e, c=c, t0=t0: e.scalar_tensor_tensor(
                    xr_s[:, c * T:(c + 1) * T], dscr[:, 0:T], mur_s[:, c:c + 1],
                    xn[:, c * TE + t0 - 1:c * TE + t0 - 1 + T], ALU.mult, ALU.add))
            for d in range(NCH):
                sq.step("tensor", lambda e, d=d: [
                    e.matmul(ps_mm[:], wk_s[:, c * D_MODEL + d * 128:
                                            c * D_MODEL + (d + 1) * 128],
                             xk_s[:, c * T:(c + 1) * T],
                             start=(c == 0), stop=(c == NCH - 1))
                    for c in range(NCH)])
                sq.step("vector", lambda e, d=d: e.tensor_single_scalar(
                    dscr[:, 0:T], ps_mm[:], 0.0, ALU.max))
                sq.step("scalar", lambda e, d=d: e.activation(
                    kact[:, d * T:(d + 1) * T], dscr[:, 0:T], AF.Square))
            for d in range(NCH):
                sq.step("tensor", lambda e, d=d: [
                    e.matmul(ps_mm[:], wr_s[:, c * D_MODEL + d * 128:
                                            c * D_MODEL + (d + 1) * 128],
                             xr_s[:, c * T:(c + 1) * T],
                             start=(c == 0), stop=(c == NCH - 1))
                    for c in range(NCH)])
                sq.step("scalar", lambda e, d=d: e.activation(
                    r_s[:, d * T:(d + 1) * T], ps_mm[:], AF.Sigmoid))
            for d in range(NCH):
                sq.step("tensor", lambda e, d=d: [
                    e.matmul(ps_mm[:], wv_s[:, c * D_MODEL + d * 128:
                                            c * D_MODEL + (d + 1) * 128],
                             kact[:, c * T:(c + 1) * T],
                             start=(c == 0), stop=(c == NCH - 1))
                    for c in range(NCH)])
                sq.step("vector", lambda e, d=d: e.tensor_mul(
                    oscr[:, 0:T], r_s[:, d * T:(d + 1) * T], ps_mm[:]))
                sq.dma(Xt[:, 0:T], xe[:, d, t0:t0 + T])
                sq.step("vector", lambda e, d=d: e.tensor_add(
                    oscr[:, 0:T], oscr[:, 0:T], Xt[:, 0:T]))
                sq.dma(xo[:, d, i * T:(i + 1) * T], oscr[:, 0:T])

    return seq_program(prog)


def host_l1_ffn_inputs(x, weights, core):
    b, h = core // 2, core % 2
    sl = core_slice_tokens(x, core)
    if h == 0:
        halo = np.zeros((1, D_MODEL), np.float32)
    else:
        halo = np.asarray(x[b, h * 2048 - 1:h * 2048])
    x_ext = fmT(np.concatenate([halo, sl], axis=0))
    return {
        "x_ext": x_ext,
        "wrT": fmT(weights["W_r"]),
        "wkT": fmT(weights["W_k"]),
        "wvT": fmT(weights["W_v"]),
        "mu_r": pack_vec(weights["mu_r"]),
        "mu_k": pack_vec(weights["mu_k"]),
    }


def unpack_fm_out(res_list, key="x_out"):
    out = np.empty((4, 4096, 1024), np.float32)
    for core in range(N_CORES):
        b, h = core // 2, core % 2
        out[b, h * 2048:(h + 1) * 2048] = res_list[core][key].T
    return out



def build_l0_ffn():
    NEXP = 8
    HID = 4096
    NHT = HID // 128
    NHG = 4
    NT, T = 4, 512

    def prog(nc, sq, es):
        x_fm = nc.dram_tensor("x_fm", [D_MODEL, T_CORE], F32, kind="ExternalInput").ap()
        router_wT = nc.dram_tensor("router_wT", [D_MODEL, 8], F32, kind="ExternalInput").ap()
        w_gate = nc.dram_tensor("w_gate", [NEXP, D_MODEL, HID], F32, kind="ExternalInput").ap()
        w_up = nc.dram_tensor("w_up", [NEXP, D_MODEL, HID], F32, kind="ExternalInput").ap()
        w_down = nc.dram_tensor("w_down", [NEXP, HID, D_MODEL], F32, kind="ExternalInput").ap()
        x_out = nc.dram_tensor("x_out", [D_MODEL, T_CORE], F32, kind="ExternalOutput").ap()
        cmb_dram = nc.dram_tensor("cmb_dram", [8, T_CORE], F32, kind="Internal").ap()

        xv = x_fm.rearrange("(c p) t -> p c t", p=128)
        xo = x_out.rearrange("(c p) t -> p c t", p=128)
        wg_v = w_gate.rearrange("e (c p) h -> e p c h", p=128)
        wu_v = w_up.rearrange("e (c p) h -> e p c h", p=128)
        wd_v = w_down.rearrange("e (ht p) d -> e p ht d", p=128)
        rw_v = router_wT.rearrange("(c p) e -> p c e", p=128)

        sb = lambda n, s, d=F32: es.enter_context(nc.sbuf_tensor(n, s, d))
        pt = lambda n, s, d=F32: es.enter_context(nc.psum_tensor(n, s, d))

        Xt = sb("Xt", [128, NCH * 512])
        xn = sb("xn", [128, NCH * T_CORE], BF16)
        sq32 = sb("sq32", [128, 512])
        a_row = sb("a_row", [1, 512])
        b_row = sb("b_row", [1, 512])
        t_row = sb("t_row", [1, 512])
        ones_col = sb("ones_col", [128, 1])
        ones_row = sb("ones_row", [1, 128])
        eps_row = sb("eps_row", [1, 1])
        rw_s = sb("rw_s", [128, NCH * 8], BF16)
        logit = sb("logit", [128, 8])
        m1 = sb("m1", [128, 1])
        m2 = sb("m2", [128, 1])
        nm1 = sb("nm1", [128, 1])
        e2 = sb("e2", [128, 1])
        rden = sb("rden", [128, 1])
        gemsk = sb("gemsk", [128, 8])
        expv = sb("expv", [128, 8])
        cmb = sb("cmb", [128, 8])
        neg_inf8 = sb("neg_inf8", [128, 8])
        msk8 = sb("msk8", [128, 8], U8)
        cmb_row1 = sb("cmb_row1", [1, 128])
        cmb_row = sb("cmb_row", [1, T_CORE])
        cmb_bc = sb("cmb_bc", [128, T_CORE])
        identw = sb("identw", [128, 128], F32)
        wgs = sb("wgs", [128, NCH * 1024], BF16)
        wus = sb("wus", [128, NCH * 1024], BF16)
        wds = sb("wds", [128, 8 * 1024], BF16)
        hact = sb("hact", [128, 8 * 512], BF16)
        silu_scr = sb("silu_scr", [128, 512])
        h_scr = sb("h_scr", [128, 512])
        mixed = sb("mixed", [128, NCH * T_CORE])
        oscr = sb("oscr", [128, 512])

        ps_a = pt("ps_a", [1, 512])
        ps_b = pt("ps_b", [1, 512])
        ps_bc1 = pt("ps_bc1", [128, 512])
        ps_bc2 = pt("ps_bc2", [128, 512])
        ps_g = pt("ps_g", [128, 512])
        ps_u = pt("ps_u", [128, 512])
        ps_dn = pt("ps_dn", [128, 512])
        ps_r = pt("ps_r", [128, 128])

        sq.step("gpsimd", lambda e: e.memset(ones_col[:], 1.0))
        sq.step("gpsimd", lambda e: e.memset(ones_row[:], 1.0))
        sq.step("gpsimd", lambda e: e.memset(eps_row[:], LN_EPS))
        sq.step("gpsimd", lambda e: e.memset(neg_inf8[:], -1e30))
        sq.step("gpsimd", lambda e: e.memset(identw[:], 1.0))
        sq.step("gpsimd", lambda e: e.affine_select(
            identw[:], identw[:], pattern=[[1, 128]], compare_op=ALU.is_equal,
            fill=0.0, base=0, channel_multiplier=-1))
        identb = sb("identb", [128, 128], BF16)
        sq.step("vector", lambda e: e.tensor_copy(identb[:], identw[:]))
        sq.dma(rw_s[:].rearrange("p (c e) -> p c e", c=NCH), rw_v, engine="gpsimd")

        for it in range(4):
            c0, w = it * 512, 512
            sq.dma(Xt[:].rearrange("p (c t) -> p c t", c=NCH), xv[:, :, c0:c0 + w])
            sq.step("tensor", lambda e, w=w: [
                e.matmul(ps_a[:, 0:w], ones_col[:], Xt[:, c * 512:c * 512 + w],
                         start=(c == 0), stop=(c == NCH - 1)) for c in range(NCH)])
            for c in range(NCH):
                sq.step("scalar", lambda e, c=c, w=w: e.activation(
                    sq32[:, 0:w], Xt[:, c * 512:c * 512 + w], AF.Square))
                sq.step("tensor", lambda e, c=c, w=w: e.matmul(
                    ps_b[:, 0:w], ones_col[:], sq32[:, 0:w],
                    start=(c == 0), stop=(c == NCH - 1)))
            sq.step("vector", lambda e, c0=c0, w=w: e.tensor_single_scalar(
                b_row[:, 0:w], ps_a[:, 0:w], 1.0 / D_MODEL, ALU.mult))
            sq.step("vector", lambda e, c0=c0, w=w: e.tensor_single_scalar(
                a_row[:, 0:w], ps_b[:, 0:w], 1.0 / D_MODEL, ALU.mult))
            sq.step("scalar", lambda e, c0=c0, w=w: e.activation(
                t_row[:, 0:w], b_row[:, 0:w], AF.Square))
            sq.step("vector", lambda e, c0=c0, w=w: e.tensor_sub(
                a_row[:, 0:w], a_row[:, 0:w], t_row[:, 0:w]))
            sq.step("scalar", lambda e, c0=c0, w=w: e.activation(
                a_row[:, 0:w], a_row[:, 0:w], AF.Ln, bias=eps_row[:]))
            sq.step("scalar", lambda e, c0=c0, w=w: e.activation(
                a_row[:, 0:w], a_row[:, 0:w], AF.Exp, scale=-0.5))
            sq.step("tensor", lambda e, c0=c0, w=w: e.matmul(
                ps_bc1[:, 0:w], ones_row[:], b_row[:, 0:w]))
            sq.step("tensor", lambda e, c0=c0, w=w: e.matmul(
                ps_bc2[:, 0:w], ones_row[:], a_row[:, 0:w]))
            for c in range(NCH):
                sq.step("vector", lambda e, c=c, w=w: e.tensor_sub(
                    sq32[:, 0:w], Xt[:, c * 512:c * 512 + w], ps_bc1[:, 0:w]))
                sq.step("vector", lambda e, c=c, c0=c0, w=w: e.tensor_mul(
                    xn[:, c * T_CORE + c0:c * T_CORE + c0 + w], sq32[:, 0:w],
                    ps_bc2[:, 0:w]))

        for q in range(T_CORE // 128):
            sq.step("tensor", lambda e, q=q: [
                e.matmul(ps_r[:, 0:8],
                         xn[:, c * T_CORE + q * 128:c * T_CORE + (q + 1) * 128],
                         rw_s[:, c * 8:(c + 1) * 8],
                         start=(c == 0), stop=(c == NCH - 1)) for c in range(NCH)])
            sq.step("vector", lambda e: e.tensor_copy(logit[:], ps_r[:, 0:8]))
            sq.step("vector", lambda e: e.reduce_max(m1[:], logit[:, 0:7], axis=AX.X))
            sq.step("vector", lambda e: e.tensor_scalar(
                msk8[:, 0:7], logit[:, 0:7], m1[:], None, ALU.is_ge))
            sq.step("vector", lambda e: e.select(
                sq32[:, 0:7], msk8[:, 0:7], neg_inf8[:, 0:7], logit[:, 0:7]))
            sq.step("vector", lambda e: e.reduce_max(m2[:], sq32[:, 0:7], axis=AX.X))
            sq.step("vector", lambda e: e.tensor_single_scalar(
                nm1[:], m1[:], -1.0, ALU.mult))
            sq.step("scalar", lambda e: e.activation(e2[:], m2[:], AF.Exp, bias=nm1[:]))
            sq.step("vector", lambda e: e.tensor_single_scalar(
                e2[:], e2[:], 1.0, ALU.add))
            sq.step("vector", lambda e: e.reciprocal(rden[:], e2[:]))
            sq.step("vector", lambda e: e.tensor_scalar(
                gemsk[:, 0:7], logit[:, 0:7], m2[:], None, ALU.is_ge))
            sq.step("scalar", lambda e: e.activation(
                expv[:, 0:7], logit[:, 0:7], AF.Exp, bias=nm1[:]))
            sq.step("vector", lambda e: e.tensor_mul(
                cmb[:, 0:7], expv[:, 0:7], gemsk[:, 0:7]))
            sq.step("vector", lambda e: e.tensor_scalar(
                cmb[:, 0:7], cmb[:, 0:7], rden[:], None, ALU.mult))
            sq.step("gpsimd", lambda e: e.memset(cmb[:, 7:8], 1.0))
            for ee in range(NEXP):
                sq.step("tensor", lambda e, ee=ee: e.transpose(
                    ps_r[0:1, 0:128], cmb[:, ee:ee + 1], identw[:]))
                sq.step("vector", lambda e: e.tensor_copy(
                    cmb_row1[:], ps_r[0:1, 0:128]))
                sq.dma(cmb_dram[ee:ee + 1, q * 128:(q + 1) * 128], cmb_row1[:])

        sq.step("gpsimd", lambda e: e.memset(mixed[:], 0.0))
        for ee in range(NEXP):
            sq.dma(cmb_row[:], cmb_dram[ee:ee + 1, :])
            for tt in range(NT):
                sq.step("tensor", lambda e, ee=ee, tt=tt: e.matmul(
                    ps_bc1[:], ones_row[:],
                    cmb_row[:, tt * T:(tt + 1) * T]))
                sq.step("vector", lambda e, tt=tt: e.tensor_copy(
                    cmb_bc[:, tt * T:(tt + 1) * T], ps_bc1[:]))
            for hg in range(NHG):
                sq.dma(wgs[:].rearrange("p (c h) -> p c h", c=NCH),
                       wg_v[ee, :, :, hg * 1024:(hg + 1) * 1024], engine="gpsimd")
                sq.dma(wus[:].rearrange("p (c h) -> p c h", c=NCH),
                       wu_v[ee, :, :, hg * 1024:(hg + 1) * 1024], engine="gpsimd")
                sq.dma(wds[:].rearrange("p (ht d) -> p ht d", ht=8),
                       wd_v[ee, :, hg * 8:(hg + 1) * 8, :], engine="gpsimd")
                for tt in range(NT):
                    for ht in range(8):
                        sq.step("tensor", lambda e, ht=ht, tt=tt: [
                            e.matmul(ps_g[:],
                                     wgs[:, c * 1024 + ht * 128:c * 1024 + (ht + 1) * 128],
                                     xn[:, c * T_CORE + tt * T:c * T_CORE + tt * T + T],
                                     start=(c == 0), stop=(c == NCH - 1))
                            for c in range(NCH)])
                        sq.step("tensor", lambda e, ht=ht, tt=tt: [
                            e.matmul(ps_u[:],
                                     wus[:, c * 1024 + ht * 128:c * 1024 + (ht + 1) * 128],
                                     xn[:, c * T_CORE + tt * T:c * T_CORE + tt * T + T],
                                     start=(c == 0), stop=(c == NCH - 1))
                            for c in range(NCH)])
                        sq.step("scalar", lambda e: e.activation(
                            silu_scr[:], ps_g[:], AF.Silu))
                        sq.step("vector", lambda e: e.tensor_mul(
                            h_scr[:], silu_scr[:], ps_u[:]))
                        sq.step("vector", lambda e, ht=ht, tt=tt: e.tensor_mul(
                            hact[:, ht * 512:(ht + 1) * 512], h_scr[:],
                            cmb_bc[:, tt * T:(tt + 1) * T]))
                    for dm in range(NCH):
                        sq.step("tensor", lambda e, dm=dm: [
                            e.matmul(ps_dn[:],
                                     wds[:, ht * 1024 + dm * 128:ht * 1024 + (dm + 1) * 128],
                                     hact[:, ht * 512:(ht + 1) * 512],
                                     start=(ht == 0), stop=(ht == 7))
                            for ht in range(8)])
                        sq.step("vector", lambda e, dm=dm, tt=tt: e.tensor_add(
                            mixed[:, dm * T_CORE + tt * T:dm * T_CORE + tt * T + T],
                            mixed[:, dm * T_CORE + tt * T:dm * T_CORE + tt * T + T],
                            ps_dn[:]))

        for tt in range(NT):
            sq.dma(Xt[:].rearrange("p (c t) -> p c t", c=NCH), xv[:, :, tt * T:(tt + 1) * T])
            for c in range(NCH):
                sq.step("vector", lambda e, c=c, tt=tt: e.tensor_add(
                    oscr[:], mixed[:, c * T_CORE + tt * T:c * T_CORE + tt * T + T],
                    Xt[:, c * 512:(c + 1) * 512]))
                sq.dma(xo[:, c, tt * T:(tt + 1) * T], oscr[:])

    return seq_program(prog)


def host_l0_ffn_inputs(x, weights, core):
    sl = core_slice_tokens(x, core)
    rw = np.full((8, D_MODEL), -1e30, np.float32)
    rw[0:7] = np.asarray(weights["router_w"], np.float32)
    wg = np.concatenate([np.asarray(weights["w_gate"], np.float32),
                         np.asarray(weights["sw_gate"], np.float32)], axis=0)
    wu = np.concatenate([np.asarray(weights["w_up"], np.float32),
                         np.asarray(weights["sw_up"], np.float32)], axis=0)
    wd = np.concatenate([np.asarray(weights["w_down"], np.float32),
                         np.asarray(weights["sw_down"], np.float32)], axis=0)
    return {
        "x_fm": fmT(sl),
        "router_wT": fmT(rw),
        "w_gate": np.ascontiguousarray(wg),
        "w_up": np.ascontiguousarray(wu),
        "w_down": np.ascontiguousarray(wd),
    }



D_INNER = 2048
NEIN = D_INNER // 128
D_STATE = 16


def build_mamba_pre():
    TE = T_CORE + 3
    COLT = [(0, 512), (512, 512), (1024, 512), (1536, 512), (2048, 3)]
    TT4 = [(0, 512), (512, 512), (1024, 512), (1536, 512)]

    def prog(nc, sq, es):
        x_ext = nc.dram_tensor("x_ext", [D_MODEL, TE], F32, kind="ExternalInput").ap()
        in_projT = nc.dram_tensor("in_projT", [D_MODEL, 2 * D_INNER], F32, kind="ExternalInput").ap()
        x_projT = nc.dram_tensor("x_projT", [D_INNER, 2080], F32, kind="ExternalInput").ap()
        dt_projT = nc.dram_tensor("dt_projT", [D_INNER, D_INNER], F32, kind="ExternalInput").ap()
        conv_w = nc.dram_tensor("conv_w", [128, 64], F32, kind="ExternalInput").ap()
        conv_b = nc.dram_tensor("conv_b", [128, NEIN], F32, kind="ExternalInput").ap()
        dt_proj_b = nc.dram_tensor("dt_proj_b", [128, NEIN], F32, kind="ExternalInput").ap()
        dt_o = nc.dram_tensor("dt_o", [D_INNER, T_CORE], F32, kind="ExternalOutput").ap()
        xin_o = nc.dram_tensor("xin_o", [D_INNER, T_CORE], F32, kind="ExternalOutput").ap()
        zs_o = nc.dram_tensor("zs_o", [D_INNER, T_CORE], F32, kind="ExternalOutput").ap()
        B_o = nc.dram_tensor("B_o", [D_STATE, T_CORE], F32, kind="ExternalOutput").ap()
        C_o = nc.dram_tensor("C_o", [D_STATE, T_CORE], F32, kind="ExternalOutput").ap()

        xe = x_ext.rearrange("(c p) t -> p c t", p=128)
        ipv = in_projT.rearrange("(c p) e -> p c e", p=128)
        xpv = x_projT.rearrange("(c p) e -> p c e", p=128)
        dpv = dt_projT.rearrange("(c p) e -> p c e", p=128)
        dt_v = dt_o.rearrange("(c p) t -> p c t", p=128)
        xin_v = xin_o.rearrange("(c p) t -> p c t", p=128)
        zs_v = zs_o.rearrange("(c p) t -> p c t", p=128)

        sb = lambda n, s, d=F32: es.enter_context(nc.sbuf_tensor(n, s, d))
        pt = lambda n, s, d=F32: es.enter_context(nc.psum_tensor(n, s, d))

        ones_col = sb("ones_col", [128, 1])
        ones_row = sb("ones_row", [1, 128])
        eps_row = sb("eps_row", [1, 1])
        Xt = sb("Xt", [128, NCH * 512])
        sq32 = sb("sq32", [128, 512])
        scr2 = sb("scr2", [128, 512])
        a_row = sb("a_row", [1, 512])
        b_row = sb("b_row", [1, 512])
        t_row = sb("t_row", [1, 512])
        xn = sb("xn", [128, NCH * TE], BF16)
        wsl = sb("wsl", [128, NCH * 128], BF16)
        wsl2 = sb("wsl2", [128, NEIN * 128], BF16)
        xraw = sb("xraw", [128, TE], BF16)
        xconv = sb("xconv", [128, NEIN * T_CORE], BF16)
        din = sb("din", [128, NEIN * T_CORE], BF16)
        convw_s = sb("convw_s", [128, 64])
        convb_s = sb("convb_s", [128, NEIN])
        dtb_s = sb("dtb_s", [128, NEIN])
        bc_scr = sb("bc_scr", [16, 512])

        ps_a = pt("ps_a", [1, 512])
        ps_b = pt("ps_b", [1, 512])
        ps_bc1 = pt("ps_bc1", [128, 512])
        ps_bc2 = pt("ps_bc2", [128, 512])
        ps_mm = pt("ps_mm", [128, 512])
        ps_sm = pt("ps_sm", [16, 512])

        sq.step("gpsimd", lambda e: e.memset(ones_col[:], 1.0))
        sq.step("gpsimd", lambda e: e.memset(ones_row[:], 1.0))
        sq.step("gpsimd", lambda e: e.memset(eps_row[:], LN_EPS))
        sq.dma(convw_s[:], conv_w)
        sq.dma(convb_s[:], conv_b)
        sq.dma(dtb_s[:], dt_proj_b)

        for (c0, w) in COLT:
            slow = {"allow_slow_non_contiguous": True} if w < 8 else {}
            sq.dma(Xt[:].rearrange("p (c t) -> p c t", c=NCH)[:, :, 0:w],
                   xe[:, :, c0:c0 + w], **slow)
            sq.step("tensor", lambda e, w=w: [
                e.matmul(ps_a[:, 0:w], ones_col[:], Xt[:, c * 512:c * 512 + w],
                         start=(c == 0), stop=(c == NCH - 1)) for c in range(NCH)])
            for c in range(NCH):
                sq.step("scalar", lambda e, c=c, w=w: e.activation(
                    sq32[:, 0:w], Xt[:, c * 512:c * 512 + w], AF.Square))
                sq.step("tensor", lambda e, c=c, w=w: e.matmul(
                    ps_b[:, 0:w], ones_col[:], sq32[:, 0:w],
                    start=(c == 0), stop=(c == NCH - 1)))
            sq.step("vector", lambda e, w=w: e.tensor_single_scalar(
                b_row[:, 0:w], ps_a[:, 0:w], 1.0 / D_MODEL, ALU.mult))
            sq.step("vector", lambda e, w=w: e.tensor_single_scalar(
                a_row[:, 0:w], ps_b[:, 0:w], 1.0 / D_MODEL, ALU.mult))
            sq.step("scalar", lambda e, w=w: e.activation(
                t_row[:, 0:w], b_row[:, 0:w], AF.Square))
            sq.step("vector", lambda e, w=w: e.tensor_sub(
                a_row[:, 0:w], a_row[:, 0:w], t_row[:, 0:w]))
            sq.step("scalar", lambda e, w=w: e.activation(
                a_row[:, 0:w], a_row[:, 0:w], AF.Ln, bias=eps_row[:]))
            sq.step("scalar", lambda e, w=w: e.activation(
                a_row[:, 0:w], a_row[:, 0:w], AF.Exp, scale=-0.5))
            sq.step("tensor", lambda e, w=w: e.matmul(
                ps_bc1[:, 0:w], ones_row[:], b_row[:, 0:w]))
            sq.step("tensor", lambda e, w=w: e.matmul(
                ps_bc2[:, 0:w], ones_row[:], a_row[:, 0:w]))
            for c in range(NCH):
                sq.step("vector", lambda e, c=c, w=w: e.tensor_sub(
                    sq32[:, 0:w], Xt[:, c * 512:c * 512 + w], ps_bc1[:, 0:w]))
                sq.step("vector", lambda e, c=c, c0=c0, w=w: e.tensor_mul(
                    xn[:, c * TE + c0:c * TE + c0 + w], sq32[:, 0:w], ps_bc2[:, 0:w]))

        for et in range(NEIN):
            sq.dma(wsl[:].rearrange("p (c d) -> p c d", c=NCH),
                   ipv[:, :, et * 128:(et + 1) * 128], engine="gpsimd")
            for (c0, w) in COLT:
                sq.step("tensor", lambda e, et=et, c0=c0, w=w: [
                    e.matmul(ps_mm[:, 0:w], wsl[:, c * 128:(c + 1) * 128],
                             xn[:, c * TE + c0:c * TE + c0 + w],
                             start=(c == 0), stop=(c == NCH - 1)) for c in range(NCH)])
                sq.step("vector", lambda e, c0=c0, w=w: e.tensor_copy(
                    xraw[:, c0:c0 + w], ps_mm[:, 0:w]))
            for (t0, w) in TT4:
                sq.step("vector", lambda e, et=et, t0=t0, w=w: e.tensor_scalar(
                    sq32[:, 0:w], xraw[:, t0:t0 + w], convw_s[:, et * 4:et * 4 + 1],
                    None, ALU.mult))
                for j in range(1, 4):
                    sq.step("vector", lambda e, et=et, t0=t0, w=w, j=j:
                        e.scalar_tensor_tensor(
                            sq32[:, 0:w], xraw[:, t0 + j:t0 + j + w],
                            convw_s[:, et * 4 + j:et * 4 + j + 1],
                            sq32[:, 0:w], ALU.mult, ALU.add))
                sq.step("scalar", lambda e, et=et, t0=t0, w=w: e.activation(
                    xconv[:, et * T_CORE + t0:et * T_CORE + t0 + w], sq32[:, 0:w],
                    AF.Silu, bias=convb_s[:, et:et + 1]))
                sq.step("vector", lambda e, et=et, t0=t0, w=w: e.tensor_copy(
                    scr2[:, 0:w], xconv[:, et * T_CORE + t0:et * T_CORE + t0 + w]))
                sq.dma(xin_v[:, et, t0:t0 + w], scr2[:, 0:w])

        for et in range(NEIN):
            sq.dma(wsl[:].rearrange("p (c d) -> p c d", c=NCH),
                   ipv[:, :, D_INNER + et * 128:D_INNER + (et + 1) * 128],
                   engine="gpsimd")
            for (t0, w) in TT4:
                sq.step("tensor", lambda e, t0=t0, w=w: [
                    e.matmul(ps_mm[:, 0:w], wsl[:, c * 128:(c + 1) * 128],
                             xn[:, c * TE + 3 + t0:c * TE + 3 + t0 + w],
                             start=(c == 0), stop=(c == NCH - 1)) for c in range(NCH)])
                sq.step("scalar", lambda e, w=w: e.activation(
                    scr2[:, 0:w], ps_mm[:, 0:w], AF.Silu))
                sq.dma(zs_v[:, et, t0:t0 + w], scr2[:, 0:w])

        for et in range(NEIN):
            sq.dma(wsl2[:].rearrange("p (c d) -> p c d", c=NEIN),
                   xpv[:, :, et * 128:(et + 1) * 128], engine="gpsimd")
            for (t0, w) in TT4:
                sq.step("tensor", lambda e, t0=t0, w=w: [
                    e.matmul(ps_mm[:, 0:w], wsl2[:, c * 128:(c + 1) * 128],
                             xconv[:, c * T_CORE + t0:c * T_CORE + t0 + w],
                             start=(c == 0), stop=(c == NEIN - 1)) for c in range(NEIN)])
                sq.step("vector", lambda e, et=et, t0=t0, w=w: e.tensor_copy(
                    din[:, et * T_CORE + t0:et * T_CORE + t0 + w], ps_mm[:, 0:w]))
        for bi, out_t in ((0, B_o), (1, C_o)):
            sq.dma(wsl2[:, 0:NEIN * 16].rearrange("p (c d) -> p c d", c=NEIN),
                   xpv[:, :, D_INNER + bi * 16:D_INNER + (bi + 1) * 16],
                   engine="gpsimd")
            for (t0, w) in TT4:
                sq.step("tensor", lambda e, t0=t0, w=w: [
                    e.matmul(ps_sm[:, 0:w], wsl2[:, c * 16:(c + 1) * 16],
                             xconv[:, c * T_CORE + t0:c * T_CORE + t0 + w],
                             start=(c == 0), stop=(c == NEIN - 1)) for c in range(NEIN)])
                sq.step("vector", lambda e, w=w: e.tensor_copy(
                    bc_scr[:, 0:w], ps_sm[:, 0:w]))
                sq.dma(out_t[:, t0:t0 + w], bc_scr[:, 0:w])

        for et in range(NEIN):
            sq.dma(wsl2[:].rearrange("p (c d) -> p c d", c=NEIN),
                   dpv[:, :, et * 128:(et + 1) * 128], engine="gpsimd")
            for (t0, w) in TT4:
                sq.step("tensor", lambda e, t0=t0, w=w: [
                    e.matmul(ps_mm[:, 0:w], wsl2[:, c * 128:(c + 1) * 128],
                             din[:, c * T_CORE + t0:c * T_CORE + t0 + w],
                             start=(c == 0), stop=(c == NEIN - 1)) for c in range(NEIN)])
                sq.step("vector", lambda e, et=et, w=w: e.tensor_scalar(
                    sq32[:, 0:w], ps_mm[:, 0:w], dtb_s[:, et:et + 1], None, ALU.add))
                sq.step("scalar", lambda e, w=w: e.activation(
                    scr2[:, 0:w], sq32[:, 0:w], AF.Abs))
                sq.step("scalar", lambda e, w=w: e.activation(
                    scr2[:, 0:w], scr2[:, 0:w], AF.Exp, scale=-1.0))
                sq.step("scalar", lambda e, w=w: e.activation(
                    scr2[:, 0:w], scr2[:, 0:w], AF.Ln, bias=1.0))
                sq.step("vector", lambda e, w=w: e.tensor_single_scalar(
                    sq32[:, 0:w], sq32[:, 0:w], 0.0, ALU.max))
                sq.step("vector", lambda e, w=w: e.tensor_add(
                    sq32[:, 0:w], sq32[:, 0:w], scr2[:, 0:w]))
                sq.dma(dt_v[:, et, t0:t0 + w], sq32[:, 0:w])

    return seq_program(prog)


def build_mamba_scan():
    L = 4096
    NDT = 8
    NTC = 8

    def prog(nc, sq, es):
        dt_i = nc.dram_tensor("dt_i", [1024, L], F32, kind="ExternalInput").ap()
        xin_i = nc.dram_tensor("xin_i", [1024, L], F32, kind="ExternalInput").ap()
        B_i = nc.dram_tensor("B_i", [D_STATE, L], F32, kind="ExternalInput").ap()
        C_i = nc.dram_tensor("C_i", [D_STATE, L], F32, kind="ExternalInput").ap()
        Alog_i = nc.dram_tensor("Alog_i", [128, NDT * D_STATE], F32, kind="ExternalInput").ap()
        D_i = nc.dram_tensor("D_i", [128, NDT], F32, kind="ExternalInput").ap()
        y_o = nc.dram_tensor("y_o", [1024, L], F32, kind="ExternalOutput").ap()

        dt_vv = dt_i.rearrange("(c p) t -> p c t", p=128)
        xin_vv = xin_i.rearrange("(c p) t -> p c t", p=128)
        y_vv = y_o.rearrange("(c p) t -> p c t", p=128)

        sb = lambda n, s, d=F32: es.enter_context(nc.sbuf_tensor(n, s, d))
        pt = lambda n, s, d=F32: es.enter_context(nc.psum_tensor(n, s, d))

        ones_row = sb("ones_row", [1, 128])
        dt_s = sb("dt_s", [128, NDT * 512])
        xin_s = sb("xin_s", [128, NDT * 512])
        dtx = sb("dtx", [128, NDT * 512])
        y_s = sb("y_s", [128, NDT * 512])
        A_s = sb("A_s", [128, NDT * D_STATE])
        D_s = sb("D_s", [128, NDT])
        h_st = sb("h_st", [128, NDT * D_STATE])
        B_row = sb("B_row", [1, 512])
        C_row = sb("C_row", [1, 512])
        Brep = sb("Brep", [128, 512])
        Crep = sb("Crep", [128, 512])
        dA = sb("dA", [128, 512])
        u_s = sb("u_s", [128, 512])
        hs = sb("hs", [128, 512])
        t1 = sb("t1", [128, 512])

        ps_b1 = pt("ps_b1", [128, 512])
        ps_b2 = pt("ps_b2", [128, 512])

        sq.step("gpsimd", lambda e: e.memset(ones_row[:], 1.0))
        sq.step("gpsimd", lambda e: e.memset(h_st[:], 0.0))
        sq.dma(A_s[:], Alog_i)
        sq.dma(D_s[:], D_i)
        sq.step("scalar", lambda e: e.activation(A_s[:], A_s[:], AF.Exp))
        sq.step("vector", lambda e: e.tensor_single_scalar(A_s[:], A_s[:], -1.0, ALU.mult))

        for tc in range(NTC):
            t0 = tc * 512
            sq.dma(dt_s[:].rearrange("p (c t) -> p c t", c=NDT), dt_vv[:, :, t0:t0 + 512])
            sq.dma(xin_s[:].rearrange("p (c t) -> p c t", c=NDT), xin_vv[:, :, t0:t0 + 512])
            for d in range(NDT):
                sq.step("vector", lambda e, d=d: e.tensor_mul(
                    dtx[:, d * 512:(d + 1) * 512], dt_s[:, d * 512:(d + 1) * 512],
                    xin_s[:, d * 512:(d + 1) * 512]))
            sq.step("gpsimd", lambda e: e.memset(y_s[:], 0.0))
            for s in range(D_STATE):
                sq.dma(B_row[:], B_i[s:s + 1, t0:t0 + 512])
                sq.dma(C_row[:], C_i[s:s + 1, t0:t0 + 512])
                sq.step("tensor", lambda e: e.matmul(ps_b1[:], ones_row[:], B_row[:]))
                sq.step("tensor", lambda e: e.matmul(ps_b2[:], ones_row[:], C_row[:]))
                sq.step("vector", lambda e: e.tensor_copy(Brep[:], ps_b1[:]))
                sq.step("vector", lambda e: e.tensor_copy(Crep[:], ps_b2[:]))
                for d in range(NDT):
                    col = d * D_STATE + s
                    sq.step("scalar", lambda e, d=d, col=col: e.activation(
                        dA[:], dt_s[:, d * 512:(d + 1) * 512], AF.Exp,
                        scale=A_s[:, col:col + 1]))
                    sq.step("vector", lambda e, d=d: e.tensor_mul(
                        u_s[:], dtx[:, d * 512:(d + 1) * 512], Brep[:]))
                    sq.step("vector", lambda e, col=col: e.tensor_tensor_scan(
                        hs[:], dA[:], u_s[:], h_st[:, col:col + 1],
                        ALU.mult, ALU.add))
                    sq.step("vector", lambda e, col=col: e.tensor_copy(
                        h_st[:, col:col + 1], hs[:, 511:512]))
                    sq.step("vector", lambda e: e.tensor_mul(t1[:], hs[:], Crep[:]))
                    sq.step("vector", lambda e, d=d: e.tensor_add(
                        y_s[:, d * 512:(d + 1) * 512],
                        y_s[:, d * 512:(d + 1) * 512], t1[:]))
            for d in range(NDT):
                sq.step("vector", lambda e, d=d: e.scalar_tensor_tensor(
                    y_s[:, d * 512:(d + 1) * 512], xin_s[:, d * 512:(d + 1) * 512],
                    D_s[:, d:d + 1], y_s[:, d * 512:(d + 1) * 512],
                    ALU.mult, ALU.add))
                sq.dma(y_vv[:, d, t0:t0 + 512], y_s[:, d * 512:(d + 1) * 512])

    return seq_program(prog)


def build_mamba_post():
    NT, T = 4, 512

    def prog(nc, sq, es):
        x_fm = nc.dram_tensor("x_fm", [D_MODEL, T_CORE], F32, kind="ExternalInput").ap()
        y_fm = nc.dram_tensor("y_fm", [D_INNER, T_CORE], F32, kind="ExternalInput").ap()
        zs_fm = nc.dram_tensor("zs_fm", [D_INNER, T_CORE], F32, kind="ExternalInput").ap()
        opT = nc.dram_tensor("opT", [D_INNER, D_MODEL], F32, kind="ExternalInput").ap()
        x_out = nc.dram_tensor("x_out", [D_MODEL, T_CORE], F32, kind="ExternalOutput").ap()

        xv = x_fm.rearrange("(c p) t -> p c t", p=128)
        yv = y_fm.rearrange("(c p) t -> p c t", p=128)
        zv = zs_fm.rearrange("(c p) t -> p c t", p=128)
        ov = opT.rearrange("(c p) d -> p c d", p=128)
        xo = x_out.rearrange("(c p) t -> p c t", p=128)

        sb = lambda n, s, d=F32: es.enter_context(nc.sbuf_tensor(n, s, d))
        pt = lambda n, s, d=F32: es.enter_context(nc.psum_tensor(n, s, d))

        w_s = sb("w_s", [128, NEIN * D_MODEL], BF16)
        y_t = sb("y_t", [128, NEIN * T])
        z_t = sb("z_t", [128, NEIN * T])
        g_t = sb("g_t", [128, NEIN * T], BF16)
        Xt = sb("Xt", [128, NCH * T])
        oscr = sb("oscr", [128, T])
        ps_mm = pt("ps_mm", [128, 512])

        sq.dma(w_s[:].rearrange("p (c d) -> p c d", c=NEIN), ov, engine="gpsimd")
        for tt in range(NT):
            t0 = tt * T
            sq.dma(y_t[:].rearrange("p (c t) -> p c t", c=NEIN), yv[:, :, t0:t0 + T])
            sq.dma(z_t[:].rearrange("p (c t) -> p c t", c=NEIN), zv[:, :, t0:t0 + T])
            sq.dma(Xt[:].rearrange("p (c t) -> p c t", c=NCH), xv[:, :, t0:t0 + T])
            for c in range(NEIN):
                sq.step("vector", lambda e, c=c: e.tensor_mul(
                    g_t[:, c * T:(c + 1) * T], y_t[:, c * T:(c + 1) * T],
                    z_t[:, c * T:(c + 1) * T]))
            for d in range(NCH):
                sq.step("tensor", lambda e, d=d: [
                    e.matmul(ps_mm[:], w_s[:, c * D_MODEL + d * 128:
                                           c * D_MODEL + (d + 1) * 128],
                             g_t[:, c * T:(c + 1) * T],
                             start=(c == 0), stop=(c == NEIN - 1))
                    for c in range(NEIN)])
                sq.step("vector", lambda e, d=d: e.tensor_add(
                    oscr[:], ps_mm[:], Xt[:, d * T:(d + 1) * T]))
                sq.dma(xo[:, d, t0:t0 + T], oscr[:])

    return seq_program(prog)


def host_mamba_pre_inputs(x, W, core):
    b, h = core // 2, core % 2
    sl = core_slice_tokens(x, core)
    if h == 0:
        halo = np.zeros((3, D_MODEL), np.float32)
    else:
        halo = np.asarray(x[b, h * 2048 - 3:h * 2048])
    x_ext = fmT(np.concatenate([halo, sl], axis=0))
    cw = np.asarray(W["conv_w"], np.float32).reshape(D_INNER, 4)
    cw_p = np.ascontiguousarray(cw.reshape(NEIN, 128, 4).transpose(1, 0, 2).reshape(128, 64))
    return {
        "x_ext": x_ext,
        "in_projT": fmT(W["in_proj_w"]),
        "x_projT": fmT(W["x_proj_w"]),
        "dt_projT": fmT(W["dt_proj_w"]),
        "conv_w": cw_p,
        "conv_b": pack_vec(W["conv_b"]),
        "dt_proj_b": pack_vec(W["dt_proj_b"]),
    }


def host_mamba_scan_inputs(pre_outs, W, core):
    b, j = core // 2, core % 2
    def full(key, rows):
        return np.concatenate([pre_outs[2 * b][key], pre_outs[2 * b + 1][key]], axis=1)
    dt_f = full("dt_o", D_INNER)[j * 1024:(j + 1) * 1024]
    xin_f = full("xin_o", D_INNER)[j * 1024:(j + 1) * 1024]
    B_f = full("B_o", D_STATE)
    C_f = full("C_o", D_STATE)
    Alog = np.asarray(W["A_log"], np.float32)[j * 1024:(j + 1) * 1024]
    Alog_p = np.ascontiguousarray(
        Alog.reshape(8, 128, D_STATE).transpose(1, 0, 2).reshape(128, 8 * D_STATE))
    Dv = np.asarray(W["D"], np.float32)[j * 1024:(j + 1) * 1024]
    return {
        "dt_i": np.ascontiguousarray(dt_f), "xin_i": np.ascontiguousarray(xin_f),
        "B_i": np.ascontiguousarray(B_f), "C_i": np.ascontiguousarray(C_f),
        "Alog_i": Alog_p, "D_i": pack_vec(Dv),
    }


def host_mamba_post_inputs(x, scan_outs, pre_outs, W, core):
    b, h = core // 2, core % 2
    y_full = np.concatenate([scan_outs[2 * b]["y_o"], scan_outs[2 * b + 1]["y_o"]],
                            axis=0)
    y_slice = y_full[:, h * 2048:(h + 1) * 2048]
    zs = pre_outs[core]["zs_o"]
    return {
        "x_fm": fmT(core_slice_tokens(x, core)),
        "y_fm": np.ascontiguousarray(y_slice),
        "zs_fm": np.ascontiguousarray(zs),
        "opT": fmT(W["out_proj_w"]),
    }



def build_wkv_kv():
    NT, T = 4, 512

    def prog(nc, sq, es):
        x_fm = nc.dram_tensor("x_fm", [D_MODEL, T_CORE], F32, kind="ExternalInput").ap()
        xp_fm = nc.dram_tensor("xp_fm", [D_MODEL, T_CORE], F32, kind="ExternalInput").ap()
        wkT = nc.dram_tensor("wkT", [D_MODEL, D_MODEL], F32, kind="ExternalInput").ap()
        wvT = nc.dram_tensor("wvT", [D_MODEL, D_MODEL], F32, kind="ExternalInput").ap()
        wrT = nc.dram_tensor("wrT", [D_MODEL, D_MODEL], F32, kind="ExternalInput").ap()
        mu_k = nc.dram_tensor("mu_k", [128, NCH], F32, kind="ExternalInput").ap()
        mu_v = nc.dram_tensor("mu_v", [128, NCH], F32, kind="ExternalInput").ap()
        mu_r = nc.dram_tensor("mu_r", [128, NCH], F32, kind="ExternalInput").ap()
        ek_o = nc.dram_tensor("ek_o", [D_MODEL, T_CORE], F32, kind="ExternalOutput").ap()
        ekv_o = nc.dram_tensor("ekv_o", [D_MODEL, T_CORE], F32, kind="ExternalOutput").ap()
        r_o = nc.dram_tensor("r_o", [D_MODEL, T_CORE], F32, kind="ExternalOutput").ap()

        xv_ = x_fm.rearrange("(c p) t -> p c t", p=128)
        xpv = xp_fm.rearrange("(c p) t -> p c t", p=128)
        eko = ek_o.rearrange("(c p) t -> p c t", p=128)
        ekvo = ekv_o.rearrange("(c p) t -> p c t", p=128)
        ro = r_o.rearrange("(c p) t -> p c t", p=128)

        sb = lambda n, s, d=F32: es.enter_context(nc.sbuf_tensor(n, s, d))
        pt = lambda n, s, d=F32: es.enter_context(nc.psum_tensor(n, s, d))

        ones_col = sb("ones_col", [128, 1])
        ones_row = sb("ones_row", [1, 128])
        eps_row = sb("eps_row", [1, 1])
        Xt = sb("Xt", [128, NCH * 512])
        sq32 = sb("sq32", [128, 512])
        a_row = sb("a_row", [1, 512])
        b_row = sb("b_row", [1, 512])
        t_row = sb("t_row", [1, 512])
        xn = sb("xn", [128, NCH * T_CORE], BF16)
        xnp = sb("xnp", [128, NCH * T_CORE], BF16)
        w_s = sb("w_s", [128, NCH * D_MODEL], BF16)
        mus = {nm: sb(f"mus_{nm}", [128, NCH]) for nm in ("k", "v", "r")}
        xm = sb("xm", [128, NCH * T], BF16)
        ek_t = sb("ek_t", [128, 512])
        v_t = sb("v_t", [128, 512])
        oscr = sb("oscr", [128, 512])
        ps_a = pt("ps_a", [1, 512])
        ps_b = pt("ps_b", [1, 512])
        ps_bc1 = pt("ps_bc1", [128, 512])
        ps_bc2 = pt("ps_bc2", [128, 512])
        ps_mm = pt("ps_mm", [128, 512])
        ps_mm2 = pt("ps_mm2", [128, 512])

        sq.step("gpsimd", lambda e: e.memset(ones_col[:], 1.0))
        sq.step("gpsimd", lambda e: e.memset(ones_row[:], 1.0))
        sq.step("gpsimd", lambda e: e.memset(eps_row[:], LN_EPS))
        for nm, t in mus.items():
            sq.dma(t[:], {"k": mu_k, "v": mu_v, "r": mu_r}[nm])

        def ln_to(xsrc, dst):
            for it in range(4):
                c0, w = it * 512, 512
                sq.dma(Xt[:].rearrange("p (c t) -> p c t", c=NCH), xsrc[:, :, c0:c0 + w])
                sq.step("tensor", lambda e, w=w: [
                    e.matmul(ps_a[:, 0:w], ones_col[:], Xt[:, c * 512:c * 512 + w],
                             start=(c == 0), stop=(c == NCH - 1)) for c in range(NCH)])
                for c in range(NCH):
                    sq.step("scalar", lambda e, c=c, w=w: e.activation(
                        sq32[:, 0:w], Xt[:, c * 512:c * 512 + w], AF.Square))
                    sq.step("tensor", lambda e, c=c, w=w: e.matmul(
                        ps_b[:, 0:w], ones_col[:], sq32[:, 0:w],
                        start=(c == 0), stop=(c == NCH - 1)))
                sq.step("vector", lambda e, w=w: e.tensor_single_scalar(
                    b_row[:, 0:w], ps_a[:, 0:w], 1.0 / D_MODEL, ALU.mult))
                sq.step("vector", lambda e, w=w: e.tensor_single_scalar(
                    a_row[:, 0:w], ps_b[:, 0:w], 1.0 / D_MODEL, ALU.mult))
                sq.step("scalar", lambda e, w=w: e.activation(
                    t_row[:, 0:w], b_row[:, 0:w], AF.Square))
                sq.step("vector", lambda e, w=w: e.tensor_sub(
                    a_row[:, 0:w], a_row[:, 0:w], t_row[:, 0:w]))
                sq.step("scalar", lambda e, w=w: e.activation(
                    a_row[:, 0:w], a_row[:, 0:w], AF.Ln, bias=eps_row[:]))
                sq.step("scalar", lambda e, w=w: e.activation(
                    a_row[:, 0:w], a_row[:, 0:w], AF.Exp, scale=-0.5))
                sq.step("tensor", lambda e, w=w: e.matmul(
                    ps_bc1[:, 0:w], ones_row[:], b_row[:, 0:w]))
                sq.step("tensor", lambda e, w=w: e.matmul(
                    ps_bc2[:, 0:w], ones_row[:], a_row[:, 0:w]))
                for c in range(NCH):
                    sq.step("vector", lambda e, c=c, w=w: e.tensor_sub(
                        sq32[:, 0:w], Xt[:, c * 512:c * 512 + w], ps_bc1[:, 0:w]))
                    sq.step("vector", lambda e, c=c, c0=c0, w=w: e.tensor_mul(
                        dst[:, c * T_CORE + c0:c * T_CORE + c0 + w], sq32[:, 0:w],
                        ps_bc2[:, 0:w]))

        ln_to(xv_, xn)
        ln_to(xpv, xnp)

        for kind in ("k", "v", "r"):
            wsrc = {"k": wkT, "v": wvT, "r": wrT}[kind]
            sq.dma(w_s[:].rearrange("p (c d) -> p c d", c=NCH),
                   wsrc.rearrange("(c p) d -> p c d", p=128), engine="gpsimd")
            mu = mus[kind]
            for tt in range(NT):
                t0 = tt * T
                for c in range(NCH):
                    sq.step("vector", lambda e, c=c, t0=t0: e.tensor_sub(
                        sq32[:], xn[:, c * T_CORE + t0:c * T_CORE + t0 + T],
                        xnp[:, c * T_CORE + t0:c * T_CORE + t0 + T]))
                    sq.step("vector", lambda e, c=c, t0=t0, mu=mu: e.scalar_tensor_tensor(
                        xm[:, c * T:(c + 1) * T], sq32[:], mu[:, c:c + 1],
                        xnp[:, c * T_CORE + t0:c * T_CORE + t0 + T],
                        ALU.mult, ALU.add))
                for d in range(NCH):
                    sq.step("tensor", lambda e, d=d: [
                        e.matmul(ps_mm[:], w_s[:, c * D_MODEL + d * 128:
                                               c * D_MODEL + (d + 1) * 128],
                                 xm[:, c * T:(c + 1) * T],
                                 start=(c == 0), stop=(c == NCH - 1))
                        for c in range(NCH)])
                    if kind == "k":
                        sq.step("scalar", lambda e: e.activation(
                            ek_t[:], ps_mm[:], AF.Exp))
                        sq.dma(eko[:, d, t0:t0 + T], ek_t[:])
                    elif kind == "v":
                        sq.dma(ek_t[:], eko[:, d, t0:t0 + T])
                        sq.step("vector", lambda e: e.tensor_copy(v_t[:], ps_mm[:]))
                        sq.step("vector", lambda e: e.tensor_mul(
                            oscr[:], ek_t[:], v_t[:]))
                        sq.dma(ekvo[:, d, t0:t0 + T], oscr[:])
                    else:
                        sq.step("vector", lambda e: e.tensor_copy(oscr[:], ps_mm[:]))
                        sq.dma(ro[:, d, t0:t0 + T], oscr[:])

    return seq_program(prog)


def build_wkv_out():
    NT, T = 4, 512

    def prog(nc, sq, es):
        x_fm = nc.dram_tensor("x_fm", [D_MODEL, T_CORE], F32, kind="ExternalInput").ap()
        ek_all = nc.dram_tensor("ek_all", [4, D_MODEL, T_CORE], F32, kind="ExternalInput").ap()
        ekv_all = nc.dram_tensor("ekv_all", [4, D_MODEL, T_CORE], F32, kind="ExternalInput").ap()
        r_fm = nc.dram_tensor("r_fm", [D_MODEL, T_CORE], F32, kind="ExternalInput").ap()
        woT = nc.dram_tensor("woT", [D_MODEL, D_MODEL], F32, kind="ExternalInput").ap()
        wv_ = nc.dram_tensor("wv_", [128, NCH], F32, kind="ExternalInput").ap()
        uv_ = nc.dram_tensor("uv_", [128, NCH], F32, kind="ExternalInput").ap()
        steps = nc.dram_tensor("steps", [128, 4], F32, kind="ExternalInput").ap()
        mask = nc.dram_tensor("mask", [128, 4], F32, kind="ExternalInput").ap()
        x_out = nc.dram_tensor("x_out", [D_MODEL, T_CORE], F32, kind="ExternalOutput").ap()

        xv_ = x_fm.rearrange("(c p) t -> p c t", p=128)
        ekv_ = ek_all.rearrange("b (c p) t -> b p c t", p=128)
        ekvv = ekv_all.rearrange("b (c p) t -> b p c t", p=128)
        rv_ = r_fm.rearrange("(c p) t -> p c t", p=128)
        wov = woT.rearrange("(c p) d -> p c d", p=128)
        xo = x_out.rearrange("(c p) t -> p c t", p=128)

        sb = lambda n, s, d=F32: es.enter_context(nc.sbuf_tensor(n, s, d))
        pt = lambda n, s, d=F32: es.enter_context(nc.psum_tensor(n, s, d))

        w_s = sb("w_s", [128, NCH * D_MODEL], BF16)
        wneg = sb("wneg", [128, NCH])
        iew = sb("iew", [128, NCH])
        eu = sb("eu", [128, NCH])
        dec = sb("dec", [128, NCH * 4])
        st_s = sb("st_s", [128, 4])
        mk_s = sb("mk_s", [128, 4])
        ek_own = sb("ek_own", [128, NCH * T])
        ekv_own = sb("ekv_own", [128, NCH * T])
        cek = sb("cek", [128, NCH * T])
        cekv = sb("cekv", [128, NCH * T])
        buf = sb("buf", [128, NCH * T])
        num = sb("num", [128, NCH * T])
        wkb = sb("wkb", [128, NCH * T], BF16)
        sq32 = sb("sq32", [128, 512])
        scr2 = sb("scr2", [128, 512])
        Xt = sb("Xt", [128, NCH * T])
        oscr = sb("oscr", [128, 512])
        ps_mm = pt("ps_mm", [128, 512])

        sq.dma(wneg[:], wv_)
        sq.dma(eu[:], uv_)
        sq.dma(st_s[:], steps)
        sq.dma(mk_s[:], mask)
        sq.dma(w_s[:].rearrange("p (c d) -> p c d", c=NCH),
               wov, engine="gpsimd")
        sq.step("scalar", lambda e: e.activation(wneg[:], wneg[:], AF.Exp))
        sq.step("vector", lambda e: e.tensor_single_scalar(
            wneg[:], wneg[:], -1.0, ALU.mult))
        sq.step("scalar", lambda e: e.activation(iew[:], wneg[:], AF.Exp, scale=-1.0))
        sq.step("scalar", lambda e: e.activation(eu[:], eu[:], AF.Exp))
        for c in range(NCH):
            for bp in range(4):
                sq.step("vector", lambda e, c=c, bp=bp: e.tensor_mul(
                    dec[:, c * 4 + bp:c * 4 + bp + 1], wneg[:, c:c + 1],
                    st_s[:, bp:bp + 1]))
        sq.step("scalar", lambda e: e.activation(dec[:], dec[:], AF.Exp))
        for c in range(NCH):
            sq.step("vector", lambda e, c=c: e.tensor_mul(
                dec[:, c * 4:(c + 1) * 4], dec[:, c * 4:(c + 1) * 4], mk_s[:]))

        for tt in range(NT):
            t0 = tt * T
            sq.step("gpsimd", lambda e: e.memset(cek[:], 0.0))
            sq.step("gpsimd", lambda e: e.memset(cekv[:], 0.0))
            for bp in range(4):
                sq.dma(buf[:].rearrange("p (c t) -> p c t", c=NCH),
                       ekv_[bp, :, :, t0:t0 + T])
                for c in range(NCH):
                    sq.step("vector", lambda e, c=c, bp=bp: e.scalar_tensor_tensor(
                        cek[:, c * T:(c + 1) * T], buf[:, c * T:(c + 1) * T],
                        dec[:, c * 4 + bp:c * 4 + bp + 1],
                        cek[:, c * T:(c + 1) * T], ALU.mult, ALU.add))
                sq.dma(buf[:].rearrange("p (c t) -> p c t", c=NCH),
                       ekvv[bp, :, :, t0:t0 + T])
                for c in range(NCH):
                    sq.step("vector", lambda e, c=c, bp=bp: e.scalar_tensor_tensor(
                        cekv[:, c * T:(c + 1) * T], buf[:, c * T:(c + 1) * T],
                        dec[:, c * 4 + bp:c * 4 + bp + 1],
                        cekv[:, c * T:(c + 1) * T], ALU.mult, ALU.add))
            sq.dma(ek_own[:].rearrange("p (c t) -> p c t", c=NCH),
                   ekv_[0, :, :, t0:t0 + T])
            sq.dma(ekv_own[:].rearrange("p (c t) -> p c t", c=NCH),
                   ekvv[0, :, :, t0:t0 + T])
            for c in range(NCH):
                sq.step("vector", lambda e, c=c: e.tensor_sub(
                    sq32[:], cek[:, c * T:(c + 1) * T], ek_own[:, c * T:(c + 1) * T]))
                sq.step("vector", lambda e, c=c: e.tensor_scalar(
                    sq32[:], sq32[:], iew[:, c:c + 1], None, ALU.mult))
                sq.step("vector", lambda e, c=c: e.scalar_tensor_tensor(
                    buf[:, c * T:(c + 1) * T], ek_own[:, c * T:(c + 1) * T],
                    eu[:, c:c + 1], sq32[:], ALU.mult, ALU.add))
                sq.step("vector", lambda e, c=c: e.tensor_sub(
                    sq32[:], cekv[:, c * T:(c + 1) * T], ekv_own[:, c * T:(c + 1) * T]))
                sq.step("vector", lambda e, c=c: e.tensor_scalar(
                    sq32[:], sq32[:], iew[:, c:c + 1], None, ALU.mult))
                sq.step("vector", lambda e, c=c: e.scalar_tensor_tensor(
                    num[:, c * T:(c + 1) * T], ekv_own[:, c * T:(c + 1) * T],
                    eu[:, c:c + 1], sq32[:], ALU.mult, ALU.add))
                sq.step("vector", lambda e, c=c: e.reciprocal(
                    scr2[:], buf[:, c * T:(c + 1) * T]))
                sq.step("vector", lambda e, c=c: e.tensor_mul(
                    sq32[:], num[:, c * T:(c + 1) * T], scr2[:]))
                sq.dma(Xt[:, 0:T], rv_[:, c, t0:t0 + T])
                sq.step("scalar", lambda e: e.activation(scr2[:], Xt[:, 0:T], AF.Sigmoid))
                sq.step("vector", lambda e, c=c: e.tensor_mul(
                    wkb[:, c * T:(c + 1) * T], sq32[:], scr2[:]))
            sq.dma(Xt[:].rearrange("p (c t) -> p c t", c=NCH), xv_[:, :, t0:t0 + T])
            for d in range(NCH):
                sq.step("tensor", lambda e, d=d: [
                    e.matmul(ps_mm[:], w_s[:, c * D_MODEL + d * 128:
                                           c * D_MODEL + (d + 1) * 128],
                             wkb[:, c * T:(c + 1) * T],
                             start=(c == 0), stop=(c == NCH - 1))
                    for c in range(NCH)])
                sq.step("vector", lambda e, d=d: e.tensor_add(
                    oscr[:], ps_mm[:], Xt[:, d * T:(d + 1) * T]))
                sq.dma(xo[:, d, t0:t0 + T], oscr[:])

    return seq_program(prog)


def host_wkv_kv_inputs(x, W, core):
    b, h = core // 2, core % 2
    if b == 0:
        xp = np.zeros((T_CORE, D_MODEL), np.float32)
    else:
        xp = np.asarray(x[b - 1, h * 2048:(h + 1) * 2048])
    return {
        "x_fm": fmT(core_slice_tokens(x, core)),
        "xp_fm": fmT(xp),
        "wkT": fmT(W["Wk"]), "wvT": fmT(W["Wv"]), "wrT": fmT(W["Wr"]),
        "mu_k": pack_vec(W["mu_k"]), "mu_v": pack_vec(W["mu_v"]),
        "mu_r": pack_vec(W["mu_r"]),
    }


def host_wkv_out_inputs(x, kv_outs, W, core):
    b, h = core // 2, core % 2
    order = [(b - j) % 4 for j in range(4)]
    ek = np.stack([kv_outs[2 * bp + h]["ek_o"] for bp in order])
    ekv = np.stack([kv_outs[2 * bp + h]["ekv_o"] for bp in order])
    steps = np.zeros((128, 4), np.float32)
    maskv = np.zeros((128, 4), np.float32)
    for j, bp in enumerate(order):
        steps[:, j] = b - bp
        maskv[:, j] = 1.0 if bp <= b else 0.0
    return {
        "x_fm": fmT(core_slice_tokens(x, core)),
        "ek_all": np.ascontiguousarray(ek),
        "ekv_all": np.ascontiguousarray(ekv),
        "r_fm": kv_outs[core]["r_o"],
        "woT": fmT(W["W_out"]),
        "wv_": pack_vec(W["w"]), "uv_": pack_vec(W["u"]),
        "steps": steps, "mask": maskv,
    }



ATT_SCALE = float(1.0 / np.sqrt(16 + 512 + 512))


def build_mla_qkv():
    DQ = 512
    NQ = 4

    def prog(nc, sq, es):
        x_fm = nc.dram_tensor("x_fm", [D_MODEL, T_CORE], F32, kind="ExternalInput").ap()
        wdq = nc.dram_tensor("wdq", [D_MODEL, DQ], F32, kind="ExternalInput").ap()
        wdkv = nc.dram_tensor("wdkv", [D_MODEL, DQ], F32, kind="ExternalInput").ap()
        wuq = nc.dram_tensor("wuq", [DQ, D_MODEL], F32, kind="ExternalInput").ap()
        wqr = nc.dram_tensor("wqr", [DQ, D_MODEL], F32, kind="ExternalInput").ap()
        wuk = nc.dram_tensor("wuk", [DQ, D_MODEL], F32, kind="ExternalInput").ap()
        wuv = nc.dram_tensor("wuv", [DQ, D_MODEL], F32, kind="ExternalInput").ap()
        wkr = nc.dram_tensor("wkr", [D_MODEL, D_MODEL], F32, kind="ExternalInput").ap()
        b_dq = nc.dram_tensor("b_dq", [128, NQ], F32, kind="ExternalInput").ap()
        b_dkv = nc.dram_tensor("b_dkv", [128, NQ], F32, kind="ExternalInput").ap()
        brows = {nm: nc.dram_tensor(f"brow_{nm}", [1, D_MODEL], F32,
                                    kind="ExternalInput").ap()
                 for nm in ("uq", "qr", "kr", "uk", "uv")}
        cos_t = nc.dram_tensor("cos_t", [T_CORE, DQ], F32, kind="ExternalInput").ap()
        sin_t = nc.dram_tensor("sin_t", [T_CORE, DQ], F32, kind="ExternalInput").ap()
        q_o = nc.dram_tensor("q_o", [D_MODEL, T_CORE], BF16, kind="ExternalOutput").ap()
        k_o = nc.dram_tensor("k_o", [D_MODEL, T_CORE], BF16, kind="ExternalOutput").ap()
        v_o = nc.dram_tensor("v_o", [T_CORE, 16 * 65], BF16, kind="ExternalOutput").ap()

        xv_ = x_fm.rearrange("(c p) t -> p c t", p=128)
        qo = q_o.rearrange("(c p) t -> p c t", p=128)
        ko = k_o.rearrange("(c p) t -> p c t", p=128)

        sb = lambda n, s, d=F32: es.enter_context(nc.sbuf_tensor(n, s, d))
        pt = lambda n, s, d=F32: es.enter_context(nc.psum_tensor(n, s, d))

        ones_col = sb("ones_col", [128, 1])
        ones_row = sb("ones_row", [1, 128])
        eps_row = sb("eps_row", [1, 1])
        identw = sb("identw", [128, 128])
        identb = sb("identb", [128, 128], BF16)
        Xt = sb("Xt", [128, NCH * 512])
        sq32 = sb("sq32", [128, 512])
        scr_a = sb("scr_a", [128, 512])
        scr_b = sb("scr_b", [128, 512])
        a_row = sb("a_row", [1, 512])
        b_row = sb("b_row", [1, 512])
        t_row = sb("t_row", [1, 512])
        xn = sb("xn", [128, NCH * T_CORE], BF16)
        ctq = sb("ctq", [128, NQ * T_CORE], BF16)
        ctkv = sb("ctkv", [128, NQ * T_CORE], BF16)
        w_dq = sb("w_dq", [128, NCH * DQ], BF16)
        w_dkv = sb("w_dkv", [128, NCH * DQ], BF16)
        w_uq = sb("w_uq", [128, NQ * D_MODEL], BF16)
        w_qr = sb("w_qr", [128, NQ * D_MODEL], BF16)
        w_uk = sb("w_uk", [128, NQ * D_MODEL], BF16)
        w_uv = sb("w_uv", [128, NQ * D_MODEL], BF16)
        w_kr = sb("w_kr", [128, NCH * D_MODEL], BF16)
        bdq_s = sb("bdq_s", [128, NQ])
        bdkv_s = sb("bdkv_s", [128, NQ])
        brep = {nm: sb(f"brep_{nm}", [128, D_MODEL]) for nm in
                ("uq", "qr", "kr", "uk", "uv")}
        brow_s = sb("brow_s", [1, D_MODEL])
        cs_t = sb("cs_t", [128, DQ])
        sn_t = sb("sn_t", [128, DQ])
        qtm = sb("qtm", [128, D_MODEL])
        rtm = sb("rtm", [128, D_MODEL])
        qtb = sb("qtb", [128, D_MODEL], BF16)
        vaug_tb = sb("vaug_tb", [128, 16 * 65], BF16)
        tfm = sb("tfm", [128, 128], BF16)

        ps_a = pt("ps_a", [1, 512])
        ps_b = pt("ps_b", [1, 512])
        ps_bc1 = pt("ps_bc1", [128, 512])
        ps_bc2 = pt("ps_bc2", [128, 512])
        ps_m1 = pt("ps_m1", [128, 512])
        ps_m2 = pt("ps_m2", [128, 512])
        ps_tr = pt("ps_tr", [128, 128], BF16)

        sq.step("gpsimd", lambda e: e.memset(ones_col[:], 1.0))
        sq.step("gpsimd", lambda e: e.memset(ones_row[:], 1.0))
        sq.step("gpsimd", lambda e: e.memset(eps_row[:], LN_EPS))
        sq.step("gpsimd", lambda e: e.memset(identw[:], 1.0))
        sq.step("gpsimd", lambda e: e.affine_select(
            identw[:], identw[:], pattern=[[1, 128]], compare_op=ALU.is_equal,
            fill=0.0, base=0, channel_multiplier=-1))
        sq.step("vector", lambda e: e.tensor_copy(identb[:], identw[:]))
        sq.dma(bdq_s[:], b_dq)
        sq.dma(bdkv_s[:], b_dkv)
        sq.dma(w_dq[:].rearrange("p (c d) -> p c d", c=NCH),
               wdq.rearrange("(c p) d -> p c d", p=128), engine="gpsimd")
        sq.dma(w_dkv[:].rearrange("p (c d) -> p c d", c=NCH),
               wdkv.rearrange("(c p) d -> p c d", p=128), engine="gpsimd")
        sq.dma(w_uq[:].rearrange("p (c d) -> p c d", c=NQ),
               wuq.rearrange("(c p) d -> p c d", p=128), engine="gpsimd")
        sq.dma(w_qr[:].rearrange("p (c d) -> p c d", c=NQ),
               wqr.rearrange("(c p) d -> p c d", p=128), engine="gpsimd")
        sq.dma(w_uk[:].rearrange("p (c d) -> p c d", c=NQ),
               wuk.rearrange("(c p) d -> p c d", p=128), engine="gpsimd")
        sq.dma(w_uv[:].rearrange("p (c d) -> p c d", c=NQ),
               wuv.rearrange("(c p) d -> p c d", p=128), engine="gpsimd")
        sq.dma(w_kr[:].rearrange("p (c d) -> p c d", c=NCH),
               wkr.rearrange("(c p) d -> p c d", p=128), engine="gpsimd")
        for nm in brep:
            sq.dma(brow_s[:], brows[nm])
            for n in range(2):
                sq.step("tensor", lambda e, n=n, nm=nm: e.matmul(
                    ps_bc1[:], ones_row[:], brow_s[:, n * 512:(n + 1) * 512]))
                sq.step("vector", lambda e, n=n, nm=nm: e.tensor_copy(
                    brep[nm][:, n * 512:(n + 1) * 512], ps_bc1[:]))

        for it in range(4):
            c0, w = it * 512, 512
            sq.dma(Xt[:].rearrange("p (c t) -> p c t", c=NCH), xv_[:, :, c0:c0 + w])
            sq.step("tensor", lambda e, w=w: [
                e.matmul(ps_a[:, 0:w], ones_col[:], Xt[:, c * 512:c * 512 + w],
                         start=(c == 0), stop=(c == NCH - 1)) for c in range(NCH)])
            for c in range(NCH):
                sq.step("scalar", lambda e, c=c, w=w: e.activation(
                    sq32[:, 0:w], Xt[:, c * 512:c * 512 + w], AF.Square))
                sq.step("tensor", lambda e, c=c, w=w: e.matmul(
                    ps_b[:, 0:w], ones_col[:], sq32[:, 0:w],
                    start=(c == 0), stop=(c == NCH - 1)))
            sq.step("vector", lambda e, w=w: e.tensor_single_scalar(
                b_row[:, 0:w], ps_a[:, 0:w], 1.0 / D_MODEL, ALU.mult))
            sq.step("vector", lambda e, w=w: e.tensor_single_scalar(
                a_row[:, 0:w], ps_b[:, 0:w], 1.0 / D_MODEL, ALU.mult))
            sq.step("scalar", lambda e, w=w: e.activation(
                t_row[:, 0:w], b_row[:, 0:w], AF.Square))
            sq.step("vector", lambda e, w=w: e.tensor_sub(
                a_row[:, 0:w], a_row[:, 0:w], t_row[:, 0:w]))
            sq.step("scalar", lambda e, w=w: e.activation(
                a_row[:, 0:w], a_row[:, 0:w], AF.Ln, bias=eps_row[:]))
            sq.step("scalar", lambda e, w=w: e.activation(
                a_row[:, 0:w], a_row[:, 0:w], AF.Exp, scale=-0.5))
            sq.step("tensor", lambda e, w=w: e.matmul(
                ps_bc1[:, 0:w], ones_row[:], b_row[:, 0:w]))
            sq.step("tensor", lambda e, w=w: e.matmul(
                ps_bc2[:, 0:w], ones_row[:], a_row[:, 0:w]))
            for c in range(NCH):
                sq.step("vector", lambda e, c=c, w=w: e.tensor_sub(
                    sq32[:, 0:w], Xt[:, c * 512:c * 512 + w], ps_bc1[:, 0:w]))
                sq.step("vector", lambda e, c=c, c0=c0, w=w: e.tensor_mul(
                    xn[:, c * T_CORE + c0:c * T_CORE + c0 + w], sq32[:, 0:w],
                    ps_bc2[:, 0:w]))

        for nm, wsl, bias, dst in (("q", w_dq, bdq_s, ctq), ("kv", w_dkv, bdkv_s, ctkv)):
            for dqt in range(NQ):
                for tt in range(4):
                    t0 = tt * 512
                    sq.step("tensor", lambda e, wsl=wsl, dqt=dqt, t0=t0: [
                        e.matmul(ps_m1[:], wsl[:, c * DQ + dqt * 128:
                                               c * DQ + (dqt + 1) * 128],
                                 xn[:, c * T_CORE + t0:c * T_CORE + t0 + 512],
                                 start=(c == 0), stop=(c == NCH - 1))
                        for c in range(NCH)])
                    sq.step("vector", lambda e, dst=dst, bias=bias, dqt=dqt, t0=t0:
                        e.tensor_scalar(
                            dst[:, dqt * T_CORE + t0:dqt * T_CORE + t0 + 512],
                            ps_m1[:], bias[:, dqt:dqt + 1], None, ALU.add))

        for qt in range(T_CORE // 128):
            t0 = qt * 128
            sq.dma(cs_t[:], cos_t[t0:t0 + 128, :])
            sq.dma(sn_t[:], sin_t[t0:t0 + 128, :])
            for which in ("q", "k"):
                ct = ctq if which == "q" else ctkv
                wc = w_uq if which == "q" else w_uk
                bc = brep["uq"] if which == "q" else brep["uk"]
                for n in range(2):
                    sq.step("tensor", lambda e, ct=ct, wc=wc, n=n, t0=t0: [
                        e.matmul(ps_m1[:], ct[:, c * T_CORE + t0:c * T_CORE + t0 + 128],
                                 wc[:, c * D_MODEL + n * 512:c * D_MODEL + n * 512 + 512],
                                 start=(c == 0), stop=(c == NQ - 1))
                        for c in range(NQ)])
                    sq.step("vector", lambda e, bc=bc, n=n: e.tensor_add(
                        qtm[:, n * 512:(n + 1) * 512], ps_m1[:],
                        bc[:, n * 512:(n + 1) * 512]))
                if which == "q":
                    for n in range(2):
                        sq.step("tensor", lambda e, n=n, t0=t0: [
                            e.matmul(ps_m2[:],
                                     ctq[:, c * T_CORE + t0:c * T_CORE + t0 + 128],
                                     w_qr[:, c * D_MODEL + n * 512:
                                          c * D_MODEL + n * 512 + 512],
                                     start=(c == 0), stop=(c == NQ - 1))
                            for c in range(NQ)])
                        sq.step("vector", lambda e, n=n: e.tensor_add(
                            rtm[:, n * 512:(n + 1) * 512], ps_m2[:],
                            brep["qr"][:, n * 512:(n + 1) * 512]))
                else:
                    for n in range(2):
                        sq.step("tensor", lambda e, n=n, t0=t0: [
                            e.matmul(ps_m2[:],
                                     xn[:, c * T_CORE + t0:c * T_CORE + t0 + 128],
                                     w_kr[:, c * D_MODEL + n * 512:
                                          c * D_MODEL + n * 512 + 512],
                                     start=(c == 0), stop=(c == NCH - 1))
                            for c in range(NCH)])
                        sq.step("vector", lambda e, n=n: e.tensor_add(
                            rtm[:, n * 512:(n + 1) * 512], ps_m2[:],
                            brep["kr"][:, n * 512:(n + 1) * 512]))
                re = rtm[:].rearrange("p (i two) -> p two i", two=2)[:, 0, :]
                ro = rtm[:].rearrange("p (i two) -> p two i", two=2)[:, 1, :]
                qe = qtm[:].rearrange("p (i two) -> p two i", two=2)[:, 0, :]
                qo_ = qtm[:].rearrange("p (i two) -> p two i", two=2)[:, 1, :]
                sq.step("vector", lambda e, re=re: e.tensor_mul(scr_a[:], re, cs_t[:]))
                sq.step("vector", lambda e, ro=ro: e.tensor_mul(scr_b[:], ro, sn_t[:]))
                sq.step("vector", lambda e: e.tensor_sub(scr_a[:], scr_a[:], scr_b[:]))
                sq.step("vector", lambda e, qe=qe: e.tensor_add(qe, qe, scr_a[:]))
                sq.step("vector", lambda e, ro=ro: e.tensor_mul(scr_a[:], ro, cs_t[:]))
                sq.step("vector", lambda e, re=re: e.tensor_mul(scr_b[:], re, sn_t[:]))
                sq.step("vector", lambda e: e.tensor_add(scr_a[:], scr_a[:], scr_b[:]))
                sq.step("vector", lambda e, qo_=qo_: e.tensor_add(qo_, qo_, scr_a[:]))
                sq.step("vector", lambda e: e.tensor_copy(qtb[:], qtm[:]))
                dst = qo if which == "q" else ko
                for c in range(NCH):
                    sq.step("tensor", lambda e, c=c: e.transpose(
                        ps_tr[:], qtb[:, c * 128:(c + 1) * 128], identb[:]))
                    sq.step("vector", lambda e: e.tensor_copy(tfm[:], ps_tr[:]))
                    sq.dma(dst[:, c, t0:t0 + 128], tfm[:])
            if qt == 0:
                sq.step("gpsimd", lambda e: e.memset(vaug_tb[:], 1.0))
            for n in range(2):
                sq.step("tensor", lambda e, n=n, t0=t0: [
                    e.matmul(ps_m1[:], ctkv[:, c * T_CORE + t0:c * T_CORE + t0 + 128],
                             w_uv[:, c * D_MODEL + n * 512:c * D_MODEL + n * 512 + 512],
                             start=(c == 0), stop=(c == NQ - 1))
                    for c in range(NQ)])
                for hh in range(8):
                    g = n * 8 + hh
                    sq.step("vector", lambda e, n=n, hh=hh, g=g: e.tensor_add(
                        vaug_tb[:, g * 65:g * 65 + 64], ps_m1[:, hh * 64:(hh + 1) * 64],
                        brep["uv"][:, g * 64:(g + 1) * 64]))
            sq.dma(v_o[t0:t0 + 128, :], vaug_tb[:])

    return seq_program(prog)


def build_mla_attn():
    S = 4096
    NKT = S // 128
    NH = 16

    def prog(nc, sq, es):
        q_i = nc.dram_tensor("q_i", [D_MODEL, T_CORE], BF16, kind="ExternalInput").ap()
        k_i = nc.dram_tensor("k_i", [D_MODEL, S], BF16, kind="ExternalInput").ap()
        v_i = nc.dram_tensor("v_i", [S, D_MODEL], BF16, kind="ExternalInput").ap()
        x_fm = nc.dram_tensor("x_fm", [D_MODEL, T_CORE], F32, kind="ExternalInput").ap()
        wo = nc.dram_tensor("wo", [D_MODEL, D_MODEL], F32, kind="ExternalInput").ap()
        b_o = nc.dram_tensor("b_o", [128, NCH], F32, kind="ExternalInput").ap()
        x_out = nc.dram_tensor("x_out", [D_MODEL, T_CORE], F32, kind="ExternalOutput").ap()

        qv = q_i.rearrange("(c p) t -> p c t", p=128)
        kv_ = k_i.rearrange("(c p) t -> p c t", p=128)
        vv = v_i.rearrange("(kt p) d -> p kt d", p=128)
        xv_ = x_fm.rearrange("(c p) t -> p c t", p=128)
        xo = x_out.rearrange("(c p) t -> p c t", p=128)

        sb = lambda n, s, d=F32: es.enter_context(nc.sbuf_tensor(n, s, d))
        pt = lambda n, s, d=F32: es.enter_context(nc.psum_tensor(n, s, d))

        ones_col = sb("ones_col", [128, 1])
        onesb_col = sb("onesb_col", [128, 1], BF16)
        ones_row = sb("ones_row", [1, 128])
        identw = sb("identw", [128, 128])
        identb = sb("identb", [128, 128], BF16)
        q_sb = sb("q_sb", [128, NCH * T_CORE], BF16)
        k_sb = sb("k_sb", [128, NCH * S], BF16)
        v_sb = sb("v_sb", [128, NKT * D_MODEL], BF16)
        wo_sb = sb("wo_sb", [128, NCH * D_MODEL], BF16)
        bo_sb = sb("bo_sb", [128, NCH])
        Ktil = sb("Ktil", [128, S], BF16)
        Qtil = sb("Qtil", [128, 128], BF16)
        mm8 = sb("mm8", [128, 8])
        m_c = sb("m_c", [128, 1])
        P_T = sb("P_T", [128, 128], BF16)
        rs_row = sb("rs_row", [1, 128])
        rc_col = sb("rc_col", [128, 1])
        out_tm = sb("out_tm", [128, NH * 64])
        otb = sb("otb", [128, NH * 64], BF16)
        ofm = sb("ofm", [128, NCH * 128], BF16)
        Xt = sb("Xt", [128, NCH * 128])
        oscr = sb("oscr", [128, 128])
        tfm = sb("tfm", [128, 128], BF16)

        ps_sc = pt("ps_sc", [128, 512])
        ps_st = pt("ps_st", [128, 128])
        ps_o = pt("ps_o", [128, 64])
        ps_rs = pt("ps_rs", [1, 128])
        ps_tr = pt("ps_tr", [128, 128], BF16)
        ps_r1 = pt("ps_r1", [1, 128])
        ps_out = pt("ps_out", [128, 128])

        sq.step("gpsimd", lambda e: e.memset(ones_col[:], 1.0))
        sq.step("gpsimd", lambda e: e.memset(onesb_col[:], 1.0))
        sq.step("gpsimd", lambda e: e.memset(ones_row[:], 1.0))
        sq.step("gpsimd", lambda e: e.memset(identw[:], 1.0))
        sq.step("gpsimd", lambda e: e.affine_select(
            identw[:], identw[:], pattern=[[1, 128]], compare_op=ALU.is_equal,
            fill=0.0, base=0, channel_multiplier=-1))
        sq.step("vector", lambda e: e.tensor_copy(identb[:], identw[:]))
        sq.dma(q_sb[:].rearrange("p (c t) -> p c t", c=NCH), qv)
        sq.dma(k_sb[:].rearrange("p (c t) -> p c t", c=NCH), kv_)
        sq.dma(v_sb[:].rearrange("p (kt d) -> p kt d", kt=NKT), vv)
        sq.dma(wo_sb[:].rearrange("p (c d) -> p c d", c=NCH),
               wo.rearrange("(c p) d -> p c d", p=128), engine="gpsimd")
        sq.dma(bo_sb[:], b_o)

        for qt in range(T_CORE // 128):
            t0 = qt * 128
            for h in range(NH):
                hb = 64 * (h % 2)
                hc = h // 2
                sq.dma(Ktil[0:64, :], k_sb[hb:hb + 64, hc * S:(hc + 1) * S],
                       engine="gpsimd")
                sq.step("gpsimd", lambda e: e.memset(Ktil[64:65, 0:S], 1.0))
                for kt8 in range(8):
                    sq.step("tensor", lambda e, h=h, hb=hb, hc=hc, t0=t0, kt8=kt8:
                        e.matmul(ps_sc[:],
                                 q_sb[hb:hb + 64, hc * T_CORE + t0:hc * T_CORE + t0 + 128],
                                 k_sb[hb:hb + 64, hc * S + kt8 * 512:hc * S + (kt8 + 1) * 512],
                                 start=True, stop=True))
                    sq.step("vector", lambda e, kt8=kt8: e.reduce_max(
                        mm8[:, kt8:kt8 + 1], ps_sc[:], axis=AX.X))
                sq.step("vector", lambda e: e.reduce_max(m_c[:], mm8[:], axis=AX.X))
                sq.step("vector", lambda e: e.tensor_single_scalar(
                    m_c[:], m_c[:], -1.0, ALU.mult))
                sq.dma(Qtil[0:64, :],
                       q_sb[hb:hb + 64, hc * T_CORE + t0:hc * T_CORE + t0 + 128],
                       engine="gpsimd")
                sq.step("tensor", lambda e: e.transpose(
                    ps_r1[:], m_c[:], identw[:]))
                sq.step("vector", lambda e: e.tensor_copy(
                    Qtil[64:65, :], ps_r1[:]))
                for kt in range(NKT):
                    sq.step("tensor", lambda e, kt=kt: e.matmul(
                        ps_st[:], Ktil[0:65, kt * 128:(kt + 1) * 128],
                        Qtil[0:65, :], start=True, stop=True))
                    sq.step("scalar", lambda e: e.activation(
                        P_T[:], ps_st[:], AF.Exp, scale=ATT_SCALE))
                    sq.step("tensor", lambda e, kt=kt: e.matmul(
                        ps_rs[:], onesb_col[:], P_T[:],
                        start=(kt == 0), stop=(kt == NKT - 1)))
                    sq.step("tensor", lambda e, h=h, kt=kt: e.matmul(
                        ps_o[:], P_T[:], v_sb[:, kt * D_MODEL + h * 64:
                                              kt * D_MODEL + (h + 1) * 64],
                        start=(kt == 0), stop=(kt == NKT - 1)))
                sq.step("vector", lambda e: e.reciprocal(rs_row[:], ps_rs[:]))
                sq.step("tensor", lambda e: e.transpose(
                    ps_st[0:128, 0:1], rs_row[:], identw[0:1, 0:1]))
                sq.step("vector", lambda e: e.tensor_copy(
                    rc_col[:], ps_st[0:128, 0:1]))
                sq.step("vector", lambda e, h=h: e.tensor_scalar(
                    out_tm[:, h * 64:(h + 1) * 64], ps_o[:], rc_col[:],
                    None, ALU.mult))
                if h == NH - 1:
                    sq.step("vector", lambda e: e.tensor_copy(otb[:], out_tm[:]))
                    for c in range(NCH):
                        sq.step("tensor", lambda e, c=c: e.transpose(
                            ps_tr[:], otb[:, c * 128:(c + 1) * 128], identb[:]))
                        sq.step("vector", lambda e, c=c: e.tensor_copy(
                            ofm[:, c * 128:(c + 1) * 128], ps_tr[:]))
                    sq.dma(Xt[:].rearrange("p (c t) -> p c t", c=NCH),
                           xv_[:, :, t0:t0 + 128])
                    for d in range(NCH):
                        sq.step("tensor", lambda e, d=d: [
                            e.matmul(ps_out[:], wo_sb[:, c * D_MODEL + d * 128:
                                                      c * D_MODEL + (d + 1) * 128],
                                     ofm[:, c * 128:(c + 1) * 128],
                                     start=(c == 0), stop=(c == NCH - 1))
                            for c in range(NCH)])
                        sq.step("vector", lambda e, d=d: e.tensor_scalar(
                            oscr[:], ps_out[:], bo_sb[:, d:d + 1], None, ALU.add))
                        sq.step("vector", lambda e, d=d: e.tensor_add(
                            oscr[:], oscr[:], Xt[:, d * 128:(d + 1) * 128]))
                        sq.dma(xo[:, d, t0:t0 + 128], oscr[:])

    return seq_program(prog)


def build_mla_attn2():
    S = 4096
    NKT = 32
    NH = 16
    NH65 = NH * 65

    def prog(nc, sq, es):
        q_i = nc.dram_tensor("q_i", [D_MODEL, T_CORE], BF16, kind="ExternalInput").ap()
        k_i = nc.dram_tensor("k_i", [D_MODEL, S], BF16, kind="ExternalInput").ap()
        vaug_i = nc.dram_tensor("vaug_i", [S, NH65], BF16, kind="ExternalInput").ap()
        x_fm = nc.dram_tensor("x_fm", [D_MODEL, T_CORE], F32, kind="ExternalInput").ap()
        wo = nc.dram_tensor("wo", [D_MODEL, D_MODEL], F32, kind="ExternalInput").ap()
        b_o = nc.dram_tensor("b_o", [128, NCH], F32, kind="ExternalInput").ap()
        x_out = nc.dram_tensor("x_out", [D_MODEL, T_CORE], F32, kind="ExternalOutput").ap()

        qv = q_i.rearrange("(c p) t -> p c t", p=128)
        kv_ = k_i.rearrange("(c p) t -> p c t", p=128)
        vv = vaug_i.rearrange("(kt p) d -> p kt d", p=128)
        xv_ = x_fm.rearrange("(c p) t -> p c t", p=128)
        xo = x_out.rearrange("(c p) t -> p c t", p=128)

        sb = lambda n, s, d=F32: es.enter_context(nc.sbuf_tensor(n, s, d))
        pt = lambda n, s, d=F32: es.enter_context(nc.psum_tensor(n, s, d))

        ones_col = sb("ones_col", [128, 1])
        ones_r64 = sb("ones_r64", [1, 64])
        vaug = sb("vaug", [128, NKT * NH65], BF16)
        wo_sb = sb("wo_sb", [128, NCH * D_MODEL], BF16)
        bo_sb = sb("bo_sb", [128, NCH])
        Ktil = sb("Ktil", [65, 2 * S], BF16)
        Qtil = sb("Qtil", [65, 2 * T_CORE], BF16)
        P_T = sb("P_T", [128, 2 * 512], BF16)
        ksq = sb("ksq", [128, 512])
        ktl_b = sb("ktl_b", [128, 512], BF16)
        kt8 = sb("kt8", [1, 16])
        negkm = sb("negkm", [1, NH])
        qn_row = sb("qn_row", [1, 512])
        rrow = sb("rrow", [1, 512])
        rbc = sb("rbc", [64, 512])
        odd_t = sb("odd_t", [64, 1024], BF16)
        ofm = sb("ofm", [128, NCH * T_CORE], BF16)
        Xt = sb("Xt", [128, NCH * 512])
        oscr = sb("oscr", [128, 512])

        ps_sc = pt("ps_sc", [128, 1024])
        ps_av = pt("ps_av", [65, 1024])
        ps_bc = pt("ps_bc", [64, 512])
        ps_q1 = pt("ps_q1", [1, 512])
        ps_out = pt("ps_out", [128, 1024])

        sq.step("gpsimd", lambda e: e.memset(ones_col[:], 1.0))
        sq.step("gpsimd", lambda e: e.memset(ones_r64[:], 1.0))
        t_init = sq.step("gpsimd", lambda e: e.memset(Ktil[64:65, :], 1.0))
        t_v = sq.dma(vaug[:].rearrange("p (kt d) -> p kt d", kt=NKT), vv,
                     engine="gpsimd")
        t_wo = sq.dma(wo_sb[:].rearrange("p (c d) -> p c d", c=NCH),
                      wo.rearrange("(c p) d -> p c d", p=128), engine="gpsimd")
        t_bo = sq.dma(bo_sb[:], b_o)

        t_p = t_init
        for hc in range(NCH):
            for tile in range(8):
                t_p = sq.dma(ktl_b[:, 0:512], kv_[:, hc, tile * 512:(tile + 1) * 512],
                             wait=t_p)
                t_p = sq.step("scalar", lambda e: e.activation(
                    ksq[:], ktl_b[:], AF.Square), wait=t_p)
                t_p = sq.step("tensor", lambda e: e.matmul(
                    ps_q1[:], ones_col[0:64, :], ksq[0:64, :],
                    start=True, stop=True), wait=t_p)
                t_p = sq.step("vector", lambda e, tile=tile: e.reduce_max(
                    kt8[:, tile:tile + 1], ps_q1[:], axis=AX.X), wait=t_p)
                t_p = sq.step("tensor", lambda e: e.matmul(
                    ps_q1[:], ones_col[64:128, :], ksq[64:128, :],
                    start=True, stop=True), wait=t_p)
                t_p = sq.step("vector", lambda e, tile=tile: e.reduce_max(
                    kt8[:, 8 + tile:8 + tile + 1], ps_q1[:], axis=AX.X), wait=t_p)
            for par in range(2):
                h = 2 * hc + par
                t_p = sq.step("vector", lambda e, par=par, h=h: e.reduce_max(
                    negkm[:, h:h + 1], kt8[:, par * 8:(par + 1) * 8], axis=AX.X),
                    wait=t_p)
        t_p = sq.step("scalar", lambda e: e.activation(
            negkm[:], negkm[:], AF.Sqrt), wait=t_p)
        t_km = sq.step("vector", lambda e: e.tensor_single_scalar(
            negkm[:], negkm[:], -1.0, ALU.mult), wait=t_p)

        tok_kload = [None] * (NH + 2)
        tok_headS = [None] * (NH + 2)
        tok_exp = {}
        tok_norm = [None, None]
        tok_bcmul = None
        t_odd_dma = [None, None]

        lh_chain = [t_km]

        def load_head(h):
            par = h % 2
            hc, hb = h // 2, 64 * (h % 2)
            w1 = tmax(tok_headS[h - 2] if h >= 2 else None, t_km, t_init)
            tk = sq.dma(Ktil[0:64, par * S:(par + 1) * S],
                        kv_[hb:hb + 64, hc, :], engine="gpsimd", wait=w1)
            tq = sq.dma(Qtil[0:64, par * T_CORE:(par + 1) * T_CORE],
                        qv[hb:hb + 64, hc, :], engine="gpsimd", wait=w1)
            t_p = tmax(tq, lh_chain[0])
            for tile in range(4):
                qs = par * T_CORE + tile * 512
                t_p = sq.step("scalar", lambda e, qs=qs: e.activation(
                    ksq[0:64, 0:512], Qtil[0:64, qs:qs + 512], AF.Square), wait=t_p)
                t_p = sq.step("tensor", lambda e: e.matmul(
                    ps_q1[:], ones_col[0:64, :], ksq[0:64, :],
                    start=True, stop=True), wait=t_p)
                t_p = sq.step("scalar", lambda e: e.activation(
                    qn_row[:], ps_q1[:], AF.Sqrt), wait=t_p)
                t_p = sq.step("vector", lambda e, qs=qs, h=h: e.tensor_scalar(
                    Qtil[64:65, qs:qs + 512], qn_row[:], negkm[:, h:h + 1],
                    None, ALU.mult), wait=t_p)
            lh_chain[0] = t_p
            tok_kload[h] = tmax(tk, t_p)

        load_head(0)
        load_head(1)
        for h in range(NH):
            par = h % 2
            hc, hb = h // 2, 64 * (h % 2)
            for qt in range(4):
                pv = (h * 4 + qt) % 2
                qs = par * T_CORE + qt * 512
                t_S = None
                for kt in range(NKT):
                    sp = kt % 2
                    w = tmax(tok_kload[h], tok_exp.get((h, qt, kt - 2), None),
                             t_v if kt == 0 else None,
                             tok_norm[pv] if kt == 0 else None)
                    t_S = sq.step("tensor", lambda e, par=par, kt=kt, sp=sp, qs=qs:
                        e.matmul(ps_sc[:, sp * 512:sp * 512 + 512],
                                 Ktil[0:65, par * S + kt * 128:par * S + kt * 128 + 128],
                                 Qtil[0:65, qs:qs + 512], start=True, stop=True),
                        wait=w)
                    t_E = sq.step("scalar", lambda e, sp=sp: e.activation(
                        P_T[:, sp * 512:sp * 512 + 512],
                        ps_sc[:, sp * 512:sp * 512 + 512], AF.Exp,
                        scale=ATT_SCALE), wait=t_S)
                    tok_exp[(h, qt, kt)] = t_E
                    t_AV = sq.step("tensor", lambda e, h=h, kt=kt, sp=sp, pv=pv:
                        e.matmul(ps_av[0:65, pv * 512:pv * 512 + 512],
                                 vaug[:, kt * NH65 + h * 65:kt * NH65 + h * 65 + 65],
                                 P_T[:, sp * 512:sp * 512 + 512],
                                 start=(kt == 0), stop=(kt == NKT - 1)),
                        wait=t_E)
                tok_headS[h] = t_S
                t_r = sq.step("vector", lambda e, pv=pv: e.reciprocal(
                    rrow[:], ps_av[64:65, pv * 512:pv * 512 + 512]), wait=t_AV)
                t_b = sq.step("tensor", lambda e: e.matmul(
                    ps_bc[:], ones_r64[:], rrow[:], start=True, stop=True),
                    wait=tmax(t_r, tok_bcmul))
                t_c = sq.step("vector", lambda e: e.tensor_copy(rbc[:], ps_bc[:]),
                              wait=t_b)
                if hb == 0:
                    tok_bcmul = sq.step("vector",
                        lambda e, hc=hc, qt=qt, pv=pv: e.tensor_mul(
                            ofm[0:64, hc * T_CORE + qt * 512:hc * T_CORE + qt * 512 + 512],
                            ps_av[0:64, pv * 512:pv * 512 + 512], rbc[:]), wait=t_c)
                else:
                    tok_bcmul = sq.step("vector",
                        lambda e, pv=pv: e.tensor_mul(
                            odd_t[:, pv * 512:pv * 512 + 512],
                            ps_av[0:64, pv * 512:pv * 512 + 512], rbc[:]),
                        wait=tmax(t_c, t_odd_dma[pv]))
                    t_odd_dma[pv] = sq.dma(
                        ofm[64:128, hc * T_CORE + qt * 512:hc * T_CORE + qt * 512 + 512],
                        odd_t[:, pv * 512:pv * 512 + 512], wait=tok_bcmul)
                tok_norm[pv] = sq.barrier()
            if h + 2 < NH:
                load_head(h + 2)

        t_all = sq.barrier()
        t_evac = [None, None]
        t_lastadd = None
        t_store = None
        for qt in range(4):
            t0 = qt * 512
            tx = sq.dma(Xt[:].rearrange("p (c t) -> p c t", c=NCH),
                        xv_[:, :, t0:t0 + 512], wait=tmax(t_all, t_lastadd))
            for d in range(NCH):
                po = (qt * NCH + d) % 2
                t_mm = sq.step("tensor", lambda e, d=d, t0=t0, po=po: [
                    e.matmul(ps_out[:, po * 512:po * 512 + 512],
                             wo_sb[:, c * D_MODEL + d * 128:c * D_MODEL + (d + 1) * 128],
                             ofm[:, c * T_CORE + t0:c * T_CORE + t0 + 512],
                             start=(c == 0), stop=(c == NCH - 1))
                    for c in range(NCH)], wait=tmax(t_all, t_wo, t_evac[po]))
                t_evac[po] = sq.step("vector", lambda e, d=d, po=po: e.tensor_scalar(
                    oscr[:], ps_out[:, po * 512:po * 512 + 512],
                    bo_sb[:, d:d + 1], None, ALU.add),
                    wait=tmax(t_mm, tx, t_bo, t_store))
                t_lastadd = sq.step("vector", lambda e, d=d: e.tensor_add(
                    oscr[:], oscr[:], Xt[:, d * 512:(d + 1) * 512]))
                t_store = sq.dma(xo[:, d, t0:t0 + 512], oscr[:], wait=t_lastadd)

    return seq_program(prog)


def host_mla_qkv_inputs(x, W, core):
    ids = np.arange(512, dtype=np.float64)
    theta = (10000.0 ** (-2.0 * ids)).astype(np.float32) / 512.0
    b, h = core // 2, core % 2
    pos = np.arange(h * 2048, (h + 1) * 2048, dtype=np.float32)
    ang = pos[:, None] * theta[None, :]
    return {
        "x_fm": fmT(core_slice_tokens(x, core)),
        "wdq": np.ascontiguousarray(W["Wd_q_w"]),
        "wdkv": np.ascontiguousarray(W["Wd_kv_w"]),
        "wuq": np.ascontiguousarray(W["Wu_q_w"]),
        "wqr": np.ascontiguousarray(W["Wq_r_w"]),
        "wuk": np.ascontiguousarray(W["Wu_k_w"]),
        "wuv": np.ascontiguousarray(W["Wu_v_w"]),
        "wkr": np.ascontiguousarray(W["Wk_r_w"]),
        "b_dq": pack_vec(W["Wd_q_b"]),
        "b_dkv": pack_vec(W["Wd_kv_b"]),
        "brow_uq": np.asarray(W["Wu_q_b"], np.float32)[None, :],
        "brow_qr": np.asarray(W["Wq_r_b"], np.float32)[None, :],
        "brow_kr": np.asarray(W["Wk_r_b"], np.float32)[None, :],
        "brow_uk": np.asarray(W["Wu_k_b"], np.float32)[None, :],
        "brow_uv": np.asarray(W["Wu_v_b"], np.float32)[None, :],
        "cos_t": np.cos(ang).astype(np.float32),
        "sin_t": np.sin(ang).astype(np.float32),
    }


def host_mla_attn_inputs(x, qkv_outs, W, core):
    b, h = core // 2, core % 2
    k_full = np.concatenate([qkv_outs[2 * b]["k_o"], qkv_outs[2 * b + 1]["k_o"]],
                            axis=1)
    v_full = np.concatenate([qkv_outs[2 * b]["v_o"], qkv_outs[2 * b + 1]["v_o"]],
                            axis=0)
    return {
        "q_i": qkv_outs[core]["q_o"],
        "k_i": np.ascontiguousarray(k_full),
        "vaug_i": np.ascontiguousarray(v_full),
        "x_fm": fmT(core_slice_tokens(x, core)),
        "wo": np.ascontiguousarray(W["Wo_w"]),
        "b_o": pack_vec(W["Wo_b"]),
    }



D_CKV = 256
D_CQ = 512
TOPK_NH = 8
D_HEAD = 64
D_ROPE = 32
K_TS = 96
LW = 64
ATT2_SCALE = float((D_HEAD + D_ROPE) ** -0.5)
PADK = 63


def build_topk_proj():
    def prog(nc, sq, es):
        x_fm = nc.dram_tensor("x_fm", [D_MODEL, T_CORE], F32, kind="ExternalInput").ap()
        wdkv = nc.dram_tensor("wdkv", [D_MODEL, D_CKV], F32, kind="ExternalInput").ap()
        wuk = nc.dram_tensor("wuk", [D_CKV, 512], F32, kind="ExternalInput").ap()
        wuv = nc.dram_tensor("wuv", [D_CKV, 512], F32, kind="ExternalInput").ap()
        wdq = nc.dram_tensor("wdq", [D_MODEL, D_CQ], F32, kind="ExternalInput").ap()
        wuq = nc.dram_tensor("wuq", [D_CQ, 512], F32, kind="ExternalInput").ap()
        wqr = nc.dram_tensor("wqr", [D_CQ, 256], F32, kind="ExternalInput").ap()
        wkr = nc.dram_tensor("wkr", [D_MODEL, 32], F32, kind="ExternalInput").ap()
        wiq = nc.dram_tensor("wiq", [D_MODEL, 256], F32, kind="ExternalInput").ap()
        wik = nc.dram_tensor("wik", [D_MODEL, 64], F32, kind="ExternalInput").ap()
        q_c_tm = nc.dram_tensor("q_c_tm", [T_CORE, 512], BF16, kind="ExternalOutput").ap()
        qr_tm = nc.dram_tensor("qr_tm", [T_CORE, 256], F32, kind="ExternalOutput").ap()
        q_i_tm = nc.dram_tensor("q_i_tm", [T_CORE, 256], BF16, kind="ExternalOutput").ap()
        k_i_tm = nc.dram_tensor("k_i_tm", [T_CORE, 64], BF16, kind="ExternalOutput").ap()
        kv_tm = nc.dram_tensor("kv_tm", [T_CORE, 512], BF16, kind="ExternalOutput").ap()
        v_tm = nc.dram_tensor("v_tm", [T_CORE, 512], BF16, kind="ExternalOutput").ap()
        mu_tm = nc.dram_tensor("mu_tm", [T_CORE, 32], F32, kind="ExternalOutput").ap()

        xv_ = x_fm.rearrange("(c p) t -> p c t", p=128)

        sb = lambda n, s, d=F32: es.enter_context(nc.sbuf_tensor(n, s, d))
        pt = lambda n, s, d=F32: es.enter_context(nc.psum_tensor(n, s, d))

        ones_col = sb("ones_col", [128, 1])
        ones_row = sb("ones_row", [1, 128])
        eps_row = sb("eps_row", [1, 1])
        Xt = sb("Xt", [128, NCH * 512])
        sq32 = sb("sq32", [128, 512])
        a_row = sb("a_row", [1, 512])
        b_row = sb("b_row", [1, 512])
        t_row = sb("t_row", [1, 512])
        xn = sb("xn", [128, NCH * T_CORE], BF16)
        ctq = sb("ctq", [128, 4 * T_CORE], BF16)
        ctkv = sb("ctkv", [128, 2 * T_CORE], BF16)
        w_a = sb("w_a", [128, NCH * 512], BF16)
        w_b = sb("w_b", [128, NCH * 512], BF16)
        otm = sb("otm", [128, 512])
        otb = sb("otb", [128, 512], BF16)

        ps_a = pt("ps_a", [1, 512])
        ps_b = pt("ps_b", [1, 512])
        ps_bc1 = pt("ps_bc1", [128, 512])
        ps_bc2 = pt("ps_bc2", [128, 512])
        ps_m1 = pt("ps_m1", [128, 512])

        sq.step("gpsimd", lambda e: e.memset(ones_col[:], 1.0))
        sq.step("gpsimd", lambda e: e.memset(ones_row[:], 1.0))
        sq.step("gpsimd", lambda e: e.memset(eps_row[:], LN_EPS))

        for it in range(4):
            c0, w = it * 512, 512
            sq.dma(Xt[:].rearrange("p (c t) -> p c t", c=NCH), xv_[:, :, c0:c0 + w])
            sq.step("tensor", lambda e, w=w: [
                e.matmul(ps_a[:, 0:w], ones_col[:], Xt[:, c * 512:c * 512 + w],
                         start=(c == 0), stop=(c == NCH - 1)) for c in range(NCH)])
            for c in range(NCH):
                sq.step("scalar", lambda e, c=c, w=w: e.activation(
                    sq32[:, 0:w], Xt[:, c * 512:c * 512 + w], AF.Square))
                sq.step("tensor", lambda e, c=c, w=w: e.matmul(
                    ps_b[:, 0:w], ones_col[:], sq32[:, 0:w],
                    start=(c == 0), stop=(c == NCH - 1)))
            sq.step("vector", lambda e, w=w: e.tensor_single_scalar(
                b_row[:, 0:w], ps_a[:, 0:w], 1.0 / D_MODEL, ALU.mult))
            sq.step("vector", lambda e, w=w: e.tensor_single_scalar(
                a_row[:, 0:w], ps_b[:, 0:w], 1.0 / D_MODEL, ALU.mult))
            sq.step("scalar", lambda e, w=w: e.activation(
                t_row[:, 0:w], b_row[:, 0:w], AF.Square))
            sq.step("vector", lambda e, w=w: e.tensor_sub(
                a_row[:, 0:w], a_row[:, 0:w], t_row[:, 0:w]))
            sq.step("scalar", lambda e, w=w: e.activation(
                a_row[:, 0:w], a_row[:, 0:w], AF.Ln, bias=eps_row[:]))
            sq.step("scalar", lambda e, w=w: e.activation(
                a_row[:, 0:w], a_row[:, 0:w], AF.Exp, scale=-0.5))
            sq.step("tensor", lambda e, w=w: e.matmul(
                ps_bc1[:, 0:w], ones_row[:], b_row[:, 0:w]))
            sq.step("tensor", lambda e, w=w: e.matmul(
                ps_bc2[:, 0:w], ones_row[:], a_row[:, 0:w]))
            for c in range(NCH):
                sq.step("vector", lambda e, c=c, w=w: e.tensor_sub(
                    sq32[:, 0:w], Xt[:, c * 512:c * 512 + w], ps_bc1[:, 0:w]))
                sq.step("vector", lambda e, c=c, c0=c0, w=w: e.tensor_mul(
                    xn[:, c * T_CORE + c0:c * T_CORE + c0 + w], sq32[:, 0:w],
                    ps_bc2[:, 0:w]))

        for nm, wsrc, nd, dst in (("q", wdq, 4, ctq), ("kv", wdkv, 2, ctkv)):
            DOUT = nd * 128
            sq.dma(w_a[:, 0:NCH * DOUT].rearrange("p (c d) -> p c d", c=NCH),
                   wsrc.rearrange("(c p) d -> p c d", p=128), engine="gpsimd")
            for dt in range(nd):
                for tt in range(4):
                    t0 = tt * 512
                    sq.step("tensor", lambda e, dt=dt, t0=t0, DOUT=DOUT: [
                        e.matmul(ps_m1[:], w_a[:, c * DOUT + dt * 128:
                                               c * DOUT + (dt + 1) * 128],
                                 xn[:, c * T_CORE + t0:c * T_CORE + t0 + 512],
                                 start=(c == 0), stop=(c == NCH - 1))
                        for c in range(NCH)])
                    sq.step("vector", lambda e, dst=dst, dt=dt, t0=t0: e.tensor_copy(
                        dst[:, dt * T_CORE + t0:dt * T_CORE + t0 + 512], ps_m1[:]))

        def tm_proj(src, src_nc, wsrc, wid, out_t, post, wslot):
            sq.dma(wslot[:, 0:src_nc * wid].rearrange("p (c d) -> p c d", c=src_nc),
                   wsrc.rearrange("(c p) d -> p c d", p=128), engine="gpsimd")
            for qt in range(T_CORE // 128):
                t0 = qt * 128
                nn = (wid + 511) // 512
                for n in range(nn):
                    w = min(512, wid - n * 512)
                    sq.step("tensor", lambda e, src=src, wslot=wslot, src_nc=src_nc,
                            wid=wid, t0=t0, n=n, w=w: [
                        e.matmul(ps_m1[:, 0:w],
                                 src[:, c * T_CORE + t0:c * T_CORE + t0 + 128],
                                 wslot[:, c * wid + n * 512:c * wid + n * 512 + w],
                                 start=(c == 0), stop=(c == src_nc - 1))
                        for c in range(src_nc)])
                    post(t0, n, w)

        def mk_post_bf16(out_t):
            def post(t0, n, w):
                sq.step("vector", lambda e, t0=t0, n=n, w=w: e.tensor_copy(
                    otb[:, 0:w], ps_m1[:, 0:w]))
                sq.dma(out_t[t0:t0 + 128, n * 512:n * 512 + w], otb[:, 0:w])
            return post

        tm_proj(ctq, 4, wuq, 512, q_c_tm, mk_post_bf16(q_c_tm), w_b)

        def post_qr(t0, n, w):
            sq.step("vector", lambda e, w=w: e.tensor_copy(otm[:, 0:w], ps_m1[:, 0:w]))
            sq.dma(qr_tm[t0:t0 + 128, 0:w], otm[:, 0:w])
        tm_proj(ctq, 4, wqr, 256, qr_tm, post_qr, w_b)

        tm_proj(xn, NCH, wiq, 256, q_i_tm,
                lambda t0, n, w: (sq.step("vector", lambda e, w=w: e.tensor_copy(
                    otb[:, 0:w], ps_m1[:, 0:w])),
                    sq.dma(q_i_tm[t0:t0 + 128, 0:w], otb[:, 0:w])), w_b)
        tm_proj(xn, NCH, wik, 64, k_i_tm,
                lambda t0, n, w: (sq.step("vector", lambda e, w=w: e.tensor_copy(
                    otb[:, 0:w], ps_m1[:, 0:w])),
                    sq.dma(k_i_tm[t0:t0 + 128, 0:w], otb[:, 0:w])), w_b)
        tm_proj(ctkv, 2, wuk, 512, kv_tm, mk_post_bf16(kv_tm), w_b)
        tm_proj(ctkv, 2, wuv, 512, v_tm, mk_post_bf16(v_tm), w_b)

        def post_mu(t0, n, w):
            sq.step("scalar", lambda e, w=w: e.activation(
                otm[:, 0:w], ps_m1[:, 0:w], AF.Abs))
            sq.step("scalar", lambda e, w=w: e.activation(
                otm[:, 0:w], otm[:, 0:w], AF.Exp, scale=-1.0))
            sq.step("scalar", lambda e, w=w: e.activation(
                otm[:, 0:w], otm[:, 0:w], AF.Ln, bias=1.0))
            sq.step("vector", lambda e, w=w: e.tensor_single_scalar(
                sq32[:, 0:w], ps_m1[:, 0:w], 0.0, ALU.max))
            sq.step("vector", lambda e, w=w: e.tensor_add(
                otm[:, 0:w], otm[:, 0:w], sq32[:, 0:w]))
            sq.dma(mu_tm[t0:t0 + 128, 0:w], otm[:, 0:w])
        tm_proj(xn, NCH, wkr, 32, mu_tm, post_mu, w_b)

    return seq_program(prog)


def host_topk_proj_inputs(x, W, core):
    return {
        "x_fm": fmT(core_slice_tokens(x, core)),
        "wdkv": np.ascontiguousarray(W["W_dkv"]),
        "wuk": np.ascontiguousarray(W["W_uk"]),
        "wuv": np.ascontiguousarray(W["W_uv"]),
        "wdq": np.ascontiguousarray(W["W_dq"]),
        "wuq": np.ascontiguousarray(W["W_uq"]),
        "wqr": np.ascontiguousarray(W["W_qr"]),
        "wkr": np.ascontiguousarray(W["W_kr"]),
        "wiq": np.ascontiguousarray(W["idx_wq"]),
        "wik": np.ascontiguousarray(W["idx_wk"]),
    }


def build_topk_attn(debug=False):
    S = 4096
    SPB = 2112
    NQT = T_CORE // 128

    def prog(nc, sq, es):
        dt_ = nc.dram_tensor
        x_fm = dt_("x_fm", [D_MODEL, T_CORE], F32, kind="ExternalInput").ap()
        q_c_tm = dt_("q_c_tm", [T_CORE, 512], BF16, kind="ExternalInput").ap()
        q_c_fmi = dt_("q_c_fmi", [512, T_CORE], BF16, kind="ExternalInput").ap()
        qr_tmi = dt_("qr_tmi", [T_CORE, 256], F32, kind="ExternalInput").ap()
        q_i_fmi = dt_("q_i_fmi", [256, T_CORE], BF16, kind="ExternalInput").ap()
        k_i_fmi = dt_("k_i_fmi", [128, S], BF16, kind="ExternalInput").ap()
        kv_fm_pad = dt_("kv_fm_pad", [512, SPB], BF16, kind="ExternalInput").ap()
        kv_tm_full = dt_("kv_tm_full", [S, 512], BF16, kind="ExternalInput").ap()
        v_tm_full = dt_("v_tm_full", [S, 512], BF16, kind="ExternalInput").ap()
        v_pad = dt_("v_pad", [SPB, 512], BF16, kind="ExternalInput").ap()
        mu1_pad = dt_("mu1_pad", [16, SPB], BF16, kind="ExternalInput").ap()
        mu2_pad = dt_("mu2_pad", [16, SPB], BF16, kind="ExternalInput").ap()
        mu_tm_full = dt_("mu_tm_full", [S, 32], F32, kind="ExternalInput").ap()
        w_out = dt_("w_out", [512, D_MODEL], F32, kind="ExternalInput").ap()
        idxw_row = dt_("idxw_row", [1, 4], F32, kind="ExternalInput").ap()
        rdelta = dt_("rdelta", [1, 16], F32, kind="ExternalInput").ap()
        thr_i = dt_("thr_i", [128, NQT], F32, kind="ExternalInput").ap()
        ramp0 = dt_("ramp0", [128, 96], F32, kind="ExternalInput").ap()
        mask0 = dt_("mask0", [128, 512], U8, kind="ExternalInput").ap()
        cth_q = dt_("cth_q", [T_CORE, 16], F32, kind="ExternalInput").ap()
        sth_q = dt_("sth_q", [T_CORE, 16], F32, kind="ExternalInput").ap()
        cth_q2 = dt_("cth_q2", [T_CORE, 16], F32, kind="ExternalInput").ap()
        sth_q2 = dt_("sth_q2", [T_CORE, 16], F32, kind="ExternalInput").ap()
        cjb_i = dt_("cjb_i", [16, 192], F32, kind="ExternalInput").ap()
        sjb_i = dt_("sjb_i", [16, 192], F32, kind="ExternalInput").ap()
        cabs96 = dt_("cabs96", [16, 96], F32, kind="ExternalInput").ap()
        sabs96 = dt_("sabs96", [16, 96], F32, kind="ExternalInput").ap()
        crk_i = dt_("crk_i", [128, 512], F32, kind="ExternalInput").ap()
        srk_i = dt_("srk_i", [128, 512], F32, kind="ExternalInput").ap()
        x_out = dt_("x_out", [D_MODEL, T_CORE], F32, kind="ExternalOutput").ap()
        if debug:
            dbg = {nm: dt_(f"dbg_{nm}", shp, dty, kind="ExternalOutput").ap()
                   for nm, shp, dty in (
                       ("I", [128, 4096], F32), ("idx", [128, 32], U32),
                       ("sctop", [128, 32], F32), ("band", [128, 192], F32),
                       ("qh1", [128, 128], F32), ("qh2", [128, 128], F32),
                       ("mh1", [128, 512], F32), ("mh2", [128, 512], F32),
                       ("outtm", [128, 512], F32), ("rs", [128, 1], F32),
                       ("ka", [16, 192], F32), ("qpa", [16, 1024], F32),
                       ("mug", [128, 1024], F32))}

        xv_ = x_fm.rearrange("(c p) t -> p c t", p=128)
        xo = x_out.rearrange("(c p) t -> p c t", p=128)

        sb = lambda n, s, d=F32: es.enter_context(nc.sbuf_tensor(n, s, d))
        pt = lambda n, s, d=F32: es.enter_context(nc.psum_tensor(n, s, d))

        ones_row = sb("ones_row", [1, 128])
        identw = sb("identw", [128, 128])
        identb = sb("identb", [128, 128], BF16)
        kvf_sb = sb("kvf_sb", [128, 4 * SPB], BF16)
        qcf_sb = sb("qcf_sb", [128, 4 * T_CORE], BF16)
        qif_sb = sb("qif_sb", [128, 2 * T_CORE], BF16)
        kif_sb = sb("kif_sb", [128, S], BF16)
        mu1_sb = sb("mu1_sb", [16, SPB], BF16)
        mu2_sb = sb("mu2_sb", [16, SPB], BF16)
        wo_sb = sb("wo_sb", [128, 4 * D_MODEL], BF16)
        iw_bc = sb("iw_bc", [128, 4])
        thr_sb = sb("thr_sb", [128, NQT])
        ramp_sb = sb("ramp_sb", [128, 96])
        mask0_sb = sb("mask0_sb", [128, 512], U8)
        crkd = sb("crkd", [128, 512])
        srkd = sb("srkd", [128, 512])
        drow = sb("drow", [1, 16])
        cd_row = sb("cd_row", [1, 16])
        sd_row = sb("sd_row", [1, 16])
        cd_bc = sb("cd_bc", [128, 16])
        sd_bc = sb("sd_bc", [128, 16])
        cd_pp = sb("cd_pp", [16, 1])
        sd_pp = sb("sd_pp", [16, 1])
        tr16 = sb("tr16", [1, 16])
        cjb_sb = sb("cjb_sb", [16, 192])
        sjb_sb = sb("sjb_sb", [16, 192])
        Cjb = sb("Cjb", [16, 192], BF16)
        Sjb = sb("Sjb", [16, 192], BF16)
        c96 = sb("c96", [16, 96])
        s96 = sb("s96", [16, 96])
        C96 = sb("C96", [16, 96], BF16)
        S96 = sb("S96", [16, 96], BF16)
        ka_band = sb("ka_band", [16, 192], BF16)
        kb_band = sb("kb_band", [16, 192], BF16)
        ka96 = sb("ka96", [16, 96], BF16)
        kb96 = sb("kb96", [16, 96], BF16)
        I_sb = sb("I_sb", [128, S])
        scrB = sb("scrB", [128, 2048])
        scrC = sb("scrC", [128, 512])
        m8 = sb("m8", [128, 8])
        idx32 = sb("idx32", [128, 32], U32)
        mug = sb("mug", [128, 32 * 32])
        KVg = sb("KVg", [128, 32 * 512], BF16)
        Vg = sb("Vg", [128, 32 * 512], BF16)
        qr_t = sb("qr_t", [128, 256])
        qm_t = sb("qm_t", [128, 256])
        cth_t = sb("cth_t", [128, 16])
        sth_t = sb("sth_t", [128, 16])
        ct2_t = sb("ct2_t", [128, 16])
        st2_t = sb("st2_t", [128, 16])
        ctd = sb("ctd", [128, 16])
        std_ = sb("std_", [128, 16])
        ct2d = sb("ct2d", [128, 16])
        st2d = sb("st2d", [128, 16])
        qh1 = sb("qh1", [128, 128])
        qh2 = sb("qh2", [128, 128])
        qp1 = sb("qp1", [128, 128])
        qp2 = sb("qp2", [128, 128])
        qp1b = sb("qp1b", [128, 128], BF16)
        qp2b = sb("qp2b", [128, 128], BF16)
        qpa_fm = sb("qpa_fm", [16, 8 * 128], BF16)
        qpb_fm = sb("qpb_fm", [16, 8 * 128], BF16)
        qsa_fm = sb("qsa_fm", [16, 8 * 128], BF16)
        qsb_fm = sb("qsb_fm", [16, 8 * 128], BF16)
        mh1 = sb("mh1", [128, 512])
        mh2 = sb("mh2", [128, 512])
        sb_band = sb("sb_band", [128, 192])
        sc_top = sb("sc_top", [128, 32])
        mb_c = sb("mb_c", [128, 1])
        mt_c = sb("mt_c", [128, 1])
        bias_c = sb("bias_c", [128, 1])
        rs1 = sb("rs1", [128, 1])
        rs2 = sb("rs2", [128, 1])
        rr_c = sb("rr_c", [128, 1])
        P_band = sb("P_band", [128, 192], BF16)
        P_top = sb("P_top", [128, 32])
        PbT1 = sb("PbT1", [128, 128], BF16)
        PbT2 = sb("PbT2", [64, 128], BF16)
        Vb1 = sb("Vb1", [128, 512], BF16)
        Vb2 = sb("Vb2", [64, 512], BF16)
        v96_sb = sb("v96_sb", [96, 512], BF16)
        ot_top = sb("ot_top", [128, 64])
        out_tm = sb("out_tm", [128, 512])
        dense_tm = sb("dense_tm", [128, 512])
        otb = sb("otb", [128, 512], BF16)
        ofm = sb("ofm", [128, 4 * 128], BF16)
        Xt = sb("Xt", [128, NCH * 128])
        oscr = sb("oscr", [128, 128])
        qct_t = sb("qct_t", [128, 512], BF16)

        ps_i = pt("ps_i", [128, 512])
        ps_band = pt("ps_band", [128, 192])
        ps_96 = pt("ps_96", [128, 96])
        ps_av = pt("ps_av", [128, 64])
        ps_tr = pt("ps_tr", [128, 128], BF16)
        ps_trf = pt("ps_trf", [128, 128])
        ps_out = pt("ps_out", [128, 128])

        sq.step("gpsimd", lambda e: e.memset(ones_row[:], 1.0))
        sq.step("gpsimd", lambda e: e.memset(identw[:], 1.0))
        sq.step("gpsimd", lambda e: e.affine_select(
            identw[:], identw[:], pattern=[[1, 128]], compare_op=ALU.is_equal,
            fill=0.0, base=0, channel_multiplier=-1))
        sq.step("vector", lambda e: e.tensor_copy(identb[:], identw[:]))
        sq.dma(kvf_sb[:].rearrange("p (c t) -> p c t", c=4),
               kv_fm_pad.rearrange("(c p) t -> p c t", p=128))
        sq.dma(qcf_sb[:].rearrange("p (c t) -> p c t", c=4),
               q_c_fmi.rearrange("(c p) t -> p c t", p=128))
        sq.dma(qif_sb[:].rearrange("p (c t) -> p c t", c=2),
               q_i_fmi.rearrange("(c p) t -> p c t", p=128))
        sq.dma(kif_sb[:], k_i_fmi)
        sq.dma(mu1_sb[:], mu1_pad)
        sq.dma(mu2_sb[:], mu2_pad)
        sq.dma(wo_sb[:].rearrange("p (c d) -> p c d", c=4),
               w_out.rearrange("(c p) d -> p c d", p=128), engine="gpsimd")
        sq.dma(thr_sb[:], thr_i)
        sq.dma(ramp_sb[:], ramp0)
        sq.dma(mask0_sb[:], mask0)
        sq.dma(crkd[:], crk_i)
        sq.dma(srkd[:], srk_i)
        sq.dma(cjb_sb[:], cjb_i)
        sq.dma(sjb_sb[:], sjb_i)
        sq.dma(c96[:], cabs96)
        sq.dma(s96[:], sabs96)
        sq.dma(drow[:], rdelta)
        sq.dma(tr16[:, 0:4], idxw_row)
        sq.step("tensor", lambda e: e.matmul(ps_i[:, 0:4], ones_row[:], tr16[:, 0:4]))
        sq.step("vector", lambda e: e.tensor_copy(iw_bc[:], ps_i[:, 0:4]))
        sq.step("scalar", lambda e: e.activation(drow[:], drow[:], AF.Sigmoid))
        sq.step("vector", lambda e: e.tensor_single_scalar(
            drow[:], drow[:], -2.0 * np.pi, ALU.mult))
        sq.step("vector", lambda e: e.tensor_single_scalar(
            tr16[:], drow[:], -np.pi, ALU.is_lt))
        sq.step("vector", lambda e: e.scalar_tensor_tensor(
            drow[:], tr16[:], 2.0 * np.pi, drow[:], ALU.mult, ALU.add))
        sq.step("scalar", lambda e: e.activation(sd_row[:], drow[:], AF.Sin))
        sq.step("vector", lambda e: e.tensor_single_scalar(
            cd_row[:], drow[:], np.pi / 2, ALU.add))
        sq.step("vector", lambda e: e.tensor_single_scalar(
            tr16[:], cd_row[:], np.pi, ALU.is_gt))
        sq.step("vector", lambda e: e.scalar_tensor_tensor(
            cd_row[:], tr16[:], -2.0 * np.pi, cd_row[:], ALU.mult, ALU.add))
        sq.step("scalar", lambda e: e.activation(cd_row[:], cd_row[:], AF.Sin))
        for src, dstb, dstp in ((cd_row, cd_bc, cd_pp), (sd_row, sd_bc, sd_pp)):
            sq.step("tensor", lambda e, src=src: e.matmul(
                ps_i[:, 0:16], ones_row[:], src[:]))
            sq.step("vector", lambda e, dstb=dstb: e.tensor_copy(dstb[:], ps_i[:, 0:16]))
            sq.step("tensor", lambda e, src=src: e.transpose(
                ps_trf[0:16, 0:1], src[:], identw[0:1, 0:1]))
            sq.step("vector", lambda e, dstp=dstp: e.tensor_copy(
                dstp[:], ps_trf[0:16, 0:1]))
        cdrep = cd_bc[:].rearrange("p (o i) -> p o i", o=1).to_broadcast([128, 32, 16])
        sdrep = sd_bc[:].rearrange("p (o i) -> p o i", o=1).to_broadcast([128, 32, 16])
        crkv = crkd[:].rearrange("p (r i) -> p r i", i=16)
        srkv = srkd[:].rearrange("p (r i) -> p r i", i=16)
        sq.step("vector", lambda e: e.tensor_mul(
            scrB[:, 0:512].rearrange("p (r i) -> p r i", i=16), crkv, cdrep))
        sq.step("vector", lambda e: e.tensor_mul(
            scrB[:, 512:1024].rearrange("p (r i) -> p r i", i=16), srkv, sdrep))
        sq.step("vector", lambda e: e.tensor_mul(
            scrB[:, 1024:1536].rearrange("p (r i) -> p r i", i=16), srkv, cdrep))
        sq.step("vector", lambda e: e.tensor_mul(
            scrB[:, 1536:2048].rearrange("p (r i) -> p r i", i=16), crkv, sdrep))
        sq.step("vector", lambda e: e.tensor_sub(
            crkd[:], scrB[:, 0:512], scrB[:, 512:1024]))
        sq.step("vector", lambda e: e.tensor_add(
            srkd[:], scrB[:, 1024:1536], scrB[:, 1536:2048]))
        sq.step("vector", lambda e: e.tensor_scalar(
            scrB[0:16, 0:192], cjb_sb[:], cd_pp[:], None, ALU.mult))
        sq.step("vector", lambda e: e.tensor_scalar(
            scrC[0:16, 0:192], sjb_sb[:], sd_pp[:], None, ALU.mult))
        sq.step("vector", lambda e: e.tensor_sub(
            Cjb[:], scrB[0:16, 0:192], scrC[0:16, 0:192]))
        sq.step("vector", lambda e: e.tensor_scalar(
            scrB[0:16, 0:192], sjb_sb[:], cd_pp[:], None, ALU.mult))
        sq.step("vector", lambda e: e.tensor_scalar(
            scrC[0:16, 0:192], cjb_sb[:], sd_pp[:], None, ALU.mult))
        sq.step("vector", lambda e: e.tensor_add(
            Sjb[:], scrB[0:16, 0:192], scrC[0:16, 0:192]))
        sq.step("vector", lambda e: e.tensor_scalar(
            scrB[0:16, 0:96], c96[:], cd_pp[:], None, ALU.mult))
        sq.step("vector", lambda e: e.tensor_scalar(
            scrC[0:16, 0:96], s96[:], sd_pp[:], None, ALU.mult))
        sq.step("vector", lambda e: e.tensor_sub(
            C96[:], scrB[0:16, 0:96], scrC[0:16, 0:96]))
        sq.step("vector", lambda e: e.tensor_scalar(
            scrB[0:16, 0:96], s96[:], cd_pp[:], None, ALU.mult))
        sq.step("vector", lambda e: e.tensor_scalar(
            scrC[0:16, 0:96], c96[:], sd_pp[:], None, ALU.mult))
        sq.step("vector", lambda e: e.tensor_add(
            S96[:], scrB[0:16, 0:96], scrC[0:16, 0:96]))
        sq.step("vector", lambda e: e.tensor_mul(
            scrB[0:16, 0:96], mu1_sb[0:16, PADK:PADK + 96], C96[:]))
        sq.step("vector", lambda e: e.tensor_mul(
            scrC[0:16, 0:96], mu2_sb[0:16, PADK:PADK + 96], S96[:]))
        sq.step("vector", lambda e: e.tensor_sub(
            ka96[:], scrB[0:16, 0:96], scrC[0:16, 0:96]))
        sq.step("vector", lambda e: e.tensor_mul(
            scrB[0:16, 0:96], mu1_sb[0:16, PADK:PADK + 96], S96[:]))
        sq.step("vector", lambda e: e.tensor_mul(
            scrC[0:16, 0:96], mu2_sb[0:16, PADK:PADK + 96], C96[:]))
        sq.step("vector", lambda e: e.tensor_add(
            kb96[:], scrB[0:16, 0:96], scrC[0:16, 0:96]))
        sq.dma(v96_sb[:], v_pad[PADK:PADK + 96, :])
        iota_c = sb("iota_c", [128, 512])
        bandmask = sb("bandmask", [128, 192])
        thr_adj = sb("thr_adj", [128, 1])
        sq.step("gpsimd", lambda e: e.iota(
            iota_c[:], pattern=[[1, 512]], base=0, channel_multiplier=0,
            allow_small_or_imprecise_dtypes=True))
        sq.step("gpsimd", lambda e: e.memset(bandmask[:], 0.0))
        sq.step("gpsimd", lambda e: e.affine_select(
            bandmask[:], bandmask[:], pattern=[[1, 192]], compare_op=ALU.is_ge,
            fill=-3e38, base=0, channel_multiplier=-1))
        sq.step("gpsimd", lambda e: e.affine_select(
            bandmask[:], bandmask[:], pattern=[[-1, 192]], compare_op=ALU.is_ge,
            fill=-3e38, base=63, channel_multiplier=1))

        for qt in range(NQT):
            t0 = qt * 128
            for kt in range(8):
                for hh in range(4):
                    hb = 64 * (hh % 2)
                    hc = hh // 2
                    sq.step("tensor", lambda e, hb=hb, hc=hc, t0=t0, kt=kt: e.matmul(
                        ps_i[:], qif_sb[hb:hb + 64,
                                        hc * T_CORE + t0:hc * T_CORE + t0 + 128],
                        kif_sb[hb:hb + 64, kt * 512:(kt + 1) * 512],
                        start=True, stop=True))
                    if hh == 0:
                        sq.step("vector", lambda e, kt=kt, hh=hh: e.tensor_scalar(
                            I_sb[:, kt * 512:(kt + 1) * 512], ps_i[:], 0.0,
                            iw_bc[:, hh:hh + 1], ALU.max, ALU.mult))
                    else:
                        sq.step("vector", lambda e, kt=kt, hh=hh: e.tensor_scalar(
                            scrC[:], ps_i[:], 0.0, iw_bc[:, hh:hh + 1],
                            ALU.max, ALU.mult))
                        sq.step("vector", lambda e, kt=kt: e.tensor_add(
                            I_sb[:, kt * 512:(kt + 1) * 512],
                            I_sb[:, kt * 512:(kt + 1) * 512], scrC[:]))
            for kt in range(8):
                sq.step("vector", lambda e, qt=qt, kt=kt: e.tensor_single_scalar(
                    thr_adj[:], thr_sb[:, qt:qt + 1], float(-512 * kt), ALU.add))
                sq.step("vector", lambda e: e.tensor_scalar(
                    scrC[:], iota_c[:], thr_adj[:], None, ALU.is_le))
                sq.step("vector", lambda e: e.tensor_scalar(
                    scrC[:], scrC[:], 2e38, -1e38, ALU.mult, ALU.add))
                sq.step("vector", lambda e, kt=kt: e.tensor_tensor(
                    I_sb[:, kt * 512:(kt + 1) * 512],
                    I_sb[:, kt * 512:(kt + 1) * 512], scrC[:], ALU.min))
            if qt == 0:
                sq.step("vector", lambda e: e.tensor_max(
                    I_sb[:, 0:96], I_sb[:, 0:96], ramp_sb[:]))
            for rd in range(4):
                sq.step("vector", lambda e: e.max(out=m8[:], in_=I_sb[:]))
                sq.step("vector", lambda e, rd=rd: e.max_index(
                    out=idx32[:, rd * 8:(rd + 1) * 8], in_max=m8[:], in_values=I_sb[:]))
                sq.step("vector", lambda e: e.match_replace(
                    out=I_sb[:], in_to_replace=m8[:], in_values=I_sb[:],
                    imm_value=-3e38))
            for r in range(32):
                sq.step("gpsimd", lambda e, r=r: e.indirect_dma_start(
                    out=mug[:, r * 32:(r + 1) * 32], out_offset=None,
                    in_=mu_tm_full,
                    in_offset=IndirectOffsetOnAxis(ap=idx32[:, r:r + 1], axis=0),
                    oob_is_err=False), inc=16, dma=True)
                sq.step("gpsimd", lambda e, r=r: e.indirect_dma_start(
                    out=KVg[:, r * 512:(r + 1) * 512], out_offset=None,
                    in_=kv_tm_full,
                    in_offset=IndirectOffsetOnAxis(ap=idx32[:, r:r + 1], axis=0),
                    oob_is_err=False), inc=16, dma=True)
                sq.step("gpsimd", lambda e, r=r: e.indirect_dma_start(
                    out=Vg[:, r * 512:(r + 1) * 512], out_offset=None,
                    in_=v_tm_full,
                    in_offset=IndirectOffsetOnAxis(ap=idx32[:, r:r + 1], axis=0),
                    oob_is_err=False), inc=16, dma=True)
            if debug and qt == 8:
                sq.dma(dbg["I"], I_sb[:])
                sq.dma(dbg["idx"], idx32[:])
            sq.dma(qr_t[:], qr_tmi[t0:t0 + 128, :])
            sq.dma(qct_t[:], q_c_tm[t0:t0 + 128, :])
            sq.dma(cth_t[:], cth_q[t0:t0 + 128, :])
            sq.dma(sth_t[:], sth_q[t0:t0 + 128, :])
            sq.dma(ct2_t[:], cth_q2[t0:t0 + 128, :])
            sq.dma(st2_t[:], sth_q2[t0:t0 + 128, :])
            sq.step("scalar", lambda e: e.activation(qm_t[:], qr_t[:], AF.Abs))
            sq.step("scalar", lambda e: e.activation(qm_t[:], qm_t[:], AF.Exp, scale=-1.0))
            sq.step("scalar", lambda e: e.activation(qm_t[:], qm_t[:], AF.Ln, bias=1.0))
            sq.step("vector", lambda e: e.tensor_single_scalar(
                scrB[:, 0:256], qr_t[:], 0.0, ALU.max))
            sq.step("vector", lambda e: e.tensor_add(qm_t[:], qm_t[:], scrB[:, 0:256]))
            for (csrc, ssrc, cdst, sdst) in ((cth_t, sth_t, ctd, std_),
                                             (ct2_t, st2_t, ct2d, st2d)):
                sq.step("vector", lambda e, csrc=csrc, cdst=cdst: e.tensor_mul(
                    cdst[:], csrc[:], cd_bc[:]))
                sq.step("vector", lambda e, ssrc=ssrc: e.tensor_mul(
                    scrC[:, 0:16], ssrc[:], sd_bc[:]))
                sq.step("vector", lambda e, cdst=cdst: e.tensor_sub(
                    cdst[:], cdst[:], scrC[:, 0:16]))
                sq.step("vector", lambda e, ssrc=ssrc, sdst=sdst: e.tensor_mul(
                    sdst[:], ssrc[:], cd_bc[:]))
                sq.step("vector", lambda e, csrc=csrc: e.tensor_mul(
                    scrC[:, 0:16], csrc[:], sd_bc[:]))
                sq.step("vector", lambda e, sdst=sdst: e.tensor_add(
                    sdst[:], sdst[:], scrC[:, 0:16]))
            qm1 = qm_t[:].rearrange("p (h two i) -> p h two i", two=2, i=16)[:, :, 0, :]
            qm2 = qm_t[:].rearrange("p (h two i) -> p h two i", two=2, i=16)[:, :, 1, :]
            for (cc, ss, d1, d2) in ((ctd, std_, qh1, qh2), (ct2d, st2d, qp1, qp2)):
                ccb = cc[:].rearrange("p (o i) -> p o i", o=1).to_broadcast([128, 8, 16])
                ssb = ss[:].rearrange("p (o i) -> p o i", o=1).to_broadcast([128, 8, 16])
                d1v = d1[:].rearrange("p (h i) -> p h i", i=16)
                d2v = d2[:].rearrange("p (h i) -> p h i", i=16)
                sB = scrB[:, 0:128].rearrange("p (h i) -> p h i", i=16)
                sq.step("vector", lambda e, d1v=d1v, ccb=ccb: e.tensor_mul(d1v, qm1, ccb))
                sq.step("vector", lambda e, sB=sB, ssb=ssb: e.tensor_mul(sB, qm2, ssb))
                sq.step("vector", lambda e, d1=d1: e.tensor_sub(
                    d1[:], d1[:], scrB[:, 0:128]))
                sq.step("vector", lambda e, d2v=d2v, ssb=ssb: e.tensor_mul(d2v, qm1, ssb))
                sq.step("vector", lambda e, sB=sB, ccb=ccb: e.tensor_mul(sB, qm2, ccb))
                sq.step("vector", lambda e, d2=d2: e.tensor_add(
                    d2[:], d2[:], scrB[:, 0:128]))
            sq.step("vector", lambda e: e.tensor_copy(qp1b[:], qp1[:]))
            sq.step("vector", lambda e: e.tensor_copy(qp2b[:], qp2[:]))
            for h in range(8):
                sq.step("tensor", lambda e, h=h: e.transpose(
                    ps_tr[0:16, 0:128], qp1b[:, h * 16:(h + 1) * 16], identb[:]))
                sq.step("vector", lambda e, h=h: e.tensor_copy(
                    qpa_fm[:, h * 128:(h + 1) * 128], ps_tr[0:16, 0:128]))
                sq.step("tensor", lambda e, h=h: e.transpose(
                    ps_tr[0:16, 0:128], qp2b[:, h * 16:(h + 1) * 16], identb[:]))
                sq.step("vector", lambda e, h=h: e.tensor_copy(
                    qpb_fm[:, h * 128:(h + 1) * 128], ps_tr[0:16, 0:128]))
            if qt == 0:
                sq.step("vector", lambda e: e.tensor_copy(qp1b[:], qh1[:]))
                sq.step("vector", lambda e: e.tensor_copy(qp2b[:], qh2[:]))
                for h in range(8):
                    sq.step("tensor", lambda e, h=h: e.transpose(
                        ps_tr[0:16, 0:128], qp1b[:, h * 16:(h + 1) * 16], identb[:]))
                    sq.step("vector", lambda e, h=h: e.tensor_copy(
                        qsa_fm[:, h * 128:(h + 1) * 128], ps_tr[0:16, 0:128]))
                    sq.step("tensor", lambda e, h=h: e.transpose(
                        ps_tr[0:16, 0:128], qp2b[:, h * 16:(h + 1) * 16], identb[:]))
                    sq.step("vector", lambda e, h=h: e.tensor_copy(
                        qsb_fm[:, h * 128:(h + 1) * 128], ps_tr[0:16, 0:128]))
            mg1 = mug[:].rearrange("p (r two i) -> p r two i", two=2, i=16)[:, :, 0, :]
            mg2 = mug[:].rearrange("p (r two i) -> p r two i", two=2, i=16)[:, :, 1, :]
            crv = crkd[:].rearrange("p (r i) -> p r i", i=16)
            srv = srkd[:].rearrange("p (r i) -> p r i", i=16)
            m1v = mh1[:].rearrange("p (r i) -> p r i", i=16)
            m2v = mh2[:].rearrange("p (r i) -> p r i", i=16)
            sBv = scrB[:, 0:512].rearrange("p (r i) -> p r i", i=16)
            sq.step("vector", lambda e: e.tensor_mul(m1v, mg1, crv))
            sq.step("vector", lambda e: e.tensor_mul(sBv, mg2, srv))
            sq.step("vector", lambda e: e.tensor_sub(mh1[:], mh1[:], scrB[:, 0:512]))
            sq.step("vector", lambda e: e.tensor_mul(m2v, mg1, srv))
            sq.step("vector", lambda e: e.tensor_mul(sBv, mg2, crv))
            sq.step("vector", lambda e: e.tensor_add(mh2[:], mh2[:], scrB[:, 0:512]))
            for (dst, A, Bt) in ((ka_band, Cjb, Sjb), (kb_band, Sjb, Cjb)):
                sq.step("vector", lambda e, t0=t0, A=A: e.tensor_mul(
                    scrB[0:16, 0:192], mu1_sb[0:16, t0:t0 + 192], A[:]))
                sq.step("vector", lambda e, t0=t0, Bt=Bt: e.tensor_mul(
                    scrC[0:16, 0:192], mu2_sb[0:16, t0:t0 + 192], Bt[:]))
                if dst is ka_band:
                    sq.step("vector", lambda e, dst=dst: e.tensor_sub(
                        dst[:], scrB[0:16, 0:192], scrC[0:16, 0:192]))
                else:
                    sq.step("vector", lambda e, dst=dst: e.tensor_add(
                        dst[:], scrB[0:16, 0:192], scrC[0:16, 0:192]))
            sq.dma(Vb1[:], v_pad[t0:t0 + 128, :])
            sq.dma(Vb2[:], v_pad[t0 + 128:t0 + 192, :])

            for h in range(8):
                hb = 64 * (h % 2)
                hc = h // 2
                sq.step("tensor", lambda e, hb=hb, hc=hc, t0=t0: e.matmul(
                    ps_band[:], qcf_sb[hb:hb + 64, hc * T_CORE + t0:hc * T_CORE + t0 + 128],
                    kvf_sb[hb:hb + 64, hc * SPB + t0:hc * SPB + t0 + 192],
                    start=True, stop=False))
                sq.step("tensor", lambda e, h=h: e.matmul(
                    ps_band[:], qpa_fm[:, h * 128:(h + 1) * 128], ka_band[:],
                    start=False, stop=False))
                sq.step("tensor", lambda e, h=h: e.matmul(
                    ps_band[:], qpb_fm[:, h * 128:(h + 1) * 128], kb_band[:],
                    start=False, stop=True))
                sq.step("vector", lambda e: e.tensor_add(
                    sb_band[:], ps_band[:], bandmask[:]))
                sq.step("vector", lambda e: e.reduce_max(mb_c[:], sb_band[:], axis=AX.X))
                qcb = qct_t[:].rearrange("p (h i) -> p h i", i=64)[
                    :, h:h + 1, :].to_broadcast([128, 32, 64])
                kgv = KVg[:].rearrange("p (r v) -> p r v", v=512)[:, :, h * 64:(h + 1) * 64]
                sq.step("vector", lambda e, qcb=qcb, kgv=kgv: e.tensor_mul(
                    scrB[:, 0:2048].rearrange("p (r i) -> p r i", i=64), kgv, qcb))
                sq.step("vector", lambda e: e.reduce_sum(
                    sc_top[:], scrB[:, 0:2048].rearrange("p (r i) -> p r i", i=64),
                    axis=AX.X))
                q1b = qh1[:].rearrange("p (h i) -> p h i", i=16)[
                    :, h:h + 1, :].to_broadcast([128, 32, 16])
                q2b = qh2[:].rearrange("p (h i) -> p h i", i=16)[
                    :, h:h + 1, :].to_broadcast([128, 32, 16])
                sq.step("vector", lambda e, q1b=q1b: e.tensor_mul(
                    scrB[:, 0:512].rearrange("p (r i) -> p r i", i=16), m1v, q1b))
                sq.step("vector", lambda e, q2b=q2b: e.tensor_mul(
                    scrB[:, 512:1024].rearrange("p (r i) -> p r i", i=16), m2v, q2b))
                sq.step("vector", lambda e: e.tensor_add(
                    scrB[:, 0:512], scrB[:, 0:512], scrB[:, 512:1024]))
                sq.step("vector", lambda e: e.reduce_sum(
                    scrC[:, 0:32], scrB[:, 0:512].rearrange("p (r i) -> p r i", i=16),
                    axis=AX.X))
                sq.step("vector", lambda e: e.tensor_add(
                    sc_top[:], sc_top[:], scrC[:, 0:32]))
                if debug and qt == 8 and h == 0:
                    sq.dma(dbg["sctop"], sc_top[:])
                    sq.dma(dbg["band"], sb_band[:])
                    sq.dma(dbg["qh1"], qh1[:])
                    sq.dma(dbg["qh2"], qh2[:])
                    sq.dma(dbg["mh1"], mh1[:])
                    sq.dma(dbg["mh2"], mh2[:])
                sq.step("vector", lambda e: e.reduce_max(mt_c[:], sc_top[:], axis=AX.X))
                sq.step("vector", lambda e: e.tensor_max(mb_c[:], mb_c[:], mt_c[:]))
                sq.step("vector", lambda e: e.tensor_single_scalar(
                    bias_c[:], mb_c[:], -ATT2_SCALE, ALU.mult))
                sq.step("scalar", lambda e: e.activation(
                    P_band[:], sb_band[:], AF.Exp, bias=bias_c[:],
                    scale=ATT2_SCALE, accum_out=rs1[:]))
                sq.step("scalar", lambda e: e.activation(
                    P_top[:], sc_top[:], AF.Exp, bias=bias_c[:],
                    scale=ATT2_SCALE, accum_out=rs2[:]))
                sq.step("vector", lambda e: e.tensor_add(rs1[:], rs1[:], rs2[:]))
                sq.step("vector", lambda e: e.reciprocal(rr_c[:], rs1[:]))
                sq.step("tensor", lambda e: e.transpose(
                    ps_tr[:], P_band[:, 0:128], identb[:]))
                sq.step("vector", lambda e: e.tensor_copy(PbT1[:], ps_tr[:]))
                sq.step("tensor", lambda e: e.transpose(
                    ps_tr[0:64, 0:128], P_band[:, 128:192], identb[:]))
                sq.step("vector", lambda e: e.tensor_copy(PbT2[:], ps_tr[0:64, 0:128]))
                sq.step("tensor", lambda e, h=h: e.matmul(
                    ps_av[:], PbT1[:], Vb1[:, h * 64:(h + 1) * 64],
                    start=True, stop=False))
                sq.step("tensor", lambda e, h=h: e.matmul(
                    ps_av[:], PbT2[:], Vb2[0:64, h * 64:(h + 1) * 64],
                    start=False, stop=True))
                vgv = Vg[:].rearrange("p (r v) -> p v r", v=512)[:, h * 64:(h + 1) * 64, :]
                ptb = P_top[:].rearrange("p (o r) -> p o r", o=1).to_broadcast([128, 64, 32])
                sq.step("vector", lambda e, vgv=vgv, ptb=ptb: e.tensor_mul(
                    scrB[:, 0:2048].rearrange("p (v r) -> p v r", r=32), vgv, ptb))
                sq.step("vector", lambda e: e.reduce_sum(
                    ot_top[:], scrB[:, 0:2048].rearrange("p (v r) -> p v r", r=32),
                    axis=AX.X))
                sq.step("vector", lambda e: e.tensor_add(ot_top[:], ot_top[:], ps_av[:]))
                sq.step("vector", lambda e, h=h: e.tensor_scalar(
                    out_tm[:, h * 64:(h + 1) * 64], ot_top[:], rr_c[:], None, ALU.mult))
                if qt == 0:
                    sq.step("tensor", lambda e, hb=hb, hc=hc: e.matmul(
                        ps_96[:], qcf_sb[hb:hb + 64, hc * T_CORE:hc * T_CORE + 128],
                        kvf_sb[hb:hb + 64, hc * SPB + PADK:hc * SPB + PADK + 96],
                        start=True, stop=False))
                    sq.step("tensor", lambda e, h=h: e.matmul(
                        ps_96[:], qsa_fm[:, h * 128:(h + 1) * 128], ka96[:],
                        start=False, stop=False))
                    sq.step("tensor", lambda e, h=h: e.matmul(
                        ps_96[:], qsb_fm[:, h * 128:(h + 1) * 128], kb96[:],
                        start=False, stop=True))
                    sq.step("vector", lambda e: e.tensor_copy(
                        scrB[:, 0:96], ps_96[:]))
                    sq.step("vector", lambda e: e.reduce_max(
                        mb_c[:], scrB[:, 0:96], axis=AX.X))
                    sq.step("vector", lambda e: e.tensor_single_scalar(
                        bias_c[:], mb_c[:], -ATT2_SCALE, ALU.mult))
                    sq.step("scalar", lambda e: e.activation(
                        P_band[:, 0:96], scrB[:, 0:96], AF.Exp, bias=bias_c[:],
                        scale=ATT2_SCALE, accum_out=rs1[:]))
                    sq.step("vector", lambda e: e.reciprocal(rr_c[:], rs1[:]))
                    sq.step("tensor", lambda e: e.transpose(
                        ps_tr[0:96, 0:128], P_band[:, 0:96], identb[:]))
                    sq.step("vector", lambda e: e.tensor_copy(
                        PbT1[0:96, :], ps_tr[0:96, 0:128]))
                    sq.step("tensor", lambda e, h=h: e.matmul(
                        ps_av[:], PbT1[0:96, :], v96_sb[0:96, h * 64:(h + 1) * 64],
                        start=True, stop=True))
                    sq.step("vector", lambda e, h=h: e.tensor_scalar(
                        dense_tm[:, h * 64:(h + 1) * 64], ps_av[:], rr_c[:],
                        None, ALU.mult))
            if qt == 0:
                sq.step("vector", lambda e: e.tensor_copy(scrB[:, 0:512], out_tm[:]))
                sq.step("vector", lambda e: e.select(
                    out_tm[:], mask0_sb[:], dense_tm[:], scrB[:, 0:512]))
            if debug and qt == 8:
                sq.dma(dbg["outtm"], out_tm[:])
                sq.dma(dbg["rs"], rs1[:])
                sq.step("vector", lambda e: e.tensor_copy(scrC[0:16, 0:192], ka_band[:]))
                sq.dma(dbg["ka"], scrC[0:16, 0:192])
                sq.step("vector", lambda e: e.tensor_copy(scrB[0:16, 0:1024], qpa_fm[:]))
                sq.dma(dbg["qpa"], scrB[0:16, 0:1024])
                sq.dma(dbg["mug"], mug[:])
            sq.step("vector", lambda e: e.tensor_copy(otb[:], out_tm[:]))
            for c in range(4):
                sq.step("tensor", lambda e, c=c: e.transpose(
                    ps_tr[:], otb[:, c * 128:(c + 1) * 128], identb[:]))
                sq.step("vector", lambda e, c=c: e.tensor_copy(
                    ofm[:, c * 128:(c + 1) * 128], ps_tr[:]))
            sq.dma(Xt[:].rearrange("p (c t) -> p c t", c=NCH), xv_[:, :, t0:t0 + 128])
            for d in range(NCH):
                sq.step("tensor", lambda e, d=d: [
                    e.matmul(ps_out[:], wo_sb[:, c * D_MODEL + d * 128:
                                              c * D_MODEL + (d + 1) * 128],
                             ofm[:, c * 128:(c + 1) * 128],
                             start=(c == 0), stop=(c == 3))
                    for c in range(4)])
                sq.step("vector", lambda e, d=d: e.tensor_add(
                    oscr[:], ps_out[:], Xt[:, d * 128:(d + 1) * 128]))
                sq.dma(xo[:, d, t0:t0 + 128], oscr[:])

    return seq_program(prog)


THETA16 = (10000.0 ** (-2.0 * np.arange(16) / 32.0)).astype(np.float64)


def host_topk_attn_inputs(x, proj_outs, W, core):
    b, h = core // 2, core % 2
    t0g = h * 2048
    full = lambda key: np.concatenate(
        [proj_outs[2 * b][key], proj_outs[2 * b + 1][key]], axis=0)
    kv_full = full("kv_tm")
    v_full = full("v_tm")
    mu_full = full("mu_tm")
    ki_full = full("k_i_tm")
    SPB = 2112
    zc = np.zeros((512, 1), kv_full.dtype)
    kv_fm_pad = np.concatenate(
        [np.zeros((512, PADK), kv_full.dtype), np.ascontiguousarray(kv_full.T),
         zc], axis=1)[:, t0g:t0g + SPB]
    v_pad = np.concatenate([np.zeros((PADK, 512), v_full.dtype), v_full,
                            np.zeros((1, 512), v_full.dtype)],
                           axis=0)[t0g:t0g + SPB]
    mu1_pad = np.concatenate(
        [np.zeros((16, PADK), kv_full.dtype),
         np.ascontiguousarray(mu_full[:, 0:16].T).astype(kv_full.dtype),
         zc[0:16]], axis=1)[:, t0g:t0g + SPB]
    mu2_pad = np.concatenate(
        [np.zeros((16, PADK), kv_full.dtype),
         np.ascontiguousarray(mu_full[:, 16:32].T).astype(kv_full.dtype),
         zc[0:16]], axis=1)[:, t0g:t0g + SPB]

    tg = np.arange(t0g, t0g + T_CORE, dtype=np.float64)
    t0tile = (np.arange(T_CORE) // 128 * 128) + t0g
    ang_q = tg[:, None] * THETA16[None, :]
    ang_q2 = (2 * tg - t0tile)[:, None] * THETA16[None, :]
    jj = np.arange(192, dtype=np.float64)
    ang_j = jj[None, :] * THETA16[:, None]
    ss = np.arange(96, dtype=np.float64)
    ang_s = ss[None, :] * THETA16[:, None]
    rr = np.arange(64, 96, dtype=np.float64)
    ang_r = rr[:, None] * THETA16[None, :]
    crk = np.tile(np.cos(ang_r).astype(np.float32).reshape(1, 512), (128, 1))
    srk = np.tile(np.sin(ang_r).astype(np.float32).reshape(1, 512), (128, 1))

    thr = np.zeros((128, 16), np.float32)
    for qt in range(16):
        thr[:, qt] = t0g + qt * 128 + np.arange(128) - 64.0
    ramp0 = np.full((128, 96), -3e38, np.float32)
    if h == 0:
        for p in range(64, 95):
            t = p
            for s in range(t + 1, 96):
                ramp0[p, s] = -1e30 - s * 1e26
    mask0 = np.zeros((128, 512), np.uint8)
    if h == 0:
        mask0[0:64, :] = 1

    return {
        "x_fm": fmT(core_slice_tokens(x, core)),
        "q_c_tm": proj_outs[core]["q_c_tm"],
        "q_c_fmi": np.ascontiguousarray(proj_outs[core]["q_c_tm"].T),
        "qr_tmi": proj_outs[core]["qr_tm"],
        "q_i_fmi": np.ascontiguousarray(proj_outs[core]["q_i_tm"].T),
        "k_i_fmi": np.ascontiguousarray(
            np.concatenate([ki_full.T, ki_full.T], axis=0)),
        "kv_fm_pad": np.ascontiguousarray(kv_fm_pad),
        "kv_tm_full": np.ascontiguousarray(kv_full),
        "v_tm_full": np.ascontiguousarray(v_full),
        "v_pad": np.ascontiguousarray(v_pad),
        "mu1_pad": np.ascontiguousarray(mu1_pad),
        "mu2_pad": np.ascontiguousarray(mu2_pad),
        "mu_tm_full": np.ascontiguousarray(mu_full),
        "w_out": np.ascontiguousarray(W["W_out"]),
        "idxw_row": np.asarray(W["idx_w"], np.float32)[None, :],
        "rdelta": np.asarray(W["raw_delta"], np.float32)[None, :],
        "thr_i": thr,
        "ramp0": ramp0,
        "mask0": mask0,
        "cth_q": np.cos(ang_q).astype(np.float32),
        "sth_q": np.sin(ang_q).astype(np.float32),
        "cth_q2": np.cos(ang_q2).astype(np.float32),
        "sth_q2": np.sin(ang_q2).astype(np.float32),
        "cjb_i": np.cos(ang_j).astype(np.float32),
        "sjb_i": np.sin(ang_j).astype(np.float32),
        "cabs96": np.cos(ang_s).astype(np.float32),
        "sabs96": np.sin(ang_s).astype(np.float32),
        "crk_i": crk,
        "srk_i": srk,
    }



def _run(nc, in_maps):
    res = run_bass_kernel_spmd(nc, in_maps, core_ids=list(range(N_CORES)))
    return res.results


_PROG_CACHE = {}


def _prog(name, builder):
    if name not in _PROG_CACHE:
        _PROG_CACHE[name] = builder()
    return _PROG_CACHE[name]


def kernel(**inp):
    x = np.asarray(inp["L0_attn_a_x"], np.float32)

    Wm = {k: inp[f"L0_attn_a_{k}"] for k in
          ["in_proj_w", "conv_w", "conv_b", "x_proj_w", "dt_proj_w",
           "dt_proj_b", "A_log", "D", "out_proj_w"]}
    pre = _run(_prog("mamba_pre", build_mamba_pre),
               [host_mamba_pre_inputs(x, Wm, c) for c in range(8)])
    sc = _run(_prog("mamba_scan", build_mamba_scan),
              [host_mamba_scan_inputs(pre, Wm, c) for c in range(8)])
    post = _run(_prog("mamba_post", build_mamba_post),
                [host_mamba_post_inputs(x, sc, pre, Wm, c) for c in range(8)])
    x = unpack_fm_out(post)

    Wf = {k: inp[f"L0_ffn_{k}"] for k in
          ["router_w", "w_gate", "w_up", "w_down", "sw_gate", "sw_up", "sw_down"]}
    moe = _run(_prog("l0_ffn", build_l0_ffn),
               [host_l0_ffn_inputs(x, Wf, c) for c in range(8)])
    x = unpack_fm_out(moe)

    Wb = {k: inp[f"L0_attn_b_{k}"] for k in
          ["Wd_q_w", "Wd_q_b", "Wu_q_w", "Wu_q_b", "Wq_r_w", "Wq_r_b",
           "Wk_r_w", "Wk_r_b", "Wd_kv_w", "Wd_kv_b", "Wu_k_w", "Wu_k_b",
           "Wu_v_w", "Wu_v_b", "Wo_w", "Wo_b"]}
    qkv = _run(_prog("mla_qkv", build_mla_qkv),
               [host_mla_qkv_inputs(x, Wb, c) for c in range(8)])
    att = _run(_prog("mla_attn2", build_mla_attn2),
               [host_mla_attn_inputs(x, qkv, Wb, c) for c in range(8)])
    x = unpack_fm_out(att)

    moe = _run(_prog("l0_ffn", build_l0_ffn),
               [host_l0_ffn_inputs(x, Wf, c) for c in range(8)])
    x = unpack_fm_out(moe)

    Wa = {k: np.asarray(inp[f"L1_attn_a_{k}"], np.float32) for k in
          ["W_dkv", "W_uk", "W_uv", "W_dq", "W_uq", "W_qr", "W_kr", "W_out",
           "idx_wq", "idx_wk", "idx_w", "raw_delta"]}
    prj = _run(_prog("topk_proj", build_topk_proj),
               [host_topk_proj_inputs(x, Wa, c) for c in range(8)])
    ta = _run(_prog("topk_attn", build_topk_attn),
              [host_topk_attn_inputs(x, prj, Wa, c) for c in range(8)])
    x = unpack_fm_out(ta)

    Wl = {k: inp[f"L1_ffn_{k}"] for k in ["W_r", "W_k", "W_v", "mu_r", "mu_k"]}
    ff = _run(_prog("l1_ffn", build_l1_ffn),
              [host_l1_ffn_inputs(x, Wl, c) for c in range(8)])
    x = unpack_fm_out(ff)

    Ww = {k: inp[f"L1_attn_b_{k}"] for k in
          ["mu_k", "mu_v", "mu_r", "Wk", "Wv", "Wr", "W_out", "w", "u"]}
    kv = _run(_prog("wkv_kv", build_wkv_kv),
              [host_wkv_kv_inputs(x, Ww, c) for c in range(8)])
    wo = _run(_prog("wkv_out", build_wkv_out),
              [host_wkv_out_inputs(x, kv, Ww, c) for c in range(8)])
    x = unpack_fm_out(wo)

    ff = _run(_prog("l1_ffn", build_l1_ffn),
              [host_l1_ffn_inputs(x, Wl, c) for c in range(8)])
    x = unpack_fm_out(ff)

    return x
